# version 20
# baseline (speedup 1.0000x reference)
import math
from contextlib import ExitStack

import numpy as np

import concourse.bass as bass
import concourse.mybir as mybir
from concourse.bass_utils import run_bass_kernel_spmd

F32 = mybir.dt.float32
BF16 = mybir.dt.bfloat16
AF = mybir.ActivationFunctionType
ALU = mybir.AluOpType
AX = mybir.AxisListType

ENGS = ["PE", "DVE", "ACT", "SP", "POOL"]
ETYPE = {
    "PE": mybir.EngineType.PE,
    "DVE": mybir.EngineType.DVE,
    "ACT": mybir.EngineType.Activation,
    "SP": mybir.EngineType.SP,
    "POOL": mybir.EngineType.Pool,
}


class KB:

    def __init__(self, nc, stack):
        self.nc = nc
        self.h = {"PE": nc.tensor, "DVE": nc.vector, "ACT": nc.scalar,
                  "SP": nc.sync, "POOL": nc.gpsimd}
        self.sems = {}
        self.ctr = {}
        self.stack = stack
        self.cur = None
        self.waited = {}

    def sem(self, name):
        if name not in self.sems:
            self.sems[name] = self.stack.enter_context(self.nc.semaphore(name))
        if name not in self.ctr:
            self.ctr[name] = 0
        return self.sems[name]

    def new_streams(self):
        return {e: [] for e in ENGS}

    def emit(self, eng, fn, wait=(), inc=None, inc_n=1):
        if eng in ("DVE", "ACT", "POOL"):
            wait = list(wait) + [("s" + eng, self.ctr["s" + eng])]
        w = []
        for (s, v) in wait:
            if v <= 0:
                continue
            k = (eng, s)
            if self.waited.get(k, -1) < v:
                self.waited[k] = v
                w.append((s, v))
        ret = None
        if inc is not None:
            self.sem(inc)
            self.ctr[inc] += inc_n
            ret = self.ctr[inc]
        handle = self.h[eng]
        sems = self.sems

        def closure(handle=handle, w=tuple(w), fn=fn, inc=inc, inc_n=inc_n):
            for (s, v) in w:
                handle.wait_ge(sems[s], v)
            inst = fn(handle)
            if inc is not None:
                assert inst is not None, "op with inc returned no instruction"
                inst.then_inc(sems[inc], inc_n)

        self.cur[eng].append(closure)
        return ret

    def fence(self):
        return [(s, self.ctr[s]) for s in ("sPE", "sDVE", "sACT")]

    def snap(self):
        return (dict(self.ctr), dict(self.waited))

    def restore(self, s):
        self.ctr = dict(s[0])
        self.waited = dict(s[1])


class DS:

    def __init__(self, kb, name, qeng):
        self.kb = kb
        self.name = name
        self.q = qeng
        self.n = {}

    def issue(self, fn, wait=(), slot=0):
        sem = f"{self.name}_{slot}"
        self.kb.sem(sem)
        self.n[slot] = self.n.get(slot, 0) + 1
        self.kb.emit(self.q, fn, wait=wait, inc=sem, inc_n=16)
        return (sem, 16 * self.n[slot])

    def final_tokens(self):
        return [(f"{self.name}_{k}", 16 * v) for k, v in self.n.items()]

    def state(self):
        return dict(self.n)

    def set_state(self, s):
        self.n = dict(s)


class Arena:

    def __init__(self, ap, words):
        self.ap = ap
        self.words = words
        self.off = 0
        self.marks = []

    def f32(self, n):
        o = self.off
        self.off += n
        assert self.off <= self.words, f"arena overflow: {self.off} > {self.words}"
        return self.ap[:, o:o + n]

    def bf16(self, nel):
        assert nel % 2 == 0
        return self.f32(nel // 2).bitcast(BF16)

    def at_f32(self, o, n):
        assert o + n <= self.words
        return self.ap[:, o:o + n]

    def at_bf16(self, o, nel):
        return self.at_f32(o, nel // 2).bitcast(BF16)

    def push(self):
        self.marks.append(self.off)

    def pop(self):
        self.off = self.marks.pop()


def _r3(ap2d):
    return ap2d.rearrange("(a p) b -> p a b", p=128)


def _s3(sbap, a):
    return sbap.rearrange("p (a b) -> p a b", a=a)


def build_program(S, D, F, n_cores=8, arena_words=53200):
    NCH = 16
    CS = S // NCH
    T = 2 * CS
    NT = T // 128
    DC = D // 128
    FC = F // 128
    HC = DC // 2
    R = CS // 128
    TB = min(512, T)
    NTB = T // TB
    KL = CS if CS < 256 else CS // 2
    NKL = CS // KL
    MAXKT = S // 128
    SCALE = 1.0 / math.sqrt(D)
    RS1 = D // n_cores
    RS2 = F // n_cores
    QF = min(F, 512)
    QF2 = min(D, 512)
    OG = min(256, D)
    FB = min(2048, F)
    OGW = min(512, D)
    NOG = D // OGW
    NBC = (D + 511) // 512

    nc = bass.Bass("TRN2", target_bir_lowering=False, debug=False)

    def din(name, shape):
        return nc.dram_tensor(name, shape, F32, kind="ExternalInput").ap()

    xloc = din("xloc", [T, D])
    cosl = din("cosl", [D // 2, T])
    sinl = din("sinl", [D // 2, T])
    wq = din("wq", [D, D])
    wk = din("wk", [D, D])
    wv = din("wv", [D, D])
    wo = din("wo", [D, D])
    w1s = din("w1s", [RS1, F])
    w2s = din("w2s", [RS2, D])
    b1r = din("b1r", [1, F])
    b2r = din("b2r", [1, D])
    identf = din("identf", [128, 128])
    masksf = din("masksf", [128, R * CS])
    out = nc.dram_tensor("out", [T, D], F32, kind="ExternalOutput").ap()

    kcon = nc.dram_tensor("kcon", [D, T], BF16)
    vcon = nc.dram_tensor("vcon", [T, D], BF16)
    kbuf = nc.dram_tensor("kbuf", [n_cores * D, T], BF16, addr_space="Shared")
    vbuf = nc.dram_tensor("vbuf", [n_cores * T, D], BF16, addr_space="Shared")
    w1qc = nc.dram_tensor("w1qc", [RS1, F], BF16)
    w2qc = nc.dram_tensor("w2qc", [RS2, D], BF16)
    w1qb = nc.dram_tensor("w1qb", [D, F], BF16, addr_space="Shared")
    w2qb = nc.dram_tensor("w2qb", [F, D], BF16, addr_space="Shared")
    pcon = nc.dram_tensor("pcon", [1, 2], F32)
    pbuf = nc.dram_tensor("pbuf", [1, 2], F32)
    sspill = nc.dram_tensor("sspill", [T, D], BF16)
    wodram = nc.dram_tensor("wodram", [D, D], BF16)

    st = ExitStack()
    kb = KB(nc, st)
    arena_t = st.enter_context(nc.sbuf_tensor("AR", [128, arena_words], F32))
    A = Arena(arena_t[:], arena_words)

    ident = A.bf16(128)
    b1c = A.f32(FC)
    b2c = A.f32(DC)
    rcp = A.f32(NT)
    svec = A.f32(2)
    nsvec = A.f32(2)
    onesb = A.f32(1).bitcast(BF16)
    onesf = A.f32(1)
    sums_sb = A.f32(CS)
    sumc = A.f32(max(R, 2))
    allr = A.f32(2)
    sc2 = A.f32(2)
    acc = A.f32(2)
    rqt = A.f32(2)
    epsv = A.f32(1)
    persist_end = A.off

    ps = [st.enter_context(nc.psum_tensor(f"ps{i}", [128, 512], F32)) for i in range(8)]

    def P(i, n=512):
        return ps[i][:, 0:n]

    pt6 = ps[6][:].bitcast(BF16)
    pt7 = ps[7][:].bitcast(BF16)

    for e in ENGS:
        kb.sem("s" + e)
    kb.sem("sCC")

    XN_W = DC * T // 2
    xn = A.bf16(DC * T)
    scratch0 = A.off

    common1 = kb.new_streams()
    kb.cur = common1

    dmisc = DS(kb, "dmisc", "POOL")
    dseq = DS(kb, "dseq", "SP")
    dqnt = DS(kb, "dqnt", "POOL")
    dq1 = DS(kb, "dq1", "POOL")
    dbo = DS(kb, "dbo", "ACT")
    dcs = DS(kb, "dcs", "SP")
    dwp = DS(kb, "dwp", "POOL")
    dkh = DS(kb, "dkh", "SP")
    dvl = DS(kb, "dvl", "ACT")
    dw1 = DS(kb, "dw1", "SP")
    dw2 = DS(kb, "dw2", "SP")
    dsl = DS(kb, "dsl", "ACT")
    dxc = DS(kb, "dxc", "SP")
    dst_ = DS(kb, "dst", "SP")

    t_ident = dmisc.issue(lambda h: h.dma_start(out=ident[:, 0:128], in_=identf), slot="id")
    kb.emit("DVE", lambda h: h.memset(onesb[:, 0:1], 1.0), inc="sDVE")
    kb.emit("DVE", lambda h: h.memset(onesf[:, 0:1], 1.0), inc="sDVE")
    kb.emit("DVE", lambda h: h.memset(epsv[:, 0:1], 1e-5), inc="sDVE")
    ev_const = kb.ctr["sDVE"]

    A.push()
    masks = A.bf16(R * CS)
    qh = A.bf16(DC * T)
    khl = A.bf16(DC * T)
    t_masks = dmisc.issue(lambda h: h.dma_start(out=masks[:], in_=masksf), slot="mk")

    A.push()
    qwb = [A.f32(QF) for _ in range(2)]
    qa = A.f32(QF)
    qb = A.f32(QF)
    q2b = A.bf16(QF)

    A.push()
    WOB = min(512, D)
    NWOB = D // WOB
    stgw = A.f32(WOB * WOB // 128)
    wcast = [A.bf16(WOB * WOB // 128) for _ in range(2)]
    bbuf = A.f32(min(1024, F))
    xbuf = [A.f32(D) for _ in range(2)]
    xntok = [A.bf16(D) for _ in range(2)]
    bno = [A.f32(6 * NBC) for _ in range(2)]
    mv = [A.f32(2) for _ in range(2)]
    rstd = [A.f32(2) for _ in range(2)]

    BQ = min(1024, F)
    ev_bprev = 0
    for src_row, n, pslot in ((b1r, FC, 4), (b2r, DC, 5)):
        ncol = 128 * n
        for j4 in range(ncol // BQ if ncol >= BQ else 1):
            w_ = min(BQ, ncol)
            tb_ = dseq.issue(lambda h, src_row=src_row, j4=j4, w_=w_: h.dma_start(
                out=bbuf[0:1, 0:w_], in_=src_row[0:1, BQ * j4:BQ * j4 + w_]),
                wait=[("sPE", ev_bprev)], slot="bb")

            def mmb(h, j4=j4, w_=w_, pslot=pslot):
                inst = None
                for j in range(w_ // 128):
                    inst = h.matmul(ps[pslot][:, (BQ // 128) * j4 + j:(BQ // 128) * j4 + j + 1],
                                    bbuf[0:1, 128 * j:128 * j + 128],
                                    onesf[0:1, 0:1], start=True, stop=True)
                return inst
            kb.emit("PE", mmb, wait=[tb_, ("sDVE", ev_const)], inc="sPE")
            ev_bprev = kb.ctr["sPE"]
    ev_bcol_pe = kb.ctr["sPE"]
    kb.emit("DVE", lambda h: h.tensor_copy(out=b1c[:], in_=ps[4][:, 0:FC]),
            wait=[("sPE", ev_bcol_pe)], inc="sDVE")
    kb.emit("DVE", lambda h: h.tensor_copy(out=b2c[:], in_=ps[5][:, 0:DC]),
            wait=[("sDVE", kb.ctr["sDVE"])], inc="sDVE")
    ev_bc = kb.ctr["sDVE"]

    st_wo = {"ld": 0, "cast": 0, "i": 0}
    wo_units = []
    for wi in range(NWOB * NWOB):
        rblk, cblk = wi // NWOB, wi % NWOB

        def wo_unit(rblk=rblk, cblk=cblk):
            i = st_wo["i"]
            pw = i % 2
            tl = dseq.issue(lambda h: h.dma_start(
                out=_s3(stgw[:], WOB // 128),
                in_=_r3(wo[WOB * rblk:WOB * rblk + WOB, WOB * cblk:WOB * cblk + WOB])),
                wait=[("sACT", st_wo["cast"])], slot="wop")
            kb.emit("ACT", lambda h: h.copy(out=wcast[pw][:], in_=stgw[:]),
                    wait=[tl, (f"dbo_wo{pw}", 16 * (i // 2)) if i >= 2 else ("sACT", 0)],
                    inc="sACT")
            st_wo["cast"] = kb.ctr["sACT"]
            dbo.issue(lambda h: h.dma_start(
                out=_r3(wodram.ap()[WOB * rblk:WOB * rblk + WOB, WOB * cblk:WOB * cblk + WOB]),
                in_=_s3(wcast[pw][:], WOB // 128)),
                wait=[("sACT", st_wo["cast"])], slot=f"wo{pw}")
            st_wo["i"] += 1
        wo_units.append(wo_unit)

    ev_xfree = [0, 0]
    ev_ptfree = 0
    ev_xn_t = []
    for tt in range(NT):
        p = tt % 2
        tx = dseq.issue(
            lambda h, tt=tt, p=p: h.dma_start(out=xbuf[p][:], in_=xloc[128 * tt:128 * tt + 128, :]),
            wait=[("sDVE", ev_xfree[p])], slot=f"x{p}")

        def bn(h, p=p):
            inst = None
            for k in range(NBC):
                w = min(512, D - 512 * k)
                inst = h.bn_stats(out=bno[p][:, 6 * k:6 * k + 6],
                                  in_=xbuf[p][:, 512 * k:512 * k + w])
            return inst
        kb.emit("DVE", bn, wait=[tx], inc="sDVE")
        kb.emit("DVE", lambda h, p=p: h.bn_aggr(out=mv[p][:], in_=bno[p][:]),
                wait=[("sDVE", kb.ctr["sDVE"])], inc="sDVE")
        ev_mv = kb.ctr["sDVE"]
        kb.emit("ACT", lambda h, p=p: h.activation(out=rstd[p][:, 0:1], in_=mv[p][:, 1:2],
                                                   func=AF.Ln, bias=epsv[:, 0:1]),
                wait=[("sDVE", ev_mv), ("sDVE", ev_const)], inc="sACT")
        kb.emit("ACT", lambda h, p=p: h.activation(out=rstd[p][:, 0:1], in_=rstd[p][:, 0:1],
                                                   func=AF.Exp, scale=-0.5),
                wait=[("sACT", kb.ctr["sACT"])], inc="sACT")
        ev_rstd = kb.ctr["sACT"]
        kb.emit("DVE", lambda h, p=p: h.tensor_scalar(out=xntok[p][:], in0=xbuf[p][:],
                                                      scalar1=mv[p][:, 0:1], scalar2=rstd[p][:, 0:1],
                                                      op0=ALU.subtract, op1=ALU.mult),
                wait=[("sACT", ev_rstd)], inc="sDVE")
        ev_xfree[p] = kb.ctr["sDVE"]
        for half in range((DC + 7) // 8):
            pt = pt6 if half % 2 == 0 else pt7
            lo, hi = 8 * half, min(DC, 8 * half + 8)

            def tr(h, p=p, pt=pt, lo=lo, hi=hi):
                inst = None
                for dc in range(lo, hi):
                    inst = h.transpose(pt[:, 128 * (dc - lo):128 * (dc - lo) + 128],
                                       xntok[p][:, 128 * dc:128 * dc + 128], ident[:, 0:128])
                return inst
            kb.emit("PE", tr, wait=[("sDVE", ev_xfree[p]), t_ident, ("sDVE", ev_ptfree)],
                    inc="sPE")
            ev_tr = kb.ctr["sPE"]

            def ev(h, pt=pt, lo=lo, hi=hi, tt=tt):
                inst = None
                for dc in range(lo, hi):
                    inst = h.tensor_copy(out=xn[:, T * dc + 128 * tt:T * dc + 128 * tt + 128],
                                         in_=pt[:, 128 * (dc - lo):128 * (dc - lo) + 128])
                return inst
            kb.emit("DVE", ev, wait=[("sPE", ev_tr)], inc="sDVE")
            ev_ptfree = kb.ctr["sDVE"]
        ev_xn_t.append(kb.ctr["sDVE"])
        for _ in range(2):
            if wo_units:
                wo_units.pop(0)()
    while wo_units:
        wo_units.pop(0)()
    F_wopc = [(f"dbo_wo{pw}", 16 * dbo.n.get(f"wo{pw}", 0)) for pw in range(2)]
    ev_xn = kb.ctr["sDVE"]

    def slice_chunks(rows, cols, qf):
        res = []
        r = 0
        while r < rows:
            rr = min(128, rows - r)
            c = 0
            while c < cols:
                cc = min(qf, cols - c)
                res.append((r, rr, c, cc))
                c += cc
            r += rr
        return res

    ch1 = slice_chunks(RS1, F, QF)
    ch2 = slice_chunks(RS2, D, QF2)

    kb.emit("DVE", lambda h: h.memset(acc[:], 0.0), inc="sDVE")
    ev_accz = kb.ctr["sDVE"]
    st_p1 = {"evqf": [0, 0], "i": 0}
    p1units = []
    for mat, chunks, col in ((w1s, ch1, 0), (w2s, ch2, 1)):
        for (r, rr, c, cc) in chunks:
            def p1unit(mat=mat, r=r, rr=rr, c=c, cc=cc, col=col):
                p = st_p1["i"] % 2
                tq = dqnt.issue(lambda h: h.dma_start(
                    out=qwb[p][0:rr, 0:cc], in_=mat[r:r + rr, c:c + cc]),
                    wait=[("sDVE", st_p1["evqf"][p])], slot=f"p1{p}")
                kb.emit("DVE", lambda h: h.reduce_sum(
                    out=rqt[0:rr, 0:1], in_=qwb[p][0:rr, 0:cc], axis=AX.X,
                    apply_absolute_value=True), wait=[tq, ("sDVE", ev_accz)], inc="sDVE")
                kb.emit("DVE", lambda h: h.tensor_tensor(
                    out=acc[0:rr, col:col + 1], in0=acc[0:rr, col:col + 1],
                    in1=rqt[0:rr, 0:1], op=ALU.add),
                    wait=[("sDVE", kb.ctr["sDVE"])], inc="sDVE")
                st_p1["evqf"][p] = kb.ctr["sDVE"]
                st_p1["i"] += 1
            p1units.append(p1unit)




    A.push()
    OGV = min(256, D)
    NOGV = D // OGV
    VCH = min(4, DC)
    NVF = DC // VCH
    NQUAD = max(1, NOGV // 4)
    QSZ = min(4, NOGV)
    vl = A.at_bf16(khl.offset // 2, NT * D)
    wvb = [A.bf16(DC * OGV) for _ in range(QSZ)]
    stgv = [A.f32(VCH * OGV) for _ in range(2)]
    ev_wv_used = [0] * QSZ
    cast_v = {}
    iv = 0
    vslot_ev = {}
    for q in range(NQUAD):
        for j in range(QSZ):
            og = QSZ * q + j
            for hf in range(NVF):
                sp_ = iv % 2
                twv_ = dseq.issue(lambda h, og=og, hf=hf, sp_=sp_: h.dma_start(
                    out=_s3(stgv[sp_][:], VCH),
                    in_=_r3(wv[128 * VCH * hf:128 * VCH * (hf + 1), OGV * og:OGV * og + OGV])),
                    wait=[("sACT", cast_v.get(iv - 2, 0))] + F_wopc, slot=f"wv{sp_}")
                kb.emit("ACT", lambda h, j=j, hf=hf, sp_=sp_: h.copy(
                    out=wvb[j][:, VCH * OGV * hf:VCH * OGV * (hf + 1)], in_=stgv[sp_][:]),
                    wait=[twv_, ("sPE", ev_wv_used[j])], inc="sACT")
                cast_v[iv] = kb.ctr["sACT"]
                iv += 1
        tquad = ("sACT", kb.ctr["sACT"])
        for tt in range(NT):
            for j in range(QSZ):
                og = QSZ * q + j
                slot = (tt * QSZ + j) % 2
                idx = (q * NT + tt) * QSZ + j

                def mmv(h, j=j, tt=tt, slot=slot):
                    inst = None
                    for dc in range(DC):
                        inst = h.matmul(P(slot, OGV),
                                        xn[:, T * dc + 128 * tt:T * dc + 128 * tt + 128],
                                        wvb[j][:, OGV * dc:OGV * dc + OGV],
                                        start=(dc == 0), stop=(dc == DC - 1))
                    return inst
                w = [tquad, ("sDVE", ev_xn_t[tt])]
                if idx >= 2:
                    w.append(("sDVE", vslot_ev[idx - 2]))
                kb.emit("PE", mmv, wait=w, inc="sPE")
                ev_mmv = kb.ctr["sPE"]
                kb.emit("DVE", lambda h, og=og, tt=tt, slot=slot: h.tensor_copy(
                    out=vl[:, D * tt + OGV * og:D * tt + OGV * og + OGV], in_=P(slot, OGV)),
                    wait=[("sPE", ev_mmv)], inc="sDVE")
                vslot_ev[idx] = kb.ctr["sDVE"]
            if tt % 2 == 1:
                if p1units:
                    p1units.pop(0)()
        for j in range(QSZ):
            ev_wv_used[j] = kb.ctr["sPE"]
    while p1units:
        p1units.pop(0)()
    ev_v = kb.ctr["sDVE"]
    t_vcon = dbo.issue(lambda h: h.dma_start(
        out=_r3(vcon.ap()), in_=_s3(vl[:], NT)),
        wait=[("sDVE", ev_v)], slot="vc")
    kb.emit("POOL", lambda h: h.collective_compute(
        "AllGather", ALU.bypass, replica_groups=[list(range(n_cores))],
        ins=[vcon.ap().opt()], outs=[vbuf.ap().opt()]), wait=[t_vcon], inc="sCC")
    cc_v = kb.ctr["sCC"]
    ev_acc = kb.ctr["sDVE"]

    def mm_part(h):
        h.matmul(ps[3][0:1, 0:1], acc[:, 0:1], onesf[:, 0:1], start=True, stop=True)
        return h.matmul(ps[3][0:1, 1:2], acc[:, 1:2], onesf[:, 0:1], start=True, stop=True)
    kb.emit("PE", mm_part, wait=[("sDVE", ev_acc)], inc="sPE")
    kb.emit("DVE", lambda h: h.tensor_copy(out=allr[0:1, 0:2], in_=ps[3][0:1, 0:2]),
            wait=[("sPE", kb.ctr["sPE"])], inc="sDVE")
    t_pw = dbo.issue(lambda h: h.dma_start(out=pcon[:, :], in_=allr[0:1, 0:2]),
                     wait=[("sDVE", kb.ctr["sDVE"])], slot="pc")
    kb.emit("POOL", lambda h: h.collective_compute(
        "AllReduce", ALU.add, replica_groups=[list(range(n_cores))],
        ins=[pcon.ap().opt()], outs=[pbuf.ap().opt()]),
        wait=[t_pw], inc="sCC")
    cc_scale = kb.ctr["sCC"]




    F_v = [("sPE", kb.ctr["sPE"]), ("sDVE", ev_v), t_vcon] + F_wopc
    A.pop()
    A.pop()
    A.push()
    cosb = [A.f32(T) for _ in range(2)]
    sinb = [A.f32(T) for _ in range(2)]
    wqb = [A.bf16(256 * DC) for _ in range(3)]
    wkb = [A.bf16(256 * DC) for _ in range(3)]
    rt = [A.f32(TB) for _ in range(4)]
    HCH = min(8, DC)
    NHF = DC // HCH
    stgp = [A.f32(HCH * 256) for _ in range(2)]
    cast_p = {}
    ip = 0

    ev_rope_par = [0, 0]
    ev_cs_free = [0, 0]
    ev_w_used = {"q": [0, 0, 0], "k": [0, 0, 0]}
    t_kcon = None
    cc_k = None
    for which, wsrc, wbt, dst, wkey in ((1, wk, None, None, "k"), (0, wq, None, None, "q")):
        wbuf = wkb if which == 1 else wqb
        dst = khl if which == 1 else qh
        for m in range(HC):
            pp = m % 2
            wp3 = m % 3
            base = 4 * pp
            tc = dcs.issue(lambda h, m=m, pp=pp: h.dma_start(
                out=cosb[pp][:], in_=cosl[128 * m:128 * m + 128, :]),
                wait=[("sDVE", ev_cs_free[pp])] + (F_v if which == 1 and m < 2 else []),
                slot=f"c{pp}")
            ts_ = dcs.issue(lambda h, m=m, pp=pp: h.dma_start(
                out=sinb[pp][:], in_=sinl[128 * m:128 * m + 128, :]), slot=f"s{pp}")
            for hf in range(NHF):
                sp_ = ip % 2
                tw_ = dseq.issue(lambda h, m=m, hf=hf, wsrc=wsrc, sp_=sp_: h.dma_start(
                    out=_s3(stgp[sp_][:], HCH),
                    in_=_r3(wsrc[128 * HCH * hf:128 * HCH * (hf + 1), 256 * m:256 * m + 256])),
                    wait=[("sACT", cast_p.get(ip - 2, 0))] + (F_v if which == 1 and m < 2 else []),
                    slot=f"wp{sp_}")
                kb.emit("ACT", lambda h, wp3=wp3, hf=hf, wbuf=wbuf, sp_=sp_: h.copy(
                    out=wbuf[wp3][:, HCH * 256 * hf:HCH * 256 * (hf + 1)], in_=stgp[sp_][:]),
                    wait=[tw_, ("sPE", ev_w_used[wkey][wp3])], inc="sACT")
                cast_p[ip] = kb.ctr["sACT"]
                ip += 1
            tw = ("sACT", kb.ctr["sACT"])

            def mmproj(h, wbuf=wbuf, wp3=wp3, base=base):
                inst = None
                for dc in range(DC):
                    for oc in range(2):
                        for tb in range(NTB):
                            slot = base + oc * NTB + tb
                            inst = h.matmul(
                                P(slot, TB),
                                wbuf[wp3][:, 256 * dc + 128 * oc:256 * dc + 128 * oc + 128],
                                xn[:, T * dc + TB * tb:T * dc + TB * tb + TB],
                                start=(dc == 0), stop=(dc == DC - 1))
                return inst
            kb.emit("PE", mmproj, wait=[tw, ("sDVE", ev_xn), ("sDVE", ev_rope_par[pp])],
                    inc="sPE")
            ev_mm = kb.ctr["sPE"]
            ev_w_used[wkey][wp3] = ev_mm
            for tb in range(NTB):
                sA = base + tb
                sB = base + NTB + tb
                o1 = T * (2 * m) + TB * tb
                o2 = T * (2 * m + 1) + TB * tb

                def rope1(h, sA=sA, sB=sB, pp=pp, tb=tb):
                    h.tensor_tensor(out=rt[0][:], in0=P(sA, TB),
                                    in1=cosb[pp][:, TB * tb:TB * tb + TB], op=ALU.mult)
                    h.tensor_tensor(out=rt[1][:], in0=P(sB, TB),
                                    in1=sinb[pp][:, TB * tb:TB * tb + TB], op=ALU.mult)
                    h.tensor_tensor(out=rt[2][:], in0=P(sA, TB),
                                    in1=sinb[pp][:, TB * tb:TB * tb + TB], op=ALU.mult)
                    return h.tensor_tensor(out=rt[3][:], in0=P(sB, TB),
                                           in1=cosb[pp][:, TB * tb:TB * tb + TB], op=ALU.mult)
                kb.emit("DVE", rope1, wait=[("sPE", ev_mm), tc, ts_], inc="sDVE")
                ev_rt = kb.ctr["sDVE"]

                def rope2(h, o1=o1, o2=o2, dst=dst):
                    h.tensor_tensor(out=dst[:, o1:o1 + TB], in0=rt[0][:], in1=rt[1][:],
                                    op=ALU.subtract)
                    return h.tensor_tensor(out=dst[:, o2:o2 + TB], in0=rt[2][:], in1=rt[3][:],
                                           op=ALU.add)
                kb.emit("DVE", rope2, wait=[("sDVE", ev_rt)], inc="sDVE")
            ev_rope_par[pp] = kb.ctr["sDVE"]
            ev_cs_free[pp] = kb.ctr["sDVE"]
        if which == 1:
            ev_k = kb.ctr["sDVE"]
            t_kcon = dbo.issue(lambda h: h.dma_start(
                out=_r3(kcon.ap()), in_=_s3(khl[:], DC)),
                wait=[("sDVE", ev_k)], slot="kc")
            kb.emit("POOL", lambda h: h.collective_compute(
                "AllGather", ALU.bypass, replica_groups=[list(range(n_cores))],
                ins=[kcon.ap().opt()], outs=[kbuf.ap().opt()]), wait=[t_kcon], inc="sCC")
            cc_k = kb.ctr["sCC"]
    t_pr = dbo.issue(lambda h: h.dma_start(out=allr[0:1, 0:2], in_=pbuf[:, :]),
                      wait=[("sCC", cc_scale)], slot="pr")
    kb.emit("DVE", lambda h: h.tensor_scalar(out=sc2[0:1, 0:2], in0=allr[0:1, 0:2],
                                             scalar1=1.0 / (D * F), scalar2=1e-8,
                                             op0=ALU.mult, op1=ALU.max),
            wait=[t_pr], inc="sDVE")
    ev_sc2 = kb.ctr["sDVE"]

    ones_row = sums_sb
    kb.emit("DVE", lambda h: h.memset(ones_row[0:1, 0:128], 1.0), inc="sDVE")
    ev_or = kb.ctr["sDVE"]

    def mm_svec2(h):
        return h.matmul(ps[3][:, 0:2], ones_row[0:1, 0:128], sc2[0:1, 0:2],
                        start=True, stop=True)
    kb.emit("PE", mm_svec2, wait=[("sDVE", ev_sc2), ("sDVE", ev_or)], inc="sPE")
    kb.emit("DVE", lambda h: h.tensor_copy(out=svec[:], in_=ps[3][:, 0:2]),
            wait=[("sPE", kb.ctr["sPE"])], inc="sDVE")
    kb.emit("DVE", lambda h: h.tensor_scalar(out=nsvec[:], in0=svec[:], scalar1=-1.0,
                                             scalar2=0.0, op0=ALU.mult, op1=ALU.add),
            wait=[("sDVE", kb.ctr["sDVE"])], inc="sDVE")
    ev_svec = kb.ctr["sDVE"]

    ev_qk = kb.ctr["sDVE"]

    def emit_quant_tail():
        res = {}
        st_q = {"evqf": [0, 0], "tq2": None, "i": 0}
        units = []
        for mat, qc, chunks, col in ((w1s, w1qc, ch1, 0), (w2s, w2qc, ch2, 1)):
            for (r, rr, c, cc) in chunks:
                def unit(mat=mat, qc=qc, r=r, rr=rr, c=c, cc=cc, col=col):
                    p = st_q["i"] % 2
                    tq = dqnt.issue(lambda h: h.dma_start(
                        out=qwb[p][0:rr, 0:cc], in_=mat[r:r + rr, c:c + cc]),
                        wait=[("sACT", st_q["evqf"][p]), ("sDVE", ev_acc)], slot=p)
                    kb.emit("ACT", lambda h: h.activation(
                        out=qa[0:rr, 0:cc], in_=qwb[p][0:rr, 0:cc], func=AF.Sign,
                        bias=nsvec[0:rr, col:col + 1], scale=2.0),
                        wait=[tq, ("sDVE", ev_svec)], inc="sACT")
                    kb.emit("ACT", lambda h: h.activation(
                        out=qb[0:rr, 0:cc], in_=qwb[p][0:rr, 0:cc], func=AF.Sign,
                        bias=svec[0:rr, col:col + 1], scale=2.0),
                        wait=[("sACT", kb.ctr["sACT"])], inc="sACT")
                    st_q["evqf"][p] = ev_sign = kb.ctr["sACT"]
                    kb.emit("DVE", lambda h: h.tensor_tensor(
                        out=q2b[0:rr, 0:cc], in0=qa[0:rr, 0:cc], in1=qb[0:rr, 0:cc],
                        op=ALU.add),
                        wait=[("sACT", ev_sign),
                              (st_q["tq2"] if st_q["tq2"] else ("sDVE", 0))],
                        inc="sDVE")
                    ev_q2 = kb.ctr["sDVE"]
                    st_q["tq2"] = dbo.issue(lambda h: h.dma_start(
                        out=qc[r:r + rr, c:c + cc], in_=q2b[0:rr, 0:cc]),
                        wait=[("sDVE", ev_q2)], slot="q2")
                    st_q["i"] += 1
                units.append(unit)
        res["units"] = units
        res["st_q"] = st_q
        return res


    F_proj = kb.fence() + [t_vcon, t_kcon]
    A.pop()

    aout = A.at_bf16(khl.offset // 2, DC * T)

    A.push()
    probsT = A.bf16(MAXKT * CS)
    khb = [A.bf16(DC * KL) for _ in range(2)]
    VB = 2 if R % 2 == 0 else 1
    vb = [A.bf16(VB * (D // 2)) for _ in range(2)]

    arm_streams = (dkh, dvl, dqnt, dbo)
    snap0 = kb.snap()
    ds0 = {d.name: d.state() for d in arm_streams}
    arms = []
    arm_end = None
    for core in range(n_cores):
        kb.restore(snap0)
        for d in arm_streams:
            d.set_state(ds0[d.name])
        rec = kb.new_streams()
        kb.cur = rec
        qt_res = emit_quant_tail()
        qunits = list(qt_res["units"])
        drip_ct = 0
        prev_exp_last = 0
        prev_evac = 0

        for ci, chunk in enumerate((core, NCH - 1 - core)):
            qoff = ci * CS
            Pg = chunk + 1
            ntile = Pg * R
            exp_ct = {}
            kh_used = {}
            act_chunk0 = prev_exp_last
            dve_evac0 = prev_evac
            pe_chunk0 = kb.ctr["sPE"]
            dve_chunk0 = kb.ctr["sDVE"]
            pend_sums = []
            it = 0
            for g in range(Pg):
                rg = min(g, NCH - 1 - g)
                lg = 0 if g == rg else CS
                for li in range(NKL):
                    ld = g * NKL + li
                    pp = ld % 2
                    w = [("sCC", cc_k)] + (F_proj if ld < 2 and ci == 0 else [])
                    if ld >= 2:
                        w.append(("sPE", kh_used[ld - 2]))
                    elif ci == 1:
                        w.append(("sPE", pe_chunk0))
                    tk = dkh.issue(lambda h, rg=rg, lg=lg, li=li, pp=pp: h.dma_start(
                        out=_s3(khb[pp][:], DC),
                        in_=_r3(kbuf[D * rg:D * rg + D, lg + KL * li:lg + KL * li + KL])),
                        wait=w, slot=pp)
                    for k in range(KL // 128):
                        sl = (0, 1, 3, 4, 5, 6)[it % 6]

                        def mmsc(h, pp=pp, k=k, sl=sl, qoff=qoff):
                            inst = None
                            for dc in range(DC):
                                inst = h.matmul(
                                    P(sl, CS),
                                    khb[pp][:, KL * dc + 128 * k:KL * dc + 128 * k + 128],
                                    qh[:, T * dc + qoff:T * dc + qoff + CS],
                                    start=(dc == 0), stop=(dc == DC - 1))
                            return inst
                        w = [tk]
                        w.append(("sACT", exp_ct[it - 6] if it >= 6 else act_chunk0))
                        if it < 6:
                            w.append(("sDVE", dve_evac0))
                        kb.emit("PE", mmsc, wait=w, inc="sPE")
                        ev_sc_mm = kb.ctr["sPE"]
                        kh_used[ld] = ev_sc_mm
                        kb.emit("ACT", lambda h, it=it, sl=sl: h.activation(
                            out=probsT[:, CS * it:CS * it + CS], in_=P(sl, CS),
                            func=AF.Exp, scale=SCALE),
                            wait=[("sPE", ev_sc_mm)], inc="sACT")
                        exp_ct[it] = kb.ctr["sACT"]
                        sum_wait = [("sACT", exp_ct[it])]
                        if g == chunk:
                            kt = li * (KL // 128) + k
                            kb.emit("DVE", lambda h, it=it, kt=kt: h.tensor_tensor(
                                out=probsT[:, CS * it:CS * it + CS],
                                in0=probsT[:, CS * it:CS * it + CS],
                                in1=masks[:, CS * kt:CS * kt + CS], op=ALU.mult),
                                wait=[("sACT", exp_ct[it]), t_masks], inc="sDVE")
                            sum_wait = [("sDVE", kb.ctr["sDVE"])]
                        pend_sums.append((it, sum_wait))
                        if qunits and drip_ct >= 8 and (drip_ct - 8) % 4 == 0:
                            qunits.pop(0)()
                        drip_ct += 1
                        if len(pend_sums) >= 2:
                            jt, jw = pend_sums.pop(0)
                            kb.emit("PE", lambda h, jt=jt, ntile=ntile: h.matmul(
                                ps[2][0:1, 0:CS], onesb[:, 0:1],
                                probsT[:, CS * jt:CS * jt + CS],
                                start=(jt == 0), stop=(jt == ntile - 1)),
                                wait=jw + [("sDVE", ev_const), ("sDVE", dve_chunk0)],
                                inc="sPE")
                        it += 1
            while pend_sums:
                jt, jw = pend_sums.pop(0)
                kb.emit("PE", lambda h, jt=jt, ntile=ntile: h.matmul(
                    ps[2][0:1, 0:CS], onesb[:, 0:1], probsT[:, CS * jt:CS * jt + CS],
                    start=(jt == 0), stop=(jt == ntile - 1)),
                    wait=jw + [("sDVE", ev_const), ("sDVE", dve_chunk0)], inc="sPE")
            ev_sums_mm = kb.ctr["sPE"]
            prev_exp_last = exp_ct[ntile - 1]
            kb.emit("DVE", lambda h: h.tensor_copy(out=sums_sb[0:1, 0:CS], in_=ps[2][0:1, 0:CS]),
                    wait=[("sPE", ev_sums_mm)], inc="sDVE")
            ev_sums = kb.ctr["sDVE"]

            def mm_sumt(h):
                inst = None
                for sx in range(R):
                    inst = h.matmul(ps[7][:, sx:sx + 1], sums_sb[0:1, 128 * sx:128 * sx + 128],
                                    onesf[0:1, 0:1], start=True, stop=True)
                return inst
            kb.emit("PE", mm_sumt, wait=[("sDVE", ev_sums)], inc="sPE")
            kb.emit("DVE", lambda h: h.tensor_copy(out=sumc[:, 0:R], in_=ps[7][:, 0:R]),
                    wait=[("sPE", kb.ctr["sPE"])], inc="sDVE")
            kb.emit("DVE", lambda h, ci=ci: h.reciprocal(out=rcp[:, R * ci:R * ci + R],
                                                         in_=sumc[:, 0:R]),
                    wait=[("sDVE", kb.ctr["sDVE"])], inc="sDVE")
            ev_rcp = kb.ctr["sDVE"]

            nsl = max(1, DC // 2)
            for half in range(2 if DC >= 2 else 1):
                v_used = {}
                pe_half0 = kb.ctr["sPE"]
                act_half0 = prev_exp_last
                dve_half0 = prev_evac
                it = 0
                for g in range(Pg):
                    rg = min(g, NCH - 1 - g)
                    lg = 0 if g == rg else CS
                    for k in range(R):
                        pp = (it // VB) % 2
                        if it % VB == 0:
                            row0 = T * rg + lg + 128 * k
                            w = [("sCC", cc_v)]
                            if it >= 2 * VB:
                                w.append(("sPE", v_used[it - VB - 1]))
                            else:
                                w.append(("sPE", pe_half0))
                            tv = dvl.issue(lambda h, row0=row0, half=half, pp=pp, g=g: h.dma_start(
                                out=_s3(vb[pp][:], VB),
                                in_=_r3(vbuf[row0:row0 + 128 * VB,
                                              (D // 2) * half:(D // 2) * half + D // 2])),
                                wait=w, slot=pp)

                        def mmpv(h, pp=pp, it=it, nsl=nsl, ntile=ntile):
                            inst = None
                            voff = (it % VB) * (D // 2)
                            for d8 in range(nsl):
                                inst = h.matmul(
                                    P(d8, CS), vb[pp][:, voff + 128 * d8:voff + 128 * d8 + 128],
                                    probsT[:, CS * it:CS * it + CS],
                                    start=(it == 0), stop=(it == ntile - 1))
                            return inst
                        w = [tv]
                        if it == 0:
                            w += [("sACT", act_half0), ("sPE", ev_sums_mm),
                                  ("sDVE", max(ev_rcp, dve_half0))]
                        kb.emit("PE", mmpv, wait=w, inc="sPE")
                        v_used[it] = kb.ctr["sPE"]
                        if qunits and it % 2 == 0:
                            qunits.pop(0)()
                        it += 1
                ev_pv = kb.ctr["sPE"]

                def evpv(h, half=half, qoff=qoff, nsl=nsl):
                    inst = None
                    for d8 in range(nsl):
                        dc = half * nsl + d8
                        inst = h.tensor_copy(out=aout[:, T * dc + qoff:T * dc + qoff + CS],
                                             in_=P(d8, CS))
                    return inst
                kb.emit("DVE", evpv, wait=[("sPE", ev_pv)] + (F_proj if ci == 0 and half == 0 else []),
                        inc="sDVE")
                prev_evac = kb.ctr["sDVE"]
        while qunits:
            qunits.pop(0)()
        arms.append(rec)
        if arm_end is None:
            arm_end = kb.snap()
            ds_end = {d.name: d.state() for d in arm_streams}
        else:
            assert kb.snap()[0] == arm_end[0], "arm semaphore counts diverge"

    kb.restore(arm_end)
    for d in arm_streams:
        d.set_state(ds_end[d.name])
    common2 = kb.new_streams()
    kb.cur = common2
    F_attn = kb.fence() + [qt_res["st_q"]["tq2"]]
    A.pop()
    A.pop()
    kb.emit("POOL", lambda h: h.collective_compute(
        "AllGather", ALU.bypass, replica_groups=[list(range(n_cores))],
        ins=[w1qc.ap().opt()], outs=[w1qb.ap().opt()]),
        wait=[qt_res["st_q"]["tq2"]], inc="sCC")
    cc_w1 = kb.ctr["sCC"]
    kb.emit("POOL", lambda h: h.collective_compute(
        "AllGather", ALU.bypass, replica_groups=[list(range(n_cores))],
        ins=[w2qc.ap().opt()], outs=[w2qb.ap().opt()]), wait=[], inc="sCC")
    cc_w2 = kb.ctr["sCC"]

    A.push()
    stileF = A.bf16(NT * D)
    wob = [A.bf16(DC * OGW) for _ in range(2)]
    ev_wo_used = [0, 0]
    sslot_ev = {}
    wo_done = dbo.final_tokens()
    for og in range(NOG):
        p = og % 2
        two = dseq.issue(lambda h, og=og, p=p: h.dma_start(
            out=_s3(wob[p][:], DC), in_=_r3(wodram.ap()[:, OGW * og:OGW * og + OGW])),
            wait=[("sPE", ev_wo_used[p])] + (F_attn if og < 2 else []) + wo_done,
            slot=f"wo{p}")
        for tt in range(NT):
            slot = tt % 2
            idx = og * NT + tt

            def mmo(h, p=p, tt=tt, slot=slot):
                inst = None
                for dc in range(DC):
                    inst = h.matmul(P(slot, OGW),
                                    aout[:, T * dc + 128 * tt:T * dc + 128 * tt + 128],
                                    wob[p][:, OGW * dc:OGW * dc + OGW],
                                    start=(dc == 0), stop=(dc == DC - 1))
                return inst
            w = [two] + F_attn
            if idx >= 2:
                w.append(("sDVE", sslot_ev[idx - 2]))
            kb.emit("PE", mmo, wait=w, inc="sPE")
            ev_mmo = kb.ctr["sPE"]
            kb.emit("DVE", lambda h, og=og, tt=tt, slot=slot: h.tensor_scalar(
                out=stileF[:, D * tt + OGW * og:D * tt + OGW * og + OGW], in0=P(slot, OGW),
                scalar1=rcp[:, tt:tt + 1], scalar2=0.0, op0=ALU.mult, op1=ALU.add),
                wait=[("sPE", ev_mmo)], inc="sDVE")
            sslot_ev[idx] = kb.ctr["sDVE"]
        ev_wo_used[p] = kb.ctr["sPE"]
    ev_s = kb.ctr["sDVE"]
    t_sspill = []
    for tt in range(NT):
        t_sspill.append(dbo.issue(lambda h, tt=tt: h.dma_start(
            out=sspill[128 * tt:128 * tt + 128, :],
            in_=stileF[:, D * tt:D * tt + D]),
            wait=[("sDVE", ev_s)], slot=f"sp{tt}"))
    F_wo = kb.fence() + t_sspill
    A.pop()
    A.pop()

    A.push()
    hsb = A.bf16(FC * T)
    w1b = [A.bf16(DC * 512) for _ in range(2)]
    NW1 = max(1, F // 512)
    W1W = min(512, F)
    ev_w1_used = [0, 0]
    gelu_ct = {}
    for fg in range(NW1):
        p = fg % 2
        tw1 = dw1.issue(lambda h, fg=fg, p=p: h.dma_start(
            out=_s3(w1b[p][:, 0:DC * W1W], DC), in_=_r3(w1qb[:, W1W * fg:W1W * fg + W1W])),
            wait=[("sPE", ev_w1_used[p]), ("sCC", cc_w1)] + (F_wo if fg < 2 else []), slot=p)
        NCH1 = W1W // 128
        for half in range(max(1, NCH1 // 2)):
            fcs = list(range(2 * half, min(NCH1, 2 * half + 2)))
            hidx = fg * max(1, NCH1 // 2) + half
            sbase = 4 * (hidx % 2)
            slots = {}
            for j, fc4 in enumerate(fcs):
                for tb in range(NTB):
                    slots[(fc4, tb)] = sbase + j * NTB + tb

            def mmf1(h, p=p, fcs=fcs, slots=slots):
                inst = None
                for dc in range(DC):
                    for fc4 in fcs:
                        for tb in range(NTB):
                            inst = h.matmul(
                                P(slots[(fc4, tb)], TB),
                                w1b[p][:, W1W * dc + 128 * fc4:W1W * dc + 128 * fc4 + 128],
                                xn[:, T * dc + TB * tb:T * dc + TB * tb + TB],
                                start=(dc == 0), stop=(dc == DC - 1))
                return inst
            w = [tw1]
            if hidx >= 2:
                w.append(("sACT", gelu_ct[hidx - 2]))
            else:
                w += F_wo
            kb.emit("PE", mmf1, wait=w, inc="sPE")
            ev_mmf1 = kb.ctr["sPE"]
            for fc4 in fcs:
                fc = NCH1 * fg + fc4
                for tb in range(NTB):
                    kb.emit("ACT", lambda h, fc=fc, tb=tb, slot=slots[(fc4, tb)]: h.activation(
                        out=hsb[:, T * fc + TB * tb:T * fc + TB * tb + TB], in_=P(slot, TB),
                        func=AF.Gelu, bias=b1c[:, fc:fc + 1], scale=0.5),
                        wait=[("sPE", ev_mmf1), ("sDVE", ev_bc)], inc="sACT")
            gelu_ct[hidx] = kb.ctr["sACT"]
        ev_w1_used[p] = kb.ctr["sPE"]
    ev_h = kb.ctr["sACT"]
    A.pop()
    A.push()
    hsb2 = A.bf16(FC * T)
    assert hsb2.offset == hsb.offset
    w2b = [A.bf16((FB // 128) * OG) for _ in range(2)]
    gsb = A.at_bf16(xn.offset // 2, DC * T)

    NOG2 = D // OG
    NFB = max(1, F // FB)
    NOC = OG // 128
    F_ffn1 = [("sPE", kb.ctr["sPE"]), ("sACT", ev_h)]
    ev_w2_used = [0, 0]
    ev_silu_par = [0, 0]
    for og in range(NOG2):
        gpar = og % 2
        sbase = 4 * gpar
        for fq in range(NFB):
            p = (og * NFB + fq) % 2
            tw2 = dw2.issue(lambda h, og=og, fq=fq, p=p: h.dma_start(
                out=_s3(w2b[p][:], FB // 128),
                in_=_r3(w2qb[FB * fq:FB * fq + FB, OG * og:OG * og + OG])),
                wait=[("sPE", ev_w2_used[p]), ("sCC", cc_w2)]
                + (F_ffn1 if og * NFB + fq < 2 else []), slot=p)
            def mmf2(h, p=p, fq=fq, sbase=sbase):
                inst = None
                for fc in range(FB // 128):
                    for oc in range(NOC):
                        for tb in range(NTB):
                            inst = h.matmul(
                                P(sbase + oc * NTB + tb, TB),
                                w2b[p][:, OG * fc + 128 * oc:OG * fc + 128 * oc + 128],
                                hsb[:, T * ((FB // 128) * fq + fc) + TB * tb:
                                    T * ((FB // 128) * fq + fc) + TB * tb + TB],
                                start=(fq == 0 and fc == 0),
                                stop=(fq == NFB - 1 and fc == FB // 128 - 1))
                return inst
            w = [tw2, ("sACT", ev_h)]
            if fq == 0:
                w.append(("sACT", ev_silu_par[gpar]))
            kb.emit("PE", mmf2, wait=w, inc="sPE")
            ev_w2_used[p] = kb.ctr["sPE"]
        ev_mmf2 = kb.ctr["sPE"]
        for oc in range(NOC):
            for tb in range(NTB):
                slot = sbase + oc * NTB + tb
                dc = (OG * og) // 128 + oc
                kb.emit("ACT", lambda h, dc=dc, tb=tb, slot=slot: h.activation(
                    out=gsb[:, T * dc + TB * tb:T * dc + TB * tb + TB], in_=P(slot, TB),
                    func=AF.Silu, bias=b2c[:, dc:dc + 1], scale=0.5),
                    wait=[("sPE", ev_mmf2)], inc="sACT")
        ev_silu_par[gpar] = kb.ctr["sACT"]
    ev_g = kb.ctr["sACT"]
    A.pop()

    A.push()
    gtok = [A.bf16(D) for _ in range(2)]
    srb = [A.bf16(D) for _ in range(2)]
    xrb = [A.f32(D) for _ in range(2)]
    prodb = A.f32(D)
    outb = [A.f32(D) for _ in range(2)]
    F_ffn = kb.fence()
    ev_gt_free = [0, 0]
    ev_out_done = [None, None]
    ev_pt2 = 0
    t_out = None
    for tt in range(NT):
        p = tt % 2
        tx = dxc.issue(lambda h, tt=tt, p=p: h.dma_start(
            out=xrb[p][:], in_=xloc[128 * tt:128 * tt + 128, :]),
            wait=([("sDVE", ev_gt_free[p])] if tt >= 2 else F_ffn), slot=p)
        ts2_ = dsl.issue(lambda h, tt=tt, p=p: h.dma_start(
            out=srb[p][:], in_=sspill[128 * tt:128 * tt + 128, :]),
            wait=([("sDVE", ev_gt_free[p])] if tt >= 2 else F_ffn) + [t_sspill[tt]], slot=p)
        for half in range((DC + 7) // 8):
            pt = pt6 if half % 2 == 0 else pt7
            lo, hi = 8 * half, min(DC, 8 * half + 8)

            def trg(h, tt=tt, pt=pt, lo=lo, hi=hi):
                inst = None
                for dc in range(lo, hi):
                    inst = h.transpose(pt[:, 128 * (dc - lo):128 * (dc - lo) + 128],
                                       gsb[:, T * dc + 128 * tt:T * dc + 128 * tt + 128],
                                       ident[:, 0:128])
                return inst
            kb.emit("PE", trg, wait=[("sACT", ev_g), ("sDVE", ev_pt2),
                                     ("sDVE", ev_gt_free[p])] + (F_ffn if tt == 0 else []),
                    inc="sPE")
            ev_trg = kb.ctr["sPE"]

            def evg(h, p=p, pt=pt, lo=lo, hi=hi):
                inst = None
                for dc in range(lo, hi):
                    inst = h.tensor_copy(out=gtok[p][:, 128 * dc:128 * dc + 128],
                                         in_=pt[:, 128 * (dc - lo):128 * (dc - lo) + 128])
                return inst
            kb.emit("DVE", evg, wait=[("sPE", ev_trg)], inc="sDVE")
            ev_pt2 = kb.ctr["sDVE"]
        kb.emit("DVE", lambda h, p=p: h.tensor_tensor(
            out=prodb[:], in0=srb[p][:], in1=gtok[p][:], op=ALU.mult),
            wait=[ts2_, ("sDVE", ev_pt2)], inc="sDVE")
        ev_gt_free[p] = kb.ctr["sDVE"]
        w = [tx, ("sDVE", kb.ctr["sDVE"])]
        if ev_out_done[p] is not None:
            w.append(ev_out_done[p])
        kb.emit("DVE", lambda h, p=p: h.tensor_tensor(
            out=outb[p][:], in0=xrb[p][:], in1=prodb[:], op=ALU.add),
            wait=w, inc="sDVE")
        ev_oc = kb.ctr["sDVE"]
        t_out = dst_.issue(lambda h, tt=tt, p=p: h.dma_start(
            out=out[128 * tt:128 * tt + 128, :], in_=outb[p][:]),
            wait=[("sDVE", ev_oc)], slot=p)
        ev_out_done[p] = t_out
    kb.emit("SP", lambda h: h.nop(), wait=dst_.final_tokens())
    A.pop()

    def replay(blk, streams):
        for ename, setter in (("PE", blk.tensor), ("DVE", blk.vector), ("ACT", blk.scalar),
                              ("SP", blk.sync), ("POOL", blk.gpsimd)):
            ops = streams[ename]
            setter(lambda h, ops=ops: [op() for op in ops])

    with nc.Block("pro") as blk:
        replay(blk, common1)

    engs = [kb.h[e] for e in ENGS]
    idx = nc.partition_id([ETYPE[e] for e in ENGS])
    for c in nc.Switch(engines=engs, index=[idx] * len(engs), n=n_cores):
        for ename in ENGS:
            for op in arms[c][ename]:
                op()

    with nc.Block("epi") as blk:
        replay(blk, common2)

    st.close()
    return nc


def _host_inputs(S, D, F, n_cores, q, Wq, Wk, Wv, Wo, w1, b1, w2, b2):
    NCH = 16
    CS = S // NCH
    x = np.ascontiguousarray(np.asarray(q, dtype=np.float32).reshape(S, D))
    Wq = np.asarray(Wq, np.float32)
    Wk = np.asarray(Wk, np.float32)
    Wv = np.ascontiguousarray(np.asarray(Wv, np.float32))
    Wo = np.ascontiguousarray(np.asarray(Wo, np.float32))
    w1t = np.ascontiguousarray(np.asarray(w1, np.float32).T)
    w2t = np.ascontiguousarray(np.asarray(w2, np.float32).T)
    b1 = np.asarray(b1, np.float32).reshape(1, F)
    b2 = np.asarray(b2, np.float32).reshape(1, D)

    HD = D // 2
    orig = np.empty(D, np.int64)
    pair = np.arange(HD)
    m = pair // 128
    i = pair % 128
    orig[m * 256 + i] = 2 * pair
    orig[m * 256 + 128 + i] = 2 * pair + 1
    Wq_p = np.ascontiguousarray(Wq[:, orig])
    Wk_p = np.ascontiguousarray(Wk[:, orig])

    inv_freq = 1.0 / (10000.0 ** (np.arange(0, D, 2, dtype=np.float64) / D))
    ident = np.eye(128, dtype=np.float32)
    R = CS // 128
    masks = np.zeros((128, R * CS), np.float32)
    for r in range(R):
        kvtok = r * 128 + np.arange(128)[:, None]
        qtok = np.arange(CS)[None, :]
        masks[:, r * CS:(r + 1) * CS] = (kvtok <= qtok).astype(np.float32)

    RS1 = D // n_cores
    RS2 = F // n_cores
    in_maps = []
    for c in range(n_cores):
        rows = np.r_[c * CS:(c + 1) * CS, (NCH - 1 - c) * CS:(NCH - c) * CS]
        ang = rows[:, None].astype(np.float64) * inv_freq[None, :]
        in_maps.append({
            "xloc": np.ascontiguousarray(x[rows]),
            "cosl": np.ascontiguousarray(np.cos(ang).T.astype(np.float32)),
            "sinl": np.ascontiguousarray(np.sin(ang).T.astype(np.float32)),
            "wq": Wq_p, "wk": Wk_p, "wv": Wv, "wo": Wo,
            "w1s": np.ascontiguousarray(w1t[c * RS1:(c + 1) * RS1]),
            "w2s": np.ascontiguousarray(w2t[c * RS2:(c + 1) * RS2]),
            "b1r": b1, "b2r": b2,
            "identf": ident, "masksf": masks,
        })
    return in_maps


def _assemble(S, D, n_cores, results):
    NCH = 16
    CS = S // NCH
    out = np.empty((S, D), np.float32)
    for c in range(n_cores):
        o = results[c]["out"]
        out[c * CS:(c + 1) * CS] = o[:CS]
        out[(NCH - 1 - c) * CS:(NCH - c) * CS] = o[CS:]
    return out.reshape(1, S, D)


def kernel(L0_attn_q, L0_attn_W_q, L0_attn_W_k, L0_attn_W_v, L0_attn_W_o,
           L0_ffn_w1, L0_ffn_b1, L0_ffn_w2, L0_ffn_b2):
    S, D, F, n_cores = 8192, 2048, 8192, 8
    in_maps = _host_inputs(S, D, F, n_cores, L0_attn_q, L0_attn_W_q, L0_attn_W_k,
                           L0_attn_W_v, L0_attn_W_o, L0_ffn_w1, L0_ffn_b1,
                           L0_ffn_w2, L0_ffn_b2)
    nc = build_program(S, D, F, n_cores)
    res = run_bass_kernel_spmd(nc, in_maps, core_ids=list(range(n_cores)))
    return _assemble(S, D, n_cores, res.results)


# version 22
# speedup vs baseline: 1.0122x; 1.0122x over previous
import math
from contextlib import ExitStack

import numpy as np

import concourse.bass as bass
import concourse.mybir as mybir
from concourse.bass_utils import run_bass_kernel_spmd

F32 = mybir.dt.float32
BF16 = mybir.dt.bfloat16
AF = mybir.ActivationFunctionType
ALU = mybir.AluOpType
AX = mybir.AxisListType

ENGS = ["PE", "DVE", "ACT", "SP", "POOL"]
ETYPE = {
    "PE": mybir.EngineType.PE,
    "DVE": mybir.EngineType.DVE,
    "ACT": mybir.EngineType.Activation,
    "SP": mybir.EngineType.SP,
    "POOL": mybir.EngineType.Pool,
}


class KB:

    def __init__(self, nc, stack):
        self.nc = nc
        self.h = {"PE": nc.tensor, "DVE": nc.vector, "ACT": nc.scalar,
                  "SP": nc.sync, "POOL": nc.gpsimd}
        self.sems = {}
        self.ctr = {}
        self.stack = stack
        self.cur = None
        self.waited = {}

    def sem(self, name):
        if name not in self.sems:
            self.sems[name] = self.stack.enter_context(self.nc.semaphore(name))
        if name not in self.ctr:
            self.ctr[name] = 0
        return self.sems[name]

    def new_streams(self):
        return {e: [] for e in ENGS}

    def emit(self, eng, fn, wait=(), inc=None, inc_n=1):
        if eng in ("DVE", "ACT", "POOL"):
            wait = list(wait) + [("s" + eng, self.ctr["s" + eng])]
        w = []
        for (s, v) in wait:
            if v <= 0:
                continue
            k = (eng, s)
            if self.waited.get(k, -1) < v:
                self.waited[k] = v
                w.append((s, v))
        ret = None
        if inc is not None:
            self.sem(inc)
            self.ctr[inc] += inc_n
            ret = self.ctr[inc]
        handle = self.h[eng]
        sems = self.sems

        def closure(handle=handle, w=tuple(w), fn=fn, inc=inc, inc_n=inc_n):
            for (s, v) in w:
                handle.wait_ge(sems[s], v)
            inst = fn(handle)
            if inc is not None:
                assert inst is not None, "op with inc returned no instruction"
                inst.then_inc(sems[inc], inc_n)

        self.cur[eng].append(closure)
        return ret

    def fence(self):
        return [(s, self.ctr[s]) for s in ("sPE", "sDVE", "sACT")]

    def snap(self):
        return (dict(self.ctr), dict(self.waited))

    def restore(self, s):
        self.ctr = dict(s[0])
        self.waited = dict(s[1])


class DS:

    def __init__(self, kb, name, qeng):
        self.kb = kb
        self.name = name
        self.q = qeng
        self.n = {}

    def issue(self, fn, wait=(), slot=0):
        sem = f"{self.name}_{slot}"
        self.kb.sem(sem)
        self.n[slot] = self.n.get(slot, 0) + 1
        self.kb.emit(self.q, fn, wait=wait, inc=sem, inc_n=16)
        return (sem, 16 * self.n[slot])

    def final_tokens(self):
        return [(f"{self.name}_{k}", 16 * v) for k, v in self.n.items()]

    def state(self):
        return dict(self.n)

    def set_state(self, s):
        self.n = dict(s)


class Arena:

    def __init__(self, ap, words):
        self.ap = ap
        self.words = words
        self.off = 0
        self.marks = []

    def f32(self, n):
        o = self.off
        self.off += n
        assert self.off <= self.words, f"arena overflow: {self.off} > {self.words}"
        return self.ap[:, o:o + n]

    def bf16(self, nel):
        assert nel % 2 == 0
        return self.f32(nel // 2).bitcast(BF16)

    def at_f32(self, o, n):
        assert o + n <= self.words
        return self.ap[:, o:o + n]

    def at_bf16(self, o, nel):
        return self.at_f32(o, nel // 2).bitcast(BF16)

    def push(self):
        self.marks.append(self.off)

    def pop(self):
        self.off = self.marks.pop()


def _r3(ap2d):
    return ap2d.rearrange("(a p) b -> p a b", p=128)


def _s3(sbap, a):
    return sbap.rearrange("p (a b) -> p a b", a=a)


def build_program(S, D, F, n_cores=8, arena_words=53200):
    NCH = 16
    CS = S // NCH
    T = 2 * CS
    NT = T // 128
    DC = D // 128
    FC = F // 128
    HC = DC // 2
    R = CS // 128
    TB = min(512, T)
    NTB = T // TB
    KL = CS if CS < 256 else CS // 2
    NKL = CS // KL
    MAXKT = S // 128
    SCALE = 1.0 / math.sqrt(D)
    RS1 = D // n_cores
    RS2 = F // n_cores
    QF = min(F, 512)
    QF2 = min(D, 512)
    OG = min(256, D)
    FB = min(2048, F)
    OGW = min(512, D)
    NOG = D // OGW
    NBC = (D + 511) // 512

    nc = bass.Bass("TRN2", target_bir_lowering=False, debug=False)

    def din(name, shape):
        return nc.dram_tensor(name, shape, F32, kind="ExternalInput").ap()

    xloc = din("xloc", [T, D])
    cosl = din("cosl", [D // 2, T])
    sinl = din("sinl", [D // 2, T])
    wq = din("wq", [D, D])
    wk = din("wk", [D, D])
    wv = din("wv", [D, D])
    wo = din("wo", [D, D])
    w1s = din("w1s", [RS1, F])
    w2s = din("w2s", [RS2, D])
    b1r = din("b1r", [1, F])
    b2r = din("b2r", [1, D])
    identf = din("identf", [128, 128])
    masksf = din("masksf", [128, R * CS])
    out = nc.dram_tensor("out", [T, D], F32, kind="ExternalOutput").ap()

    kcon = nc.dram_tensor("kcon", [D, T], BF16)
    vcon = nc.dram_tensor("vcon", [T, D], BF16)
    kbuf = nc.dram_tensor("kbuf", [n_cores * D, T], BF16, addr_space="Shared")
    vbuf = nc.dram_tensor("vbuf", [n_cores * T, D], BF16, addr_space="Shared")
    w1qc = nc.dram_tensor("w1qc", [RS1, F], BF16)
    w2qc = nc.dram_tensor("w2qc", [RS2, D], BF16)
    w1qb = nc.dram_tensor("w1qb", [D, F], BF16, addr_space="Shared")
    w2qb = nc.dram_tensor("w2qb", [F, D], BF16, addr_space="Shared")
    pcon = nc.dram_tensor("pcon", [1, 2], F32)
    pbuf = nc.dram_tensor("pbuf", [1, 2], F32)
    sspill = nc.dram_tensor("sspill", [T, D], BF16)
    wodram = nc.dram_tensor("wodram", [D, D], BF16)

    st = ExitStack()
    kb = KB(nc, st)
    arena_t = st.enter_context(nc.sbuf_tensor("AR", [128, arena_words], F32))
    A = Arena(arena_t[:], arena_words)

    ident = A.bf16(128)
    b1c = A.f32(FC)
    b2c = A.f32(DC)
    rcp = A.f32(NT)
    svec = A.f32(2)
    nsvec = A.f32(2)
    onesb = A.f32(1).bitcast(BF16)
    onesf = A.f32(1)
    sums_sb = A.f32(CS)
    sumc = A.f32(max(R, 2))
    allr = A.f32(2)
    sc2 = A.f32(2)
    acc = A.f32(2)
    rqt = A.f32(2)
    epsv = A.f32(1)
    persist_end = A.off

    ps = [st.enter_context(nc.psum_tensor(f"ps{i}", [128, 512], F32)) for i in range(8)]

    def P(i, n=512):
        return ps[i][:, 0:n]

    pt6 = ps[6][:].bitcast(BF16)
    pt7 = ps[7][:].bitcast(BF16)

    for e in ENGS:
        kb.sem("s" + e)
    kb.sem("sCC")

    XN_W = DC * T // 2
    xn = A.bf16(DC * T)
    scratch0 = A.off

    common1 = kb.new_streams()
    kb.cur = common1

    dmisc = DS(kb, "dmisc", "POOL")
    dseq = DS(kb, "dseq", "SP")
    dqnt = DS(kb, "dqnt", "POOL")
    dq1 = DS(kb, "dq1", "POOL")
    dbo = DS(kb, "dbo", "ACT")
    dcs = DS(kb, "dcs", "SP")
    dwp = DS(kb, "dwp", "POOL")
    dkh = DS(kb, "dkh", "SP")
    dvl = DS(kb, "dvl", "ACT")
    dw1 = DS(kb, "dw1", "SP")
    dw2 = DS(kb, "dw2", "SP")
    dsl = DS(kb, "dsl", "ACT")
    dxc = DS(kb, "dxc", "SP")
    dst_ = DS(kb, "dst", "SP")

    t_ident = dmisc.issue(lambda h: h.dma_start(out=ident[:, 0:128], in_=identf), slot="id")
    kb.emit("DVE", lambda h: h.memset(onesb[:, 0:1], 1.0), inc="sDVE")
    kb.emit("DVE", lambda h: h.memset(onesf[:, 0:1], 1.0), inc="sDVE")
    kb.emit("DVE", lambda h: h.memset(epsv[:, 0:1], 1e-5), inc="sDVE")
    ev_const = kb.ctr["sDVE"]

    A.push()
    masks = A.bf16(R * CS)
    qh = A.bf16(DC * T)
    khl = A.bf16(DC * T)
    t_masks = dmisc.issue(lambda h: h.dma_start(out=masks[:], in_=masksf), slot="mk")

    A.push()
    qwb = [A.f32(QF) for _ in range(2)]
    qa = A.f32(QF)
    qb = A.f32(QF)
    q2b = A.bf16(QF)

    A.push()
    WOB = min(512, D)
    NWOB = D // WOB
    stgw = A.f32(WOB * WOB // 128)
    wcast = [A.bf16(WOB * WOB // 128) for _ in range(2)]
    bbuf = A.f32(min(1024, F))
    xbuf = [A.f32(D) for _ in range(2)]
    xntok = [A.bf16(D) for _ in range(2)]
    bno = [A.f32(6 * NBC) for _ in range(2)]
    mv = [A.f32(2) for _ in range(2)]
    rstd = [A.f32(2) for _ in range(2)]

    BQ = min(1024, F)
    ev_bprev = 0
    for src_row, n, pslot in ((b1r, FC, 4), (b2r, DC, 5)):
        ncol = 128 * n
        for j4 in range(ncol // BQ if ncol >= BQ else 1):
            w_ = min(BQ, ncol)
            tb_ = dseq.issue(lambda h, src_row=src_row, j4=j4, w_=w_: h.dma_start(
                out=bbuf[0:1, 0:w_], in_=src_row[0:1, BQ * j4:BQ * j4 + w_]),
                wait=[("sPE", ev_bprev)], slot="bb")

            def mmb(h, j4=j4, w_=w_, pslot=pslot):
                inst = None
                for j in range(w_ // 128):
                    inst = h.matmul(ps[pslot][:, (BQ // 128) * j4 + j:(BQ // 128) * j4 + j + 1],
                                    bbuf[0:1, 128 * j:128 * j + 128],
                                    onesf[0:1, 0:1], start=True, stop=True)
                return inst
            kb.emit("PE", mmb, wait=[tb_, ("sDVE", ev_const)], inc="sPE")
            ev_bprev = kb.ctr["sPE"]
    ev_bcol_pe = kb.ctr["sPE"]
    kb.emit("DVE", lambda h: h.tensor_copy(out=b1c[:], in_=ps[4][:, 0:FC]),
            wait=[("sPE", ev_bcol_pe)], inc="sDVE")
    kb.emit("DVE", lambda h: h.tensor_copy(out=b2c[:], in_=ps[5][:, 0:DC]),
            wait=[("sDVE", kb.ctr["sDVE"])], inc="sDVE")
    ev_bc = kb.ctr["sDVE"]

    def slice_chunks(rows, cols, qf):
        res = []
        r = 0
        while r < rows:
            rr = min(128, rows - r)
            c = 0
            while c < cols:
                cc = min(qf, cols - c)
                res.append((r, rr, c, cc))
                c += cc
            r += rr
        return res

    ch1 = slice_chunks(RS1, F, QF)
    ch2 = slice_chunks(RS2, D, QF2)

    kb.emit("DVE", lambda h: h.memset(acc[:], 0.0), inc="sDVE")
    ev_accz = kb.ctr["sDVE"]
    st_p1 = {"evqf": [0, 0], "i": 0}
    p1units = []
    for mat, chunks, col in ((w1s, ch1, 0), (w2s, ch2, 1)):
        for (r, rr, c, cc) in chunks:
            def p1unit(mat=mat, r=r, rr=rr, c=c, cc=cc, col=col):
                p = st_p1["i"] % 2
                tq = dqnt.issue(lambda h: h.dma_start(
                    out=qwb[p][0:rr, 0:cc], in_=mat[r:r + rr, c:c + cc]),
                    wait=[("sDVE", st_p1["evqf"][p])], slot=f"p1{p}")
                kb.emit("DVE", lambda h: h.reduce_sum(
                    out=rqt[0:rr, 0:1], in_=qwb[p][0:rr, 0:cc], axis=AX.X,
                    apply_absolute_value=True), wait=[tq, ("sDVE", ev_accz)], inc="sDVE")
                kb.emit("DVE", lambda h: h.tensor_tensor(
                    out=acc[0:rr, col:col + 1], in0=acc[0:rr, col:col + 1],
                    in1=rqt[0:rr, 0:1], op=ALU.add),
                    wait=[("sDVE", kb.ctr["sDVE"])], inc="sDVE")
                st_p1["evqf"][p] = kb.ctr["sDVE"]
                st_p1["i"] += 1
            p1units.append(p1unit)




    st_wo = {"ld": 0, "cast": 0, "i": 0}
    wo_units = []
    for wi in range(NWOB * NWOB):
        rblk, cblk = wi // NWOB, wi % NWOB

        def wo_unit(rblk=rblk, cblk=cblk):
            i = st_wo["i"]
            pw = i % 2
            tl = dseq.issue(lambda h: h.dma_start(
                out=_s3(stgw[:], WOB // 128),
                in_=_r3(wo[WOB * rblk:WOB * rblk + WOB, WOB * cblk:WOB * cblk + WOB])),
                wait=[("sACT", st_wo["cast"])], slot="wop")
            kb.emit("ACT", lambda h: h.copy(out=wcast[pw][:], in_=stgw[:]),
                    wait=[tl, (f"dbo_wo{pw}", 16 * (i // 2)) if i >= 2 else ("sACT", 0)],
                    inc="sACT")
            st_wo["cast"] = kb.ctr["sACT"]
            dbo.issue(lambda h: h.dma_start(
                out=_r3(wodram.ap()[WOB * rblk:WOB * rblk + WOB, WOB * cblk:WOB * cblk + WOB]),
                in_=_s3(wcast[pw][:], WOB // 128)),
                wait=[("sACT", st_wo["cast"])], slot=f"wo{pw}")
            st_wo["i"] += 1
        wo_units.append(wo_unit)

    NWO_U = NWOB * NWOB
    F_wopc = [("dbo_wo0", 16 * ((NWO_U + 1) // 2)), ("dbo_wo1", 16 * (NWO_U // 2))]
    A.push()
    OGV = min(512, D)
    NOGV = D // OGV
    VCH = min(2, DC)
    NVF = DC // VCH
    NWAVE = max(1, NOGV // 2)
    WSZ = min(2, NOGV)
    vl = A.at_bf16(khl.offset // 2, NT * D)
    wvb = [A.bf16(DC * OGV) for _ in range(WSZ)]
    stgv = [A.f32(VCH * OGV) for _ in range(2)]
    st_v = {"iv": 0, "cast": {}, "used": [0] * WSZ, "slot_ev": {}, "idx": 0}

    def emit_v_og_loads(wave, j):
        og = WSZ * wave + j
        for hf in range(NVF):
            sp_ = st_v["iv"] % 2
            twv_ = dseq.issue(lambda h, og=og, hf=hf, sp_=sp_: h.dma_start(
                out=_s3(stgv[sp_][:], VCH),
                in_=_r3(wv[128 * VCH * hf:128 * VCH * (hf + 1), OGV * og:OGV * og + OGV])),
                wait=[("sACT", st_v["cast"].get(st_v["iv"] - 2, 0))],
                slot=f"wv{sp_}")
            kb.emit("ACT", lambda h, j=j, hf=hf, sp_=sp_: h.copy(
                out=wvb[j][:, VCH * OGV * hf:VCH * OGV * (hf + 1)], in_=stgv[sp_][:]),
                wait=[twv_, ("sPE", st_v["used"][j])], inc="sACT")
            st_v["cast"][st_v["iv"]] = kb.ctr["sACT"]
            st_v["iv"] += 1
        return ("sACT", kb.ctr["sACT"])

    def emit_v_tile(wave, toks, tt):
        for j in range(WSZ):
            og = WSZ * wave + j
            tquad = toks[j]
            slot = (tt * WSZ + j) % 2
            idx = st_v["idx"]

            def mmv(h, j=j, tt=tt, slot=slot):
                inst = None
                for dc in range(DC):
                    inst = h.matmul(P(slot, OGV),
                                    xn[:, T * dc + 128 * tt:T * dc + 128 * tt + 128],
                                    wvb[j][:, OGV * dc:OGV * dc + OGV],
                                    start=(dc == 0), stop=(dc == DC - 1))
                return inst
            w = [tquad, ("sDVE", ev_xn_t[tt])]
            if idx >= 2:
                w.append(("sDVE", st_v["slot_ev"][idx - 2]))
            kb.emit("PE", mmv, wait=w, inc="sPE")
            ev_mmv = kb.ctr["sPE"]
            kb.emit("DVE", lambda h, og=og, tt=tt, slot=slot: h.tensor_copy(
                out=vl[:, D * tt + OGV * og:D * tt + OGV * og + OGV], in_=P(slot, OGV)),
                wait=[("sPE", ev_mmv)], inc="sDVE")
            st_v["slot_ev"][idx] = kb.ctr["sDVE"]
            st_v["idx"] += 1
        if tt == NT - 1:
            for j in range(WSZ):
                st_v["used"][j] = kb.ctr["sPE"]

    ev_xfree = [0, 0]
    ev_ptfree = 0
    ev_xn_t = []
    v_toks0 = {}
    for tt in range(NT):
        p = tt % 2
        tx = dseq.issue(
            lambda h, tt=tt, p=p: h.dma_start(out=xbuf[p][:], in_=xloc[128 * tt:128 * tt + 128, :]),
            wait=[("sDVE", ev_xfree[p])], slot=f"x{p}")

        def bn(h, p=p):
            inst = None
            for k in range(NBC):
                w = min(512, D - 512 * k)
                inst = h.bn_stats(out=bno[p][:, 6 * k:6 * k + 6],
                                  in_=xbuf[p][:, 512 * k:512 * k + w])
            return inst
        kb.emit("DVE", bn, wait=[tx], inc="sDVE")
        kb.emit("DVE", lambda h, p=p: h.bn_aggr(out=mv[p][:], in_=bno[p][:]),
                wait=[("sDVE", kb.ctr["sDVE"])], inc="sDVE")
        ev_mv = kb.ctr["sDVE"]
        kb.emit("ACT", lambda h, p=p: h.activation(out=rstd[p][:, 0:1], in_=mv[p][:, 1:2],
                                                   func=AF.Ln, bias=epsv[:, 0:1]),
                wait=[("sDVE", ev_mv), ("sDVE", ev_const)], inc="sACT")
        kb.emit("ACT", lambda h, p=p: h.activation(out=rstd[p][:, 0:1], in_=rstd[p][:, 0:1],
                                                   func=AF.Exp, scale=-0.5),
                wait=[("sACT", kb.ctr["sACT"])], inc="sACT")
        ev_rstd = kb.ctr["sACT"]
        kb.emit("DVE", lambda h, p=p: h.tensor_scalar(out=xntok[p][:], in0=xbuf[p][:],
                                                      scalar1=mv[p][:, 0:1], scalar2=rstd[p][:, 0:1],
                                                      op0=ALU.subtract, op1=ALU.mult),
                wait=[("sACT", ev_rstd)], inc="sDVE")
        ev_xfree[p] = kb.ctr["sDVE"]
        for half in range((DC + 7) // 8):
            pt = pt6 if half % 2 == 0 else pt7
            lo, hi = 8 * half, min(DC, 8 * half + 8)

            def tr(h, p=p, pt=pt, lo=lo, hi=hi):
                inst = None
                for dc in range(lo, hi):
                    inst = h.transpose(pt[:, 128 * (dc - lo):128 * (dc - lo) + 128],
                                       xntok[p][:, 128 * dc:128 * dc + 128], ident[:, 0:128])
                return inst
            kb.emit("PE", tr, wait=[("sDVE", ev_xfree[p]), t_ident, ("sDVE", ev_ptfree)],
                    inc="sPE")
            ev_tr = kb.ctr["sPE"]

            def ev(h, pt=pt, lo=lo, hi=hi, tt=tt):
                inst = None
                for dc in range(lo, hi):
                    inst = h.tensor_copy(out=xn[:, T * dc + 128 * tt:T * dc + 128 * tt + 128],
                                         in_=pt[:, 128 * (dc - lo):128 * (dc - lo) + 128])
                return inst
            kb.emit("DVE", ev, wait=[("sPE", ev_tr)], inc="sDVE")
            ev_ptfree = kb.ctr["sDVE"]
        ev_xn_t.append(kb.ctr["sDVE"])
        for _ in range(2):
            if wo_units:
                wo_units.pop(0)()
        if tt < WSZ:
            v_toks0[tt] = emit_v_og_loads(0, tt)
        if tt >= 2:
            emit_v_tile(0, v_toks0, tt - 2)
        if p1units and tt % 2 == 1:
            p1units.pop(0)()
    while wo_units:
        wo_units.pop(0)()
    for tt in range(max(0, NT - 2), NT):
        emit_v_tile(0, v_toks0, tt)
    ev_xn = kb.ctr["sDVE"]
    for wv_ in range(1, NWAVE):
        tqw = [emit_v_og_loads(wv_, j) for j in range(WSZ)]
        for tt in range(NT):
            emit_v_tile(wv_, tqw, tt)
            if p1units and tt % 2 == 0:
                p1units.pop(0)()

    while p1units:
        p1units.pop(0)()
    ev_v = kb.ctr["sDVE"]
    t_vcon = dbo.issue(lambda h: h.dma_start(
        out=_r3(vcon.ap()), in_=_s3(vl[:], NT)),
        wait=[("sDVE", ev_v)], slot="vc")
    kb.emit("POOL", lambda h: h.collective_compute(
        "AllGather", ALU.bypass, replica_groups=[list(range(n_cores))],
        ins=[vcon.ap().opt()], outs=[vbuf.ap().opt()]), wait=[t_vcon], inc="sCC")
    cc_v = kb.ctr["sCC"]
    ev_acc = kb.ctr["sDVE"]

    def mm_part(h):
        h.matmul(ps[3][0:1, 0:1], acc[:, 0:1], onesf[:, 0:1], start=True, stop=True)
        return h.matmul(ps[3][0:1, 1:2], acc[:, 1:2], onesf[:, 0:1], start=True, stop=True)
    kb.emit("PE", mm_part, wait=[("sDVE", ev_acc)], inc="sPE")
    kb.emit("DVE", lambda h: h.tensor_copy(out=allr[0:1, 0:2], in_=ps[3][0:1, 0:2]),
            wait=[("sPE", kb.ctr["sPE"])], inc="sDVE")
    t_pw = dbo.issue(lambda h: h.dma_start(out=pcon[:, :], in_=allr[0:1, 0:2]),
                     wait=[("sDVE", kb.ctr["sDVE"])], slot="pc")
    kb.emit("POOL", lambda h: h.collective_compute(
        "AllReduce", ALU.add, replica_groups=[list(range(n_cores))],
        ins=[pcon.ap().opt()], outs=[pbuf.ap().opt()]),
        wait=[t_pw], inc="sCC")
    cc_scale = kb.ctr["sCC"]




    F_v = [("sPE", kb.ctr["sPE"]), ("sDVE", ev_v), t_vcon] + F_wopc
    A.pop()
    A.pop()
    A.push()
    cosb = [A.f32(T) for _ in range(2)]
    sinb = [A.f32(T) for _ in range(2)]
    wqb = [A.bf16(256 * DC) for _ in range(3)]
    wkb = [A.bf16(256 * DC) for _ in range(3)]
    rt = [A.f32(TB) for _ in range(4)]
    HCH = min(8, DC)
    NHF = DC // HCH
    stgp = [A.f32(HCH * 256) for _ in range(2)]
    cast_p = {}
    ip = 0

    ev_rope_par = [0, 0]
    ev_cs_free = [0, 0]
    ev_w_used = {"q": [0, 0, 0], "k": [0, 0, 0]}
    t_kcon = None
    cc_k = None
    for which, wsrc, wbt, dst, wkey in ((1, wk, None, None, "k"), (0, wq, None, None, "q")):
        wbuf = wkb if which == 1 else wqb
        dst = khl if which == 1 else qh
        for m in range(HC):
            pp = m % 2
            wp3 = m % 3
            base = 4 * pp
            tc = dcs.issue(lambda h, m=m, pp=pp: h.dma_start(
                out=cosb[pp][:], in_=cosl[128 * m:128 * m + 128, :]),
                wait=[("sDVE", ev_cs_free[pp])] + (F_v if which == 1 and m < 2 else []),
                slot=f"c{pp}")
            ts_ = dcs.issue(lambda h, m=m, pp=pp: h.dma_start(
                out=sinb[pp][:], in_=sinl[128 * m:128 * m + 128, :]), slot=f"s{pp}")
            for hf in range(NHF):
                sp_ = ip % 2
                tw_ = dseq.issue(lambda h, m=m, hf=hf, wsrc=wsrc, sp_=sp_: h.dma_start(
                    out=_s3(stgp[sp_][:], HCH),
                    in_=_r3(wsrc[128 * HCH * hf:128 * HCH * (hf + 1), 256 * m:256 * m + 256])),
                    wait=[("sACT", cast_p.get(ip - 2, 0))] + (F_v if which == 1 and m < 2 else []),
                    slot=f"wp{sp_}")
                kb.emit("ACT", lambda h, wp3=wp3, hf=hf, wbuf=wbuf, sp_=sp_: h.copy(
                    out=wbuf[wp3][:, HCH * 256 * hf:HCH * 256 * (hf + 1)], in_=stgp[sp_][:]),
                    wait=[tw_, ("sPE", ev_w_used[wkey][wp3])], inc="sACT")
                cast_p[ip] = kb.ctr["sACT"]
                ip += 1
            tw = ("sACT", kb.ctr["sACT"])

            def mmproj(h, wbuf=wbuf, wp3=wp3, base=base):
                inst = None
                for dc in range(DC):
                    for oc in range(2):
                        for tb in range(NTB):
                            slot = base + oc * NTB + tb
                            inst = h.matmul(
                                P(slot, TB),
                                wbuf[wp3][:, 256 * dc + 128 * oc:256 * dc + 128 * oc + 128],
                                xn[:, T * dc + TB * tb:T * dc + TB * tb + TB],
                                start=(dc == 0), stop=(dc == DC - 1))
                return inst
            kb.emit("PE", mmproj, wait=[tw, ("sDVE", ev_xn), ("sDVE", ev_rope_par[pp])],
                    inc="sPE")
            ev_mm = kb.ctr["sPE"]
            ev_w_used[wkey][wp3] = ev_mm
            for tb in range(NTB):
                sA = base + tb
                sB = base + NTB + tb
                o1 = T * (2 * m) + TB * tb
                o2 = T * (2 * m + 1) + TB * tb

                def rope1(h, sA=sA, sB=sB, pp=pp, tb=tb):
                    h.tensor_tensor(out=rt[0][:], in0=P(sA, TB),
                                    in1=cosb[pp][:, TB * tb:TB * tb + TB], op=ALU.mult)
                    h.tensor_tensor(out=rt[1][:], in0=P(sB, TB),
                                    in1=sinb[pp][:, TB * tb:TB * tb + TB], op=ALU.mult)
                    h.tensor_tensor(out=rt[2][:], in0=P(sA, TB),
                                    in1=sinb[pp][:, TB * tb:TB * tb + TB], op=ALU.mult)
                    return h.tensor_tensor(out=rt[3][:], in0=P(sB, TB),
                                           in1=cosb[pp][:, TB * tb:TB * tb + TB], op=ALU.mult)
                kb.emit("DVE", rope1, wait=[("sPE", ev_mm), tc, ts_], inc="sDVE")
                ev_rt = kb.ctr["sDVE"]

                def rope2(h, o1=o1, o2=o2, dst=dst):
                    h.tensor_tensor(out=dst[:, o1:o1 + TB], in0=rt[0][:], in1=rt[1][:],
                                    op=ALU.subtract)
                    return h.tensor_tensor(out=dst[:, o2:o2 + TB], in0=rt[2][:], in1=rt[3][:],
                                           op=ALU.add)
                kb.emit("DVE", rope2, wait=[("sDVE", ev_rt)], inc="sDVE")
            ev_rope_par[pp] = kb.ctr["sDVE"]
            ev_cs_free[pp] = kb.ctr["sDVE"]
        if which == 1:
            ev_k = kb.ctr["sDVE"]
            t_kcon = dbo.issue(lambda h: h.dma_start(
                out=_r3(kcon.ap()), in_=_s3(khl[:], DC)),
                wait=[("sDVE", ev_k)], slot="kc")
            kb.emit("POOL", lambda h: h.collective_compute(
                "AllGather", ALU.bypass, replica_groups=[list(range(n_cores))],
                ins=[kcon.ap().opt()], outs=[kbuf.ap().opt()]), wait=[t_kcon], inc="sCC")
            cc_k = kb.ctr["sCC"]
    t_pr = dbo.issue(lambda h: h.dma_start(out=allr[0:1, 0:2], in_=pbuf[:, :]),
                      wait=[("sCC", cc_scale)], slot="pr")
    kb.emit("DVE", lambda h: h.tensor_scalar(out=sc2[0:1, 0:2], in0=allr[0:1, 0:2],
                                             scalar1=1.0 / (D * F), scalar2=1e-8,
                                             op0=ALU.mult, op1=ALU.max),
            wait=[t_pr], inc="sDVE")
    ev_sc2 = kb.ctr["sDVE"]

    ones_row = sums_sb
    kb.emit("DVE", lambda h: h.memset(ones_row[0:1, 0:128], 1.0), inc="sDVE")
    ev_or = kb.ctr["sDVE"]

    def mm_svec2(h):
        return h.matmul(ps[3][:, 0:2], ones_row[0:1, 0:128], sc2[0:1, 0:2],
                        start=True, stop=True)
    kb.emit("PE", mm_svec2, wait=[("sDVE", ev_sc2), ("sDVE", ev_or)], inc="sPE")
    kb.emit("DVE", lambda h: h.tensor_copy(out=svec[:], in_=ps[3][:, 0:2]),
            wait=[("sPE", kb.ctr["sPE"])], inc="sDVE")
    kb.emit("DVE", lambda h: h.tensor_scalar(out=nsvec[:], in0=svec[:], scalar1=-1.0,
                                             scalar2=0.0, op0=ALU.mult, op1=ALU.add),
            wait=[("sDVE", kb.ctr["sDVE"])], inc="sDVE")
    ev_svec = kb.ctr["sDVE"]

    ev_qk = kb.ctr["sDVE"]

    def emit_quant_tail():
        res = {}
        st_q = {"evqf": [0, 0], "tq2": None, "i": 0}
        units = []
        for mat, qc, chunks, col in ((w1s, w1qc, ch1, 0), (w2s, w2qc, ch2, 1)):
            for (r, rr, c, cc) in chunks:
                def unit(mat=mat, qc=qc, r=r, rr=rr, c=c, cc=cc, col=col):
                    p = st_q["i"] % 2
                    tq = dqnt.issue(lambda h: h.dma_start(
                        out=qwb[p][0:rr, 0:cc], in_=mat[r:r + rr, c:c + cc]),
                        wait=[("sACT", st_q["evqf"][p]), ("sDVE", ev_acc)], slot=p)
                    kb.emit("ACT", lambda h: h.activation(
                        out=qa[0:rr, 0:cc], in_=qwb[p][0:rr, 0:cc], func=AF.Sign,
                        bias=nsvec[0:rr, col:col + 1], scale=2.0),
                        wait=[tq, ("sDVE", ev_svec)], inc="sACT")
                    kb.emit("ACT", lambda h: h.activation(
                        out=qb[0:rr, 0:cc], in_=qwb[p][0:rr, 0:cc], func=AF.Sign,
                        bias=svec[0:rr, col:col + 1], scale=2.0),
                        wait=[("sACT", kb.ctr["sACT"])], inc="sACT")
                    st_q["evqf"][p] = ev_sign = kb.ctr["sACT"]
                    kb.emit("DVE", lambda h: h.tensor_tensor(
                        out=q2b[0:rr, 0:cc], in0=qa[0:rr, 0:cc], in1=qb[0:rr, 0:cc],
                        op=ALU.add),
                        wait=[("sACT", ev_sign),
                              (st_q["tq2"] if st_q["tq2"] else ("sDVE", 0))],
                        inc="sDVE")
                    ev_q2 = kb.ctr["sDVE"]
                    st_q["tq2"] = dbo.issue(lambda h: h.dma_start(
                        out=qc[r:r + rr, c:c + cc], in_=q2b[0:rr, 0:cc]),
                        wait=[("sDVE", ev_q2)], slot="q2")
                    st_q["i"] += 1
                units.append(unit)
        res["units"] = units
        res["st_q"] = st_q
        return res


    F_proj = kb.fence() + [t_vcon, t_kcon]
    A.pop()

    aout = A.at_bf16(khl.offset // 2, DC * T)

    A.push()
    probsT = A.bf16(MAXKT * CS)
    khb = [A.bf16(DC * KL) for _ in range(2)]
    VB = 2 if R % 2 == 0 else 1
    vb = [A.bf16(VB * (D // 2)) for _ in range(2)]

    arm_streams = (dkh, dvl, dqnt, dbo)
    snap0 = kb.snap()
    ds0 = {d.name: d.state() for d in arm_streams}
    arms = []
    arm_end = None
    for core in range(n_cores):
        kb.restore(snap0)
        for d in arm_streams:
            d.set_state(ds0[d.name])
        rec = kb.new_streams()
        kb.cur = rec
        qt_res = emit_quant_tail()
        qunits = list(qt_res["units"])
        drip_ct = 0
        prev_exp_last = 0
        prev_evac = 0

        for ci, chunk in enumerate((core, NCH - 1 - core)):
            qoff = ci * CS
            Pg = chunk + 1
            ntile = Pg * R
            exp_ct = {}
            kh_used = {}
            act_chunk0 = prev_exp_last
            dve_evac0 = prev_evac
            pe_chunk0 = kb.ctr["sPE"]
            dve_chunk0 = kb.ctr["sDVE"]
            pend_sums = []
            it = 0
            for g in range(Pg):
                rg = min(g, NCH - 1 - g)
                lg = 0 if g == rg else CS
                for li in range(NKL):
                    ld = g * NKL + li
                    pp = ld % 2
                    w = [("sCC", cc_k)] + (F_proj if ld < 2 and ci == 0 else [])
                    if ld >= 2:
                        w.append(("sPE", kh_used[ld - 2]))
                    elif ci == 1:
                        w.append(("sPE", pe_chunk0))
                    tk = dkh.issue(lambda h, rg=rg, lg=lg, li=li, pp=pp: h.dma_start(
                        out=_s3(khb[pp][:], DC),
                        in_=_r3(kbuf[D * rg:D * rg + D, lg + KL * li:lg + KL * li + KL])),
                        wait=w, slot=pp)
                    for k in range(KL // 128):
                        sl = (0, 1, 3, 4, 5, 6)[it % 6]

                        def mmsc(h, pp=pp, k=k, sl=sl, qoff=qoff):
                            inst = None
                            for dc in range(DC):
                                inst = h.matmul(
                                    P(sl, CS),
                                    khb[pp][:, KL * dc + 128 * k:KL * dc + 128 * k + 128],
                                    qh[:, T * dc + qoff:T * dc + qoff + CS],
                                    start=(dc == 0), stop=(dc == DC - 1))
                            return inst
                        w = [tk]
                        w.append(("sACT", exp_ct[it - 6] if it >= 6 else act_chunk0))
                        if it < 6:
                            w.append(("sDVE", dve_evac0))
                        kb.emit("PE", mmsc, wait=w, inc="sPE")
                        ev_sc_mm = kb.ctr["sPE"]
                        kh_used[ld] = ev_sc_mm
                        kb.emit("ACT", lambda h, it=it, sl=sl: h.activation(
                            out=probsT[:, CS * it:CS * it + CS], in_=P(sl, CS),
                            func=AF.Exp, scale=SCALE),
                            wait=[("sPE", ev_sc_mm)], inc="sACT")
                        exp_ct[it] = kb.ctr["sACT"]
                        sum_wait = [("sACT", exp_ct[it])]
                        if g == chunk:
                            kt = li * (KL // 128) + k
                            kb.emit("DVE", lambda h, it=it, kt=kt: h.tensor_tensor(
                                out=probsT[:, CS * it:CS * it + CS],
                                in0=probsT[:, CS * it:CS * it + CS],
                                in1=masks[:, CS * kt:CS * kt + CS], op=ALU.mult),
                                wait=[("sACT", exp_ct[it]), t_masks], inc="sDVE")
                            sum_wait = [("sDVE", kb.ctr["sDVE"])]
                        pend_sums.append((it, sum_wait))
                        if qunits and drip_ct >= 8 and (drip_ct - 8) % 4 == 0:
                            qunits.pop(0)()
                        drip_ct += 1
                        if len(pend_sums) >= 2:
                            jt, jw = pend_sums.pop(0)
                            kb.emit("PE", lambda h, jt=jt, ntile=ntile: h.matmul(
                                ps[2][0:1, 0:CS], onesb[:, 0:1],
                                probsT[:, CS * jt:CS * jt + CS],
                                start=(jt == 0), stop=(jt == ntile - 1)),
                                wait=jw + [("sDVE", ev_const), ("sDVE", dve_chunk0)],
                                inc="sPE")
                        it += 1
            while pend_sums:
                jt, jw = pend_sums.pop(0)
                kb.emit("PE", lambda h, jt=jt, ntile=ntile: h.matmul(
                    ps[2][0:1, 0:CS], onesb[:, 0:1], probsT[:, CS * jt:CS * jt + CS],
                    start=(jt == 0), stop=(jt == ntile - 1)),
                    wait=jw + [("sDVE", ev_const), ("sDVE", dve_chunk0)], inc="sPE")
            ev_sums_mm = kb.ctr["sPE"]
            prev_exp_last = exp_ct[ntile - 1]
            kb.emit("DVE", lambda h: h.tensor_copy(out=sums_sb[0:1, 0:CS], in_=ps[2][0:1, 0:CS]),
                    wait=[("sPE", ev_sums_mm)], inc="sDVE")
            ev_sums = kb.ctr["sDVE"]

            def mm_sumt(h):
                inst = None
                for sx in range(R):
                    inst = h.matmul(ps[7][:, sx:sx + 1], sums_sb[0:1, 128 * sx:128 * sx + 128],
                                    onesf[0:1, 0:1], start=True, stop=True)
                return inst
            kb.emit("PE", mm_sumt, wait=[("sDVE", ev_sums)], inc="sPE")
            kb.emit("DVE", lambda h: h.tensor_copy(out=sumc[:, 0:R], in_=ps[7][:, 0:R]),
                    wait=[("sPE", kb.ctr["sPE"])], inc="sDVE")
            kb.emit("DVE", lambda h, ci=ci: h.reciprocal(out=rcp[:, R * ci:R * ci + R],
                                                         in_=sumc[:, 0:R]),
                    wait=[("sDVE", kb.ctr["sDVE"])], inc="sDVE")
            ev_rcp = kb.ctr["sDVE"]

            nsl = max(1, DC // 2)
            for half in range(2 if DC >= 2 else 1):
                v_used = {}
                pe_half0 = kb.ctr["sPE"]
                act_half0 = prev_exp_last
                dve_half0 = prev_evac
                it = 0
                for g in range(Pg):
                    rg = min(g, NCH - 1 - g)
                    lg = 0 if g == rg else CS
                    for k in range(R):
                        pp = (it // VB) % 2
                        if it % VB == 0:
                            row0 = T * rg + lg + 128 * k
                            w = [("sCC", cc_v)]
                            if it >= 2 * VB:
                                w.append(("sPE", v_used[it - VB - 1]))
                            else:
                                w.append(("sPE", pe_half0))
                            tv = dvl.issue(lambda h, row0=row0, half=half, pp=pp, g=g: h.dma_start(
                                out=_s3(vb[pp][:], VB),
                                in_=_r3(vbuf[row0:row0 + 128 * VB,
                                              (D // 2) * half:(D // 2) * half + D // 2])),
                                wait=w, slot=pp)

                        def mmpv(h, pp=pp, it=it, nsl=nsl, ntile=ntile):
                            inst = None
                            voff = (it % VB) * (D // 2)
                            for d8 in range(nsl):
                                inst = h.matmul(
                                    P(d8, CS), vb[pp][:, voff + 128 * d8:voff + 128 * d8 + 128],
                                    probsT[:, CS * it:CS * it + CS],
                                    start=(it == 0), stop=(it == ntile - 1))
                            return inst
                        w = [tv]
                        if it == 0:
                            w += [("sACT", act_half0), ("sPE", ev_sums_mm),
                                  ("sDVE", max(ev_rcp, dve_half0))]
                        kb.emit("PE", mmpv, wait=w, inc="sPE")
                        v_used[it] = kb.ctr["sPE"]
                        if qunits and it % 2 == 0:
                            qunits.pop(0)()
                        it += 1
                ev_pv = kb.ctr["sPE"]

                def evpv(h, half=half, qoff=qoff, nsl=nsl):
                    inst = None
                    for d8 in range(nsl):
                        dc = half * nsl + d8
                        inst = h.tensor_copy(out=aout[:, T * dc + qoff:T * dc + qoff + CS],
                                             in_=P(d8, CS))
                    return inst
                kb.emit("DVE", evpv, wait=[("sPE", ev_pv)] + (F_proj if ci == 0 and half == 0 else []),
                        inc="sDVE")
                prev_evac = kb.ctr["sDVE"]
        while qunits:
            qunits.pop(0)()
        arms.append(rec)
        if arm_end is None:
            arm_end = kb.snap()
            ds_end = {d.name: d.state() for d in arm_streams}
        else:
            assert kb.snap()[0] == arm_end[0], "arm semaphore counts diverge"

    kb.restore(arm_end)
    for d in arm_streams:
        d.set_state(ds_end[d.name])
    common2 = kb.new_streams()
    kb.cur = common2
    F_attn = kb.fence() + [qt_res["st_q"]["tq2"]]
    A.pop()
    A.pop()
    kb.emit("POOL", lambda h: h.collective_compute(
        "AllGather", ALU.bypass, replica_groups=[list(range(n_cores))],
        ins=[w1qc.ap().opt()], outs=[w1qb.ap().opt()]),
        wait=[qt_res["st_q"]["tq2"]], inc="sCC")
    cc_w1 = kb.ctr["sCC"]
    kb.emit("POOL", lambda h: h.collective_compute(
        "AllGather", ALU.bypass, replica_groups=[list(range(n_cores))],
        ins=[w2qc.ap().opt()], outs=[w2qb.ap().opt()]), wait=[], inc="sCC")
    cc_w2 = kb.ctr["sCC"]

    A.push()
    stileF = A.bf16(NT * D)
    wob = [A.bf16(DC * OGW) for _ in range(2)]
    ev_wo_used = [0, 0]
    sslot_ev = {}
    wo_done = dbo.final_tokens()
    for og in range(NOG):
        p = og % 2
        two = dseq.issue(lambda h, og=og, p=p: h.dma_start(
            out=_s3(wob[p][:], DC), in_=_r3(wodram.ap()[:, OGW * og:OGW * og + OGW])),
            wait=[("sPE", ev_wo_used[p])] + (F_attn if og < 2 else []) + wo_done,
            slot=f"wo{p}")
        for tt in range(NT):
            slot = tt % 2
            idx = og * NT + tt

            def mmo(h, p=p, tt=tt, slot=slot):
                inst = None
                for dc in range(DC):
                    inst = h.matmul(P(slot, OGW),
                                    aout[:, T * dc + 128 * tt:T * dc + 128 * tt + 128],
                                    wob[p][:, OGW * dc:OGW * dc + OGW],
                                    start=(dc == 0), stop=(dc == DC - 1))
                return inst
            w = [two] + F_attn
            if idx >= 2:
                w.append(("sDVE", sslot_ev[idx - 2]))
            kb.emit("PE", mmo, wait=w, inc="sPE")
            ev_mmo = kb.ctr["sPE"]
            kb.emit("DVE", lambda h, og=og, tt=tt, slot=slot: h.tensor_scalar(
                out=stileF[:, D * tt + OGW * og:D * tt + OGW * og + OGW], in0=P(slot, OGW),
                scalar1=rcp[:, tt:tt + 1], scalar2=0.0, op0=ALU.mult, op1=ALU.add),
                wait=[("sPE", ev_mmo)], inc="sDVE")
            sslot_ev[idx] = kb.ctr["sDVE"]
        ev_wo_used[p] = kb.ctr["sPE"]
    ev_s = kb.ctr["sDVE"]
    t_sspill = []
    for tt in range(NT):
        t_sspill.append(dbo.issue(lambda h, tt=tt: h.dma_start(
            out=sspill[128 * tt:128 * tt + 128, :],
            in_=stileF[:, D * tt:D * tt + D]),
            wait=[("sDVE", ev_s)], slot=f"sp{tt}"))
    F_wo = kb.fence() + t_sspill
    A.pop()
    A.pop()

    A.push()
    hsb = A.bf16(FC * T)
    w1b = [A.bf16(DC * 512) for _ in range(2)]
    NW1 = max(1, F // 512)
    W1W = min(512, F)
    ev_w1_used = [0, 0]
    gelu_ct = {}
    for fg in range(NW1):
        p = fg % 2
        tw1 = dw1.issue(lambda h, fg=fg, p=p: h.dma_start(
            out=_s3(w1b[p][:, 0:DC * W1W], DC), in_=_r3(w1qb[:, W1W * fg:W1W * fg + W1W])),
            wait=[("sPE", ev_w1_used[p]), ("sCC", cc_w1)] + (F_wo if fg < 2 else []), slot=p)
        NCH1 = W1W // 128
        for half in range(max(1, NCH1 // 2)):
            fcs = list(range(2 * half, min(NCH1, 2 * half + 2)))
            hidx = fg * max(1, NCH1 // 2) + half
            sbase = 4 * (hidx % 2)
            slots = {}
            for j, fc4 in enumerate(fcs):
                for tb in range(NTB):
                    slots[(fc4, tb)] = sbase + j * NTB + tb

            def mmf1(h, p=p, fcs=fcs, slots=slots):
                inst = None
                for dc in range(DC):
                    for fc4 in fcs:
                        for tb in range(NTB):
                            inst = h.matmul(
                                P(slots[(fc4, tb)], TB),
                                w1b[p][:, W1W * dc + 128 * fc4:W1W * dc + 128 * fc4 + 128],
                                xn[:, T * dc + TB * tb:T * dc + TB * tb + TB],
                                start=(dc == 0), stop=(dc == DC - 1))
                return inst
            w = [tw1]
            if hidx >= 2:
                w.append(("sACT", gelu_ct[hidx - 2]))
            else:
                w += F_wo
            kb.emit("PE", mmf1, wait=w, inc="sPE")
            ev_mmf1 = kb.ctr["sPE"]
            for fc4 in fcs:
                fc = NCH1 * fg + fc4
                for tb in range(NTB):
                    kb.emit("ACT", lambda h, fc=fc, tb=tb, slot=slots[(fc4, tb)]: h.activation(
                        out=hsb[:, T * fc + TB * tb:T * fc + TB * tb + TB], in_=P(slot, TB),
                        func=AF.Gelu, bias=b1c[:, fc:fc + 1], scale=0.5),
                        wait=[("sPE", ev_mmf1), ("sDVE", ev_bc)], inc="sACT")
            gelu_ct[hidx] = kb.ctr["sACT"]
        ev_w1_used[p] = kb.ctr["sPE"]
    ev_h = kb.ctr["sACT"]
    A.pop()
    A.push()
    hsb2 = A.bf16(FC * T)
    assert hsb2.offset == hsb.offset
    w2b = [A.bf16((FB // 128) * OG) for _ in range(2)]
    gsb = A.at_bf16(xn.offset // 2, DC * T)

    NOG2 = D // OG
    NFB = max(1, F // FB)
    NOC = OG // 128
    F_ffn1 = [("sPE", kb.ctr["sPE"]), ("sACT", ev_h)]
    ev_w2_used = [0, 0]
    ev_silu_par = [0, 0]
    for og in range(NOG2):
        gpar = og % 2
        sbase = 4 * gpar
        for fq in range(NFB):
            p = (og * NFB + fq) % 2
            tw2 = dw2.issue(lambda h, og=og, fq=fq, p=p: h.dma_start(
                out=_s3(w2b[p][:], FB // 128),
                in_=_r3(w2qb[FB * fq:FB * fq + FB, OG * og:OG * og + OG])),
                wait=[("sPE", ev_w2_used[p]), ("sCC", cc_w2)]
                + (F_ffn1 if og * NFB + fq < 2 else []), slot=p)
            def mmf2(h, p=p, fq=fq, sbase=sbase):
                inst = None
                for fc in range(FB // 128):
                    for oc in range(NOC):
                        for tb in range(NTB):
                            inst = h.matmul(
                                P(sbase + oc * NTB + tb, TB),
                                w2b[p][:, OG * fc + 128 * oc:OG * fc + 128 * oc + 128],
                                hsb[:, T * ((FB // 128) * fq + fc) + TB * tb:
                                    T * ((FB // 128) * fq + fc) + TB * tb + TB],
                                start=(fq == 0 and fc == 0),
                                stop=(fq == NFB - 1 and fc == FB // 128 - 1))
                return inst
            w = [tw2, ("sACT", ev_h)]
            if fq == 0:
                w.append(("sACT", ev_silu_par[gpar]))
            kb.emit("PE", mmf2, wait=w, inc="sPE")
            ev_w2_used[p] = kb.ctr["sPE"]
        ev_mmf2 = kb.ctr["sPE"]
        for oc in range(NOC):
            for tb in range(NTB):
                slot = sbase + oc * NTB + tb
                dc = (OG * og) // 128 + oc
                kb.emit("ACT", lambda h, dc=dc, tb=tb, slot=slot: h.activation(
                    out=gsb[:, T * dc + TB * tb:T * dc + TB * tb + TB], in_=P(slot, TB),
                    func=AF.Silu, bias=b2c[:, dc:dc + 1], scale=0.5),
                    wait=[("sPE", ev_mmf2)], inc="sACT")
        ev_silu_par[gpar] = kb.ctr["sACT"]
    ev_g = kb.ctr["sACT"]
    A.pop()

    A.push()
    gtok = [A.bf16(D) for _ in range(2)]
    srb = [A.bf16(D) for _ in range(2)]
    xrb = [A.f32(D) for _ in range(2)]
    prodb = A.f32(D)
    outb = [A.f32(D) for _ in range(2)]
    F_ffn = kb.fence()
    ev_gt_free = [0, 0]
    ev_out_done = [None, None]
    ev_pt2 = 0
    t_out = None
    for tt in range(NT):
        p = tt % 2
        tx = dxc.issue(lambda h, tt=tt, p=p: h.dma_start(
            out=xrb[p][:], in_=xloc[128 * tt:128 * tt + 128, :]),
            wait=([("sDVE", ev_gt_free[p])] if tt >= 2 else F_ffn), slot=p)
        ts2_ = dsl.issue(lambda h, tt=tt, p=p: h.dma_start(
            out=srb[p][:], in_=sspill[128 * tt:128 * tt + 128, :]),
            wait=([("sDVE", ev_gt_free[p])] if tt >= 2 else F_ffn) + [t_sspill[tt]], slot=p)
        for half in range((DC + 7) // 8):
            pt = pt6 if half % 2 == 0 else pt7
            lo, hi = 8 * half, min(DC, 8 * half + 8)

            def trg(h, tt=tt, pt=pt, lo=lo, hi=hi):
                inst = None
                for dc in range(lo, hi):
                    inst = h.transpose(pt[:, 128 * (dc - lo):128 * (dc - lo) + 128],
                                       gsb[:, T * dc + 128 * tt:T * dc + 128 * tt + 128],
                                       ident[:, 0:128])
                return inst
            kb.emit("PE", trg, wait=[("sACT", ev_g), ("sDVE", ev_pt2),
                                     ("sDVE", ev_gt_free[p])] + (F_ffn if tt == 0 else []),
                    inc="sPE")
            ev_trg = kb.ctr["sPE"]

            def evg(h, p=p, pt=pt, lo=lo, hi=hi):
                inst = None
                for dc in range(lo, hi):
                    inst = h.tensor_copy(out=gtok[p][:, 128 * dc:128 * dc + 128],
                                         in_=pt[:, 128 * (dc - lo):128 * (dc - lo) + 128])
                return inst
            kb.emit("DVE", evg, wait=[("sPE", ev_trg)], inc="sDVE")
            ev_pt2 = kb.ctr["sDVE"]
        kb.emit("DVE", lambda h, p=p: h.tensor_tensor(
            out=prodb[:], in0=srb[p][:], in1=gtok[p][:], op=ALU.mult),
            wait=[ts2_, ("sDVE", ev_pt2)], inc="sDVE")
        ev_gt_free[p] = kb.ctr["sDVE"]
        w = [tx, ("sDVE", kb.ctr["sDVE"])]
        if ev_out_done[p] is not None:
            w.append(ev_out_done[p])
        kb.emit("DVE", lambda h, p=p: h.tensor_tensor(
            out=outb[p][:], in0=xrb[p][:], in1=prodb[:], op=ALU.add),
            wait=w, inc="sDVE")
        ev_oc = kb.ctr["sDVE"]
        t_out = dst_.issue(lambda h, tt=tt, p=p: h.dma_start(
            out=out[128 * tt:128 * tt + 128, :], in_=outb[p][:]),
            wait=[("sDVE", ev_oc)], slot=p)
        ev_out_done[p] = t_out
    kb.emit("SP", lambda h: h.nop(), wait=dst_.final_tokens())
    A.pop()

    def replay(blk, streams):
        for ename, setter in (("PE", blk.tensor), ("DVE", blk.vector), ("ACT", blk.scalar),
                              ("SP", blk.sync), ("POOL", blk.gpsimd)):
            ops = streams[ename]
            setter(lambda h, ops=ops: [op() for op in ops])

    with nc.Block("pro") as blk:
        replay(blk, common1)

    engs = [kb.h[e] for e in ENGS]
    idx = nc.partition_id([ETYPE[e] for e in ENGS])
    for c in nc.Switch(engines=engs, index=[idx] * len(engs), n=n_cores):
        for ename in ENGS:
            for op in arms[c][ename]:
                op()

    with nc.Block("epi") as blk:
        replay(blk, common2)

    st.close()
    return nc


def _host_inputs(S, D, F, n_cores, q, Wq, Wk, Wv, Wo, w1, b1, w2, b2):
    NCH = 16
    CS = S // NCH
    x = np.ascontiguousarray(np.asarray(q, dtype=np.float32).reshape(S, D))
    Wq = np.asarray(Wq, np.float32)
    Wk = np.asarray(Wk, np.float32)
    Wv = np.ascontiguousarray(np.asarray(Wv, np.float32))
    Wo = np.ascontiguousarray(np.asarray(Wo, np.float32))
    w1t = np.ascontiguousarray(np.asarray(w1, np.float32).T)
    w2t = np.ascontiguousarray(np.asarray(w2, np.float32).T)
    b1 = np.asarray(b1, np.float32).reshape(1, F)
    b2 = np.asarray(b2, np.float32).reshape(1, D)

    HD = D // 2
    orig = np.empty(D, np.int64)
    pair = np.arange(HD)
    m = pair // 128
    i = pair % 128
    orig[m * 256 + i] = 2 * pair
    orig[m * 256 + 128 + i] = 2 * pair + 1
    Wq_p = np.ascontiguousarray(Wq[:, orig])
    Wk_p = np.ascontiguousarray(Wk[:, orig])

    inv_freq = 1.0 / (10000.0 ** (np.arange(0, D, 2, dtype=np.float64) / D))
    ident = np.eye(128, dtype=np.float32)
    R = CS // 128
    masks = np.zeros((128, R * CS), np.float32)
    for r in range(R):
        kvtok = r * 128 + np.arange(128)[:, None]
        qtok = np.arange(CS)[None, :]
        masks[:, r * CS:(r + 1) * CS] = (kvtok <= qtok).astype(np.float32)

    RS1 = D // n_cores
    RS2 = F // n_cores
    in_maps = []
    for c in range(n_cores):
        rows = np.r_[c * CS:(c + 1) * CS, (NCH - 1 - c) * CS:(NCH - c) * CS]
        ang = rows[:, None].astype(np.float64) * inv_freq[None, :]
        in_maps.append({
            "xloc": np.ascontiguousarray(x[rows]),
            "cosl": np.ascontiguousarray(np.cos(ang).T.astype(np.float32)),
            "sinl": np.ascontiguousarray(np.sin(ang).T.astype(np.float32)),
            "wq": Wq_p, "wk": Wk_p, "wv": Wv, "wo": Wo,
            "w1s": np.ascontiguousarray(w1t[c * RS1:(c + 1) * RS1]),
            "w2s": np.ascontiguousarray(w2t[c * RS2:(c + 1) * RS2]),
            "b1r": b1, "b2r": b2,
            "identf": ident, "masksf": masks,
        })
    return in_maps


def _assemble(S, D, n_cores, results):
    NCH = 16
    CS = S // NCH
    out = np.empty((S, D), np.float32)
    for c in range(n_cores):
        o = results[c]["out"]
        out[c * CS:(c + 1) * CS] = o[:CS]
        out[(NCH - 1 - c) * CS:(NCH - c) * CS] = o[CS:]
    return out.reshape(1, S, D)


def kernel(L0_attn_q, L0_attn_W_q, L0_attn_W_k, L0_attn_W_v, L0_attn_W_o,
           L0_ffn_w1, L0_ffn_b1, L0_ffn_w2, L0_ffn_b2):
    S, D, F, n_cores = 8192, 2048, 8192, 8
    in_maps = _host_inputs(S, D, F, n_cores, L0_attn_q, L0_attn_W_q, L0_attn_W_k,
                           L0_attn_W_v, L0_attn_W_o, L0_ffn_w1, L0_ffn_b1,
                           L0_ffn_w2, L0_ffn_b2)
    nc = build_program(S, D, F, n_cores)
    res = run_bass_kernel_spmd(nc, in_maps, core_ids=list(range(n_cores)))
    return _assemble(S, D, n_cores, res.results)


# version 23
# speedup vs baseline: 1.0520x; 1.0393x over previous
import math
from contextlib import ExitStack

import numpy as np

import concourse.bass as bass
import concourse.mybir as mybir
from concourse.bass_utils import run_bass_kernel_spmd

F32 = mybir.dt.float32
BF16 = mybir.dt.bfloat16
AF = mybir.ActivationFunctionType
ALU = mybir.AluOpType
AX = mybir.AxisListType

ENGS = ["PE", "DVE", "ACT", "SP", "POOL"]
ETYPE = {
    "PE": mybir.EngineType.PE,
    "DVE": mybir.EngineType.DVE,
    "ACT": mybir.EngineType.Activation,
    "SP": mybir.EngineType.SP,
    "POOL": mybir.EngineType.Pool,
}


class KB:

    def __init__(self, nc, stack):
        self.nc = nc
        self.h = {"PE": nc.tensor, "DVE": nc.vector, "ACT": nc.scalar,
                  "SP": nc.sync, "POOL": nc.gpsimd}
        self.sems = {}
        self.ctr = {}
        self.stack = stack
        self.cur = None
        self.waited = {}

    def sem(self, name):
        if name not in self.sems:
            self.sems[name] = self.stack.enter_context(self.nc.semaphore(name))
        if name not in self.ctr:
            self.ctr[name] = 0
        return self.sems[name]

    def new_streams(self):
        return {e: [] for e in ENGS}

    def emit(self, eng, fn, wait=(), inc=None, inc_n=1):
        if eng in ("DVE", "ACT", "POOL"):
            wait = list(wait) + [("s" + eng, self.ctr["s" + eng])]
        w = []
        for (s, v) in wait:
            if v <= 0:
                continue
            k = (eng, s)
            if self.waited.get(k, -1) < v:
                self.waited[k] = v
                w.append((s, v))
        ret = None
        if inc is not None:
            self.sem(inc)
            self.ctr[inc] += inc_n
            ret = self.ctr[inc]
        handle = self.h[eng]
        sems = self.sems

        def closure(handle=handle, w=tuple(w), fn=fn, inc=inc, inc_n=inc_n):
            for (s, v) in w:
                handle.wait_ge(sems[s], v)
            inst = fn(handle)
            if inc is not None:
                assert inst is not None, "op with inc returned no instruction"
                inst.then_inc(sems[inc], inc_n)

        self.cur[eng].append(closure)
        return ret

    def fence(self):
        return [(s, self.ctr[s]) for s in ("sPE", "sDVE", "sACT")]

    def snap(self):
        return (dict(self.ctr), dict(self.waited))

    def restore(self, s):
        self.ctr = dict(s[0])
        self.waited = dict(s[1])


class DS:

    def __init__(self, kb, name, qeng):
        self.kb = kb
        self.name = name
        self.q = qeng
        self.n = {}

    def issue(self, fn, wait=(), slot=0):
        sem = f"{self.name}_{slot}"
        self.kb.sem(sem)
        self.n[slot] = self.n.get(slot, 0) + 1
        self.kb.emit(self.q, fn, wait=wait, inc=sem, inc_n=16)
        return (sem, 16 * self.n[slot])

    def final_tokens(self):
        return [(f"{self.name}_{k}", 16 * v) for k, v in self.n.items()]

    def state(self):
        return dict(self.n)

    def set_state(self, s):
        self.n = dict(s)


class Arena:

    def __init__(self, ap, words):
        self.ap = ap
        self.words = words
        self.off = 0
        self.marks = []

    def f32(self, n):
        o = self.off
        self.off += n
        assert self.off <= self.words, f"arena overflow: {self.off} > {self.words}"
        return self.ap[:, o:o + n]

    def bf16(self, nel):
        assert nel % 2 == 0
        return self.f32(nel // 2).bitcast(BF16)

    def at_f32(self, o, n):
        assert o + n <= self.words
        return self.ap[:, o:o + n]

    def at_bf16(self, o, nel):
        return self.at_f32(o, nel // 2).bitcast(BF16)

    def push(self):
        self.marks.append(self.off)

    def pop(self):
        self.off = self.marks.pop()


def _r3(ap2d):
    return ap2d.rearrange("(a p) b -> p a b", p=128)


def _s3(sbap, a):
    return sbap.rearrange("p (a b) -> p a b", a=a)


def build_program(S, D, F, n_cores=8, arena_words=53200):
    NCH = 16
    CS = S // NCH
    T = 2 * CS
    NT = T // 128
    DC = D // 128
    FC = F // 128
    HC = DC // 2
    R = CS // 128
    TB = min(512, T)
    NTB = T // TB
    KL = CS if CS < 256 else CS // 2
    NKL = CS // KL
    MAXKT = S // 128
    SCALE = 1.0 / math.sqrt(D)
    RS1 = D // n_cores
    RS2 = F // n_cores
    QF = min(F, 512)
    QF2 = min(D, 512)
    OG = min(256, D)
    FB = min(2048, F)
    OGW = min(512, D)
    NOG = D // OGW
    NBC = (D + 511) // 512

    nc = bass.Bass("TRN2", target_bir_lowering=False, debug=False)

    def din(name, shape):
        return nc.dram_tensor(name, shape, F32, kind="ExternalInput").ap()

    xloc = din("xloc", [T, D])
    cosl = din("cosl", [D // 2, T])
    sinl = din("sinl", [D // 2, T])
    wq = din("wq", [D, D])
    wk = din("wk", [D, D])
    wv = din("wv", [D, D])
    wo = din("wo", [D, D])
    w1s = din("w1s", [RS1, F])
    w2s = din("w2s", [RS2, D])
    b1r = din("b1r", [1, F])
    b2r = din("b2r", [1, D])
    identf = din("identf", [128, 128])
    masksf = din("masksf", [128, R * CS])
    out = nc.dram_tensor("out", [T, D], F32, kind="ExternalOutput").ap()

    kcon = nc.dram_tensor("kcon", [D, T], BF16)
    vcon = nc.dram_tensor("vcon", [T, D], BF16)
    kbuf = nc.dram_tensor("kbuf", [n_cores * D, T], BF16, addr_space="Shared")
    vbuf = nc.dram_tensor("vbuf", [n_cores * T, D], BF16, addr_space="Shared")
    w1qc = nc.dram_tensor("w1qc", [RS1, F], BF16)
    w2qc = nc.dram_tensor("w2qc", [RS2, D], BF16)
    w1qb = nc.dram_tensor("w1qb", [D, F], BF16, addr_space="Shared")
    w2qb = nc.dram_tensor("w2qb", [F, D], BF16, addr_space="Shared")
    pcon = nc.dram_tensor("pcon", [1, 2], F32)
    pbuf = nc.dram_tensor("pbuf", [1, 2], F32)
    sspill = nc.dram_tensor("sspill", [T, D], BF16)
    wodram = nc.dram_tensor("wodram", [D, D], BF16)

    st = ExitStack()
    kb = KB(nc, st)
    arena_t = st.enter_context(nc.sbuf_tensor("AR", [128, arena_words], F32))
    A = Arena(arena_t[:], arena_words)

    ident = A.bf16(128)
    b1c = A.f32(FC)
    b2c = A.f32(DC)
    rcp = A.f32(NT)
    svec = A.f32(2)
    nsvec = A.f32(2)
    onesb = A.f32(1).bitcast(BF16)
    onesf = A.f32(1)
    sums_sb = A.f32(CS)
    sumc = A.f32(max(R, 2))
    allr = A.f32(2)
    sc2 = A.f32(2)
    acc = A.f32(2)
    rqt = A.f32(2)
    epsv = A.f32(1)
    persist_end = A.off

    ps = [st.enter_context(nc.psum_tensor(f"ps{i}", [128, 512], F32)) for i in range(8)]

    def P(i, n=512):
        return ps[i][:, 0:n]

    pt6 = ps[6][:].bitcast(BF16)
    pt7 = ps[7][:].bitcast(BF16)

    for e in ENGS:
        kb.sem("s" + e)
    kb.sem("sCC")

    XN_W = DC * T // 2
    xn = A.bf16(DC * T)
    scratch0 = A.off

    common1 = kb.new_streams()
    kb.cur = common1

    dmisc = DS(kb, "dmisc", "POOL")
    dseq = DS(kb, "dseq", "SP")
    dqnt = DS(kb, "dqnt", "POOL")
    dq1 = DS(kb, "dq1", "POOL")
    dvw = DS(kb, "dvw", "POOL")
    dbo = DS(kb, "dbo", "ACT")
    dcs = DS(kb, "dcs", "SP")
    dwp = DS(kb, "dwp", "POOL")
    dkh = DS(kb, "dkh", "SP")
    dvl = DS(kb, "dvl", "ACT")
    dw1 = DS(kb, "dw1", "SP")
    dw2 = DS(kb, "dw2", "SP")
    dsl = DS(kb, "dsl", "ACT")
    dxc = DS(kb, "dxc", "SP")
    dst_ = DS(kb, "dst", "SP")

    t_ident = dmisc.issue(lambda h: h.dma_start(out=ident[:, 0:128], in_=identf), slot="id")
    kb.emit("DVE", lambda h: h.memset(onesb[:, 0:1], 1.0), inc="sDVE")
    kb.emit("DVE", lambda h: h.memset(onesf[:, 0:1], 1.0), inc="sDVE")
    kb.emit("DVE", lambda h: h.memset(epsv[:, 0:1], 1e-5), inc="sDVE")
    ev_const = kb.ctr["sDVE"]

    A.push()
    masks = A.bf16(R * CS)
    qh = A.bf16(DC * T)
    khl = A.bf16(DC * T)
    t_masks = dmisc.issue(lambda h: h.dma_start(out=masks[:], in_=masksf), slot="mk")

    A.push()
    qwb = [A.f32(QF) for _ in range(2)]
    qa = A.f32(QF)
    qb = A.f32(QF)
    q2b = A.bf16(QF)

    A.push()
    WOB = min(512, D)
    NWOB = D // WOB
    stgw = A.f32(WOB * WOB // 128)
    wcast = [A.bf16(WOB * WOB // 128) for _ in range(2)]
    bbuf = A.f32(min(1024, F))
    xbuf = [A.f32(D) for _ in range(2)]
    xntok = [A.bf16(D) for _ in range(2)]
    bno = [A.f32(6 * NBC) for _ in range(2)]
    mv = [A.f32(2) for _ in range(2)]
    rstd = [A.f32(2) for _ in range(2)]

    BQ = min(1024, F)
    ev_bprev = 0
    for src_row, n, pslot in ((b1r, FC, 4), (b2r, DC, 5)):
        ncol = 128 * n
        for j4 in range(ncol // BQ if ncol >= BQ else 1):
            w_ = min(BQ, ncol)
            tb_ = dseq.issue(lambda h, src_row=src_row, j4=j4, w_=w_: h.dma_start(
                out=bbuf[0:1, 0:w_], in_=src_row[0:1, BQ * j4:BQ * j4 + w_]),
                wait=[("sPE", ev_bprev)], slot="bb")

            def mmb(h, j4=j4, w_=w_, pslot=pslot):
                inst = None
                for j in range(w_ // 128):
                    inst = h.matmul(ps[pslot][:, (BQ // 128) * j4 + j:(BQ // 128) * j4 + j + 1],
                                    bbuf[0:1, 128 * j:128 * j + 128],
                                    onesf[0:1, 0:1], start=True, stop=True)
                return inst
            kb.emit("PE", mmb, wait=[tb_, ("sDVE", ev_const)], inc="sPE")
            ev_bprev = kb.ctr["sPE"]
    ev_bcol_pe = kb.ctr["sPE"]
    kb.emit("DVE", lambda h: h.tensor_copy(out=b1c[:], in_=ps[4][:, 0:FC]),
            wait=[("sPE", ev_bcol_pe)], inc="sDVE")
    kb.emit("DVE", lambda h: h.tensor_copy(out=b2c[:], in_=ps[5][:, 0:DC]),
            wait=[("sDVE", kb.ctr["sDVE"])], inc="sDVE")
    ev_bc = kb.ctr["sDVE"]

    def slice_chunks(rows, cols, qf):
        res = []
        r = 0
        while r < rows:
            rr = min(128, rows - r)
            c = 0
            while c < cols:
                cc = min(qf, cols - c)
                res.append((r, rr, c, cc))
                c += cc
            r += rr
        return res

    ch1 = slice_chunks(RS1, F, QF)
    ch2 = slice_chunks(RS2, D, QF2)

    kb.emit("DVE", lambda h: h.memset(acc[:], 0.0), inc="sDVE")
    ev_accz = kb.ctr["sDVE"]
    st_p1 = {"evqf": [0, 0], "i": 0}
    p1units = []
    for mat, chunks, col in ((w1s, ch1, 0), (w2s, ch2, 1)):
        for (r, rr, c, cc) in chunks:
            def p1unit(mat=mat, r=r, rr=rr, c=c, cc=cc, col=col):
                p = st_p1["i"] % 2
                tq = dqnt.issue(lambda h: h.dma_start(
                    out=qwb[p][0:rr, 0:cc], in_=mat[r:r + rr, c:c + cc]),
                    wait=[("sDVE", st_p1["evqf"][p])], slot=f"p1{p}")
                kb.emit("DVE", lambda h: h.reduce_sum(
                    out=rqt[0:rr, 0:1], in_=qwb[p][0:rr, 0:cc], axis=AX.X,
                    apply_absolute_value=True), wait=[tq, ("sDVE", ev_accz)], inc="sDVE")
                kb.emit("DVE", lambda h: h.tensor_tensor(
                    out=acc[0:rr, col:col + 1], in0=acc[0:rr, col:col + 1],
                    in1=rqt[0:rr, 0:1], op=ALU.add),
                    wait=[("sDVE", kb.ctr["sDVE"])], inc="sDVE")
                st_p1["evqf"][p] = kb.ctr["sDVE"]
                st_p1["i"] += 1
            p1units.append(p1unit)




    st_wo = {"ld": 0, "cast": 0, "i": 0}
    wo_units = []
    for wi in range(NWOB * NWOB):
        rblk, cblk = wi // NWOB, wi % NWOB

        def wo_unit(rblk=rblk, cblk=cblk):
            i = st_wo["i"]
            pw = i % 2
            tl = dseq.issue(lambda h: h.dma_start(
                out=_s3(stgw[:], WOB // 128),
                in_=_r3(wo[WOB * rblk:WOB * rblk + WOB, WOB * cblk:WOB * cblk + WOB])),
                wait=[("sACT", st_wo["cast"])], slot="wop")
            kb.emit("ACT", lambda h: h.copy(out=wcast[pw][:], in_=stgw[:]),
                    wait=[tl, (f"dbo_wo{pw}", 16 * (i // 2)) if i >= 2 else ("sACT", 0)],
                    inc="sACT")
            st_wo["cast"] = kb.ctr["sACT"]
            dbo.issue(lambda h: h.dma_start(
                out=_r3(wodram.ap()[WOB * rblk:WOB * rblk + WOB, WOB * cblk:WOB * cblk + WOB]),
                in_=_s3(wcast[pw][:], WOB // 128)),
                wait=[("sACT", st_wo["cast"])], slot=f"wo{pw}")
            st_wo["i"] += 1
        wo_units.append(wo_unit)

    NWO_U = NWOB * NWOB
    F_wopc = [("dbo_wo0", 16 * ((NWO_U + 1) // 2)), ("dbo_wo1", 16 * (NWO_U // 2))]
    A.push()
    OGV = min(512, D)
    NOGV = D // OGV
    VCH = min(2, DC)
    NVF = DC // VCH
    NWAVE = max(1, NOGV // 2)
    WSZ = min(2, NOGV)
    vl = A.at_bf16(khl.offset // 2, NT * D)
    wvb = [A.bf16(DC * OGV) for _ in range(WSZ)]
    stgv = [A.f32(VCH * OGV) for _ in range(2)]
    st_v = {"iv": 0, "cast": {}, "used": [0] * WSZ, "slot_ev": {}, "idx": 0}

    def emit_v_og_loads(wave, j):
        og = WSZ * wave + j
        for hf in range(NVF):
            sp_ = st_v["iv"] % 2
            twv_ = dvw.issue(lambda h, og=og, hf=hf, sp_=sp_: h.dma_start(
                out=_s3(stgv[sp_][:], VCH),
                in_=_r3(wv[128 * VCH * hf:128 * VCH * (hf + 1), OGV * og:OGV * og + OGV])),
                wait=[("sACT", st_v["cast"].get(st_v["iv"] - 2, 0))],
                slot=f"wv{sp_}")
            kb.emit("ACT", lambda h, j=j, hf=hf, sp_=sp_: h.copy(
                out=wvb[j][:, VCH * OGV * hf:VCH * OGV * (hf + 1)], in_=stgv[sp_][:]),
                wait=[twv_, ("sPE", st_v["used"][j])], inc="sACT")
            st_v["cast"][st_v["iv"]] = kb.ctr["sACT"]
            st_v["iv"] += 1
        return ("sACT", kb.ctr["sACT"])

    def emit_v_tile(wave, toks, tt):
        for j in range(WSZ):
            og = WSZ * wave + j
            tquad = toks[j]
            slot = (tt * WSZ + j) % 2
            idx = st_v["idx"]

            def mmv(h, j=j, tt=tt, slot=slot):
                inst = None
                for dc in range(DC):
                    inst = h.matmul(P(slot, OGV),
                                    xn[:, T * dc + 128 * tt:T * dc + 128 * tt + 128],
                                    wvb[j][:, OGV * dc:OGV * dc + OGV],
                                    start=(dc == 0), stop=(dc == DC - 1))
                return inst
            w = [tquad, ("sDVE", ev_xn_t[tt])]
            if idx >= 2:
                w.append(("sDVE", st_v["slot_ev"][idx - 2]))
            kb.emit("PE", mmv, wait=w, inc="sPE")
            ev_mmv = kb.ctr["sPE"]
            kb.emit("DVE", lambda h, og=og, tt=tt, slot=slot: h.tensor_copy(
                out=vl[:, D * tt + OGV * og:D * tt + OGV * og + OGV], in_=P(slot, OGV)),
                wait=[("sPE", ev_mmv)], inc="sDVE")
            st_v["slot_ev"][idx] = kb.ctr["sDVE"]
            st_v["idx"] += 1
        if tt == NT - 1:
            for j in range(WSZ):
                st_v["used"][j] = kb.ctr["sPE"]

    ev_xfree = [0, 0]
    ev_ptfree = 0
    ev_xn_t = []
    v_toks0 = {}
    for tt in range(NT):
        p = tt % 2
        tx = dseq.issue(
            lambda h, tt=tt, p=p: h.dma_start(out=xbuf[p][:], in_=xloc[128 * tt:128 * tt + 128, :]),
            wait=[("sDVE", ev_xfree[p])], slot=f"x{p}")

        def bn(h, p=p):
            inst = None
            for k in range(NBC):
                w = min(512, D - 512 * k)
                inst = h.bn_stats(out=bno[p][:, 6 * k:6 * k + 6],
                                  in_=xbuf[p][:, 512 * k:512 * k + w])
            return inst
        kb.emit("DVE", bn, wait=[tx], inc="sDVE")
        kb.emit("DVE", lambda h, p=p: h.bn_aggr(out=mv[p][:], in_=bno[p][:]),
                wait=[("sDVE", kb.ctr["sDVE"])], inc="sDVE")
        ev_mv = kb.ctr["sDVE"]
        kb.emit("ACT", lambda h, p=p: h.activation(out=rstd[p][:, 0:1], in_=mv[p][:, 1:2],
                                                   func=AF.Ln, bias=epsv[:, 0:1]),
                wait=[("sDVE", ev_mv), ("sDVE", ev_const)], inc="sACT")
        kb.emit("ACT", lambda h, p=p: h.activation(out=rstd[p][:, 0:1], in_=rstd[p][:, 0:1],
                                                   func=AF.Exp, scale=-0.5),
                wait=[("sACT", kb.ctr["sACT"])], inc="sACT")
        ev_rstd = kb.ctr["sACT"]
        kb.emit("DVE", lambda h, p=p: h.tensor_scalar(out=xntok[p][:], in0=xbuf[p][:],
                                                      scalar1=mv[p][:, 0:1], scalar2=rstd[p][:, 0:1],
                                                      op0=ALU.subtract, op1=ALU.mult),
                wait=[("sACT", ev_rstd)], inc="sDVE")
        ev_xfree[p] = kb.ctr["sDVE"]
        for half in range((DC + 7) // 8):
            pt = pt6 if half % 2 == 0 else pt7
            lo, hi = 8 * half, min(DC, 8 * half + 8)

            def tr(h, p=p, pt=pt, lo=lo, hi=hi):
                inst = None
                for dc in range(lo, hi):
                    inst = h.transpose(pt[:, 128 * (dc - lo):128 * (dc - lo) + 128],
                                       xntok[p][:, 128 * dc:128 * dc + 128], ident[:, 0:128])
                return inst
            kb.emit("PE", tr, wait=[("sDVE", ev_xfree[p]), t_ident, ("sDVE", ev_ptfree)],
                    inc="sPE")
            ev_tr = kb.ctr["sPE"]

            def ev(h, pt=pt, lo=lo, hi=hi, tt=tt):
                inst = None
                for dc in range(lo, hi):
                    inst = h.tensor_copy(out=xn[:, T * dc + 128 * tt:T * dc + 128 * tt + 128],
                                         in_=pt[:, 128 * (dc - lo):128 * (dc - lo) + 128])
                return inst
            kb.emit("DVE", ev, wait=[("sPE", ev_tr)], inc="sDVE")
            ev_ptfree = kb.ctr["sDVE"]
        ev_xn_t.append(kb.ctr["sDVE"])
        for _ in range(2):
            if wo_units:
                wo_units.pop(0)()
        if tt < WSZ:
            v_toks0[tt] = emit_v_og_loads(0, tt)
        if tt >= 2:
            emit_v_tile(0, v_toks0, tt - 2)
        for _ in range(2):
            if p1units:
                p1units.pop(0)()
    while wo_units:
        wo_units.pop(0)()
    for tt in range(max(0, NT - 2), NT):
        emit_v_tile(0, v_toks0, tt)
    ev_xn = kb.ctr["sDVE"]
    for wv_ in range(1, NWAVE):
        tqw = [emit_v_og_loads(wv_, j) for j in range(WSZ)]
        for tt in range(NT):
            emit_v_tile(wv_, tqw, tt)
            for _ in range(2):
                if p1units:
                    p1units.pop(0)()

    while p1units:
        p1units.pop(0)()
    ev_v = kb.ctr["sDVE"]
    t_vcon = dbo.issue(lambda h: h.dma_start(
        out=_r3(vcon.ap()), in_=_s3(vl[:], NT)),
        wait=[("sDVE", ev_v)], slot="vc")
    kb.emit("POOL", lambda h: h.collective_compute(
        "AllGather", ALU.bypass, replica_groups=[list(range(n_cores))],
        ins=[vcon.ap().opt()], outs=[vbuf.ap().opt()]), wait=[t_vcon], inc="sCC")
    cc_v = kb.ctr["sCC"]
    ev_acc = kb.ctr["sDVE"]

    def mm_part(h):
        h.matmul(ps[3][0:1, 0:1], acc[:, 0:1], onesf[:, 0:1], start=True, stop=True)
        return h.matmul(ps[3][0:1, 1:2], acc[:, 1:2], onesf[:, 0:1], start=True, stop=True)
    kb.emit("PE", mm_part, wait=[("sDVE", ev_acc)], inc="sPE")
    kb.emit("DVE", lambda h: h.tensor_copy(out=allr[0:1, 0:2], in_=ps[3][0:1, 0:2]),
            wait=[("sPE", kb.ctr["sPE"])], inc="sDVE")
    t_pw = dbo.issue(lambda h: h.dma_start(out=pcon[:, :], in_=allr[0:1, 0:2]),
                     wait=[("sDVE", kb.ctr["sDVE"])], slot="pc")
    kb.emit("POOL", lambda h: h.collective_compute(
        "AllReduce", ALU.add, replica_groups=[list(range(n_cores))],
        ins=[pcon.ap().opt()], outs=[pbuf.ap().opt()]),
        wait=[t_pw], inc="sCC")
    cc_scale = kb.ctr["sCC"]




    F_v = [("sPE", kb.ctr["sPE"]), ("sDVE", ev_v), t_vcon] + F_wopc
    A.pop()
    A.pop()
    A.push()
    cosb = [A.f32(T) for _ in range(2)]
    sinb = [A.f32(T) for _ in range(2)]
    wqb = [A.bf16(256 * DC) for _ in range(3)]
    wkb = [A.bf16(256 * DC) for _ in range(3)]
    rt = [A.f32(TB) for _ in range(4)]
    HCH = min(8, DC)
    NHF = DC // HCH
    stgp = [A.f32(HCH * 256) for _ in range(2)]
    cast_p = {}
    ip = 0

    ev_rope_par = [0, 0]
    ev_cs_free = [0, 0]
    ev_w_used = {"q": [0, 0, 0], "k": [0, 0, 0]}
    t_kcon = None
    cc_k = None
    for which, wsrc, wbt, dst, wkey in ((1, wk, None, None, "k"), (0, wq, None, None, "q")):
        wbuf = wkb if which == 1 else wqb
        dst = khl if which == 1 else qh
        for m in range(HC):
            pp = m % 2
            wp3 = m % 3
            base = 4 * pp
            tc = dcs.issue(lambda h, m=m, pp=pp: h.dma_start(
                out=cosb[pp][:], in_=cosl[128 * m:128 * m + 128, :]),
                wait=[("sDVE", ev_cs_free[pp])] + (F_v if which == 1 and m < 2 else []),
                slot=f"c{pp}")
            ts_ = dcs.issue(lambda h, m=m, pp=pp: h.dma_start(
                out=sinb[pp][:], in_=sinl[128 * m:128 * m + 128, :]), slot=f"s{pp}")
            for hf in range(NHF):
                sp_ = ip % 2
                tw_ = dseq.issue(lambda h, m=m, hf=hf, wsrc=wsrc, sp_=sp_: h.dma_start(
                    out=_s3(stgp[sp_][:], HCH),
                    in_=_r3(wsrc[128 * HCH * hf:128 * HCH * (hf + 1), 256 * m:256 * m + 256])),
                    wait=[("sACT", cast_p.get(ip - 2, 0))] + (F_v if which == 1 and m < 2 else []),
                    slot=f"wp{sp_}")
                kb.emit("ACT", lambda h, wp3=wp3, hf=hf, wbuf=wbuf, sp_=sp_: h.copy(
                    out=wbuf[wp3][:, HCH * 256 * hf:HCH * 256 * (hf + 1)], in_=stgp[sp_][:]),
                    wait=[tw_, ("sPE", ev_w_used[wkey][wp3])], inc="sACT")
                cast_p[ip] = kb.ctr["sACT"]
                ip += 1
            tw = ("sACT", kb.ctr["sACT"])

            def mmproj(h, wbuf=wbuf, wp3=wp3, base=base):
                inst = None
                for dc in range(DC):
                    for oc in range(2):
                        for tb in range(NTB):
                            slot = base + oc * NTB + tb
                            inst = h.matmul(
                                P(slot, TB),
                                wbuf[wp3][:, 256 * dc + 128 * oc:256 * dc + 128 * oc + 128],
                                xn[:, T * dc + TB * tb:T * dc + TB * tb + TB],
                                start=(dc == 0), stop=(dc == DC - 1))
                return inst
            kb.emit("PE", mmproj, wait=[tw, ("sDVE", ev_xn), ("sDVE", ev_rope_par[pp])],
                    inc="sPE")
            ev_mm = kb.ctr["sPE"]
            ev_w_used[wkey][wp3] = ev_mm
            for tb in range(NTB):
                sA = base + tb
                sB = base + NTB + tb
                o1 = T * (2 * m) + TB * tb
                o2 = T * (2 * m + 1) + TB * tb

                def rope1(h, sA=sA, sB=sB, pp=pp, tb=tb):
                    h.tensor_tensor(out=rt[0][:], in0=P(sA, TB),
                                    in1=cosb[pp][:, TB * tb:TB * tb + TB], op=ALU.mult)
                    h.tensor_tensor(out=rt[1][:], in0=P(sB, TB),
                                    in1=sinb[pp][:, TB * tb:TB * tb + TB], op=ALU.mult)
                    h.tensor_tensor(out=rt[2][:], in0=P(sA, TB),
                                    in1=sinb[pp][:, TB * tb:TB * tb + TB], op=ALU.mult)
                    return h.tensor_tensor(out=rt[3][:], in0=P(sB, TB),
                                           in1=cosb[pp][:, TB * tb:TB * tb + TB], op=ALU.mult)
                kb.emit("DVE", rope1, wait=[("sPE", ev_mm), tc, ts_], inc="sDVE")
                ev_rt = kb.ctr["sDVE"]

                def rope2(h, o1=o1, o2=o2, dst=dst):
                    h.tensor_tensor(out=dst[:, o1:o1 + TB], in0=rt[0][:], in1=rt[1][:],
                                    op=ALU.subtract)
                    return h.tensor_tensor(out=dst[:, o2:o2 + TB], in0=rt[2][:], in1=rt[3][:],
                                           op=ALU.add)
                kb.emit("DVE", rope2, wait=[("sDVE", ev_rt)], inc="sDVE")
            ev_rope_par[pp] = kb.ctr["sDVE"]
            ev_cs_free[pp] = kb.ctr["sDVE"]
        if which == 1:
            ev_k = kb.ctr["sDVE"]
            t_kcon = dbo.issue(lambda h: h.dma_start(
                out=_r3(kcon.ap()), in_=_s3(khl[:], DC)),
                wait=[("sDVE", ev_k)], slot="kc")
            kb.emit("POOL", lambda h: h.collective_compute(
                "AllGather", ALU.bypass, replica_groups=[list(range(n_cores))],
                ins=[kcon.ap().opt()], outs=[kbuf.ap().opt()]), wait=[t_kcon], inc="sCC")
            cc_k = kb.ctr["sCC"]
    t_pr = dbo.issue(lambda h: h.dma_start(out=allr[0:1, 0:2], in_=pbuf[:, :]),
                      wait=[("sCC", cc_scale)], slot="pr")
    kb.emit("DVE", lambda h: h.tensor_scalar(out=sc2[0:1, 0:2], in0=allr[0:1, 0:2],
                                             scalar1=1.0 / (D * F), scalar2=1e-8,
                                             op0=ALU.mult, op1=ALU.max),
            wait=[t_pr], inc="sDVE")
    ev_sc2 = kb.ctr["sDVE"]

    ones_row = sums_sb
    kb.emit("DVE", lambda h: h.memset(ones_row[0:1, 0:128], 1.0), inc="sDVE")
    ev_or = kb.ctr["sDVE"]

    def mm_svec2(h):
        return h.matmul(ps[3][:, 0:2], ones_row[0:1, 0:128], sc2[0:1, 0:2],
                        start=True, stop=True)
    kb.emit("PE", mm_svec2, wait=[("sDVE", ev_sc2), ("sDVE", ev_or)], inc="sPE")
    kb.emit("DVE", lambda h: h.tensor_copy(out=svec[:], in_=ps[3][:, 0:2]),
            wait=[("sPE", kb.ctr["sPE"])], inc="sDVE")
    kb.emit("DVE", lambda h: h.tensor_scalar(out=nsvec[:], in0=svec[:], scalar1=-1.0,
                                             scalar2=0.0, op0=ALU.mult, op1=ALU.add),
            wait=[("sDVE", kb.ctr["sDVE"])], inc="sDVE")
    ev_svec = kb.ctr["sDVE"]

    ev_qk = kb.ctr["sDVE"]

    def emit_quant_tail():
        res = {}
        st_q = {"evqf": [0, 0], "tq2": None, "i": 0}
        units = []
        for mat, qc, chunks, col in ((w1s, w1qc, ch1, 0), (w2s, w2qc, ch2, 1)):
            for (r, rr, c, cc) in chunks:
                def unit(mat=mat, qc=qc, r=r, rr=rr, c=c, cc=cc, col=col):
                    p = st_q["i"] % 2
                    tq = dqnt.issue(lambda h: h.dma_start(
                        out=qwb[p][0:rr, 0:cc], in_=mat[r:r + rr, c:c + cc]),
                        wait=[("sACT", st_q["evqf"][p]), ("sDVE", ev_acc)], slot=p)
                    kb.emit("ACT", lambda h: h.activation(
                        out=qa[0:rr, 0:cc], in_=qwb[p][0:rr, 0:cc], func=AF.Sign,
                        bias=nsvec[0:rr, col:col + 1], scale=2.0),
                        wait=[tq, ("sDVE", ev_svec)], inc="sACT")
                    kb.emit("ACT", lambda h: h.activation(
                        out=qb[0:rr, 0:cc], in_=qwb[p][0:rr, 0:cc], func=AF.Sign,
                        bias=svec[0:rr, col:col + 1], scale=2.0),
                        wait=[("sACT", kb.ctr["sACT"])], inc="sACT")
                    st_q["evqf"][p] = ev_sign = kb.ctr["sACT"]
                    kb.emit("DVE", lambda h: h.tensor_tensor(
                        out=q2b[0:rr, 0:cc], in0=qa[0:rr, 0:cc], in1=qb[0:rr, 0:cc],
                        op=ALU.add),
                        wait=[("sACT", ev_sign),
                              (st_q["tq2"] if st_q["tq2"] else ("sDVE", 0))],
                        inc="sDVE")
                    ev_q2 = kb.ctr["sDVE"]
                    st_q["tq2"] = dbo.issue(lambda h: h.dma_start(
                        out=qc[r:r + rr, c:c + cc], in_=q2b[0:rr, 0:cc]),
                        wait=[("sDVE", ev_q2)], slot="q2")
                    st_q["i"] += 1
                units.append(unit)
        res["units"] = units
        res["st_q"] = st_q
        return res


    F_proj = kb.fence() + [t_vcon, t_kcon]
    A.pop()

    aout = A.at_bf16(khl.offset // 2, DC * T)

    A.push()
    probsT = A.bf16(MAXKT * CS)
    khb = [A.bf16(DC * KL) for _ in range(2)]
    VB = 2 if R % 2 == 0 else 1
    vb = [A.bf16(VB * (D // 2)) for _ in range(2)]

    arm_streams = (dkh, dvl, dqnt, dbo)
    snap0 = kb.snap()
    ds0 = {d.name: d.state() for d in arm_streams}
    arms = []
    arm_end = None
    for core in range(n_cores):
        kb.restore(snap0)
        for d in arm_streams:
            d.set_state(ds0[d.name])
        rec = kb.new_streams()
        kb.cur = rec
        qt_res = emit_quant_tail()
        qunits = list(qt_res["units"])
        drip_ct = 0
        prev_exp_last = 0
        prev_evac = 0

        for ci, chunk in enumerate((core, NCH - 1 - core)):
            qoff = ci * CS
            Pg = chunk + 1
            ntile = Pg * R
            exp_ct = {}
            kh_used = {}
            act_chunk0 = prev_exp_last
            dve_evac0 = prev_evac
            pe_chunk0 = kb.ctr["sPE"]
            dve_chunk0 = kb.ctr["sDVE"]
            pend_sums = []
            it = 0
            for g in range(Pg):
                rg = min(g, NCH - 1 - g)
                lg = 0 if g == rg else CS
                for li in range(NKL):
                    ld = g * NKL + li
                    pp = ld % 2
                    w = [("sCC", cc_k)] + (F_proj if ld < 2 and ci == 0 else [])
                    if ld >= 2:
                        w.append(("sPE", kh_used[ld - 2]))
                    elif ci == 1:
                        w.append(("sPE", pe_chunk0))
                    tk = dkh.issue(lambda h, rg=rg, lg=lg, li=li, pp=pp: h.dma_start(
                        out=_s3(khb[pp][:], DC),
                        in_=_r3(kbuf[D * rg:D * rg + D, lg + KL * li:lg + KL * li + KL])),
                        wait=w, slot=pp)
                    for k in range(KL // 128):
                        sl = (0, 1, 3, 4, 5, 6)[it % 6]

                        def mmsc(h, pp=pp, k=k, sl=sl, qoff=qoff):
                            inst = None
                            for dc in range(DC):
                                inst = h.matmul(
                                    P(sl, CS),
                                    khb[pp][:, KL * dc + 128 * k:KL * dc + 128 * k + 128],
                                    qh[:, T * dc + qoff:T * dc + qoff + CS],
                                    start=(dc == 0), stop=(dc == DC - 1))
                            return inst
                        w = [tk]
                        w.append(("sACT", exp_ct[it - 6] if it >= 6 else act_chunk0))
                        if it < 6:
                            w.append(("sDVE", dve_evac0))
                        kb.emit("PE", mmsc, wait=w, inc="sPE")
                        ev_sc_mm = kb.ctr["sPE"]
                        kh_used[ld] = ev_sc_mm
                        kb.emit("ACT", lambda h, it=it, sl=sl: h.activation(
                            out=probsT[:, CS * it:CS * it + CS], in_=P(sl, CS),
                            func=AF.Exp, scale=SCALE),
                            wait=[("sPE", ev_sc_mm)], inc="sACT")
                        exp_ct[it] = kb.ctr["sACT"]
                        sum_wait = [("sACT", exp_ct[it])]
                        if g == chunk:
                            kt = li * (KL // 128) + k
                            kb.emit("DVE", lambda h, it=it, kt=kt: h.tensor_tensor(
                                out=probsT[:, CS * it:CS * it + CS],
                                in0=probsT[:, CS * it:CS * it + CS],
                                in1=masks[:, CS * kt:CS * kt + CS], op=ALU.mult),
                                wait=[("sACT", exp_ct[it]), t_masks], inc="sDVE")
                            sum_wait = [("sDVE", kb.ctr["sDVE"])]
                        pend_sums.append((it, sum_wait))
                        if qunits and drip_ct >= 8 and (drip_ct - 8) % 4 == 0:
                            qunits.pop(0)()
                        drip_ct += 1
                        if len(pend_sums) >= 2:
                            jt, jw = pend_sums.pop(0)
                            kb.emit("PE", lambda h, jt=jt, ntile=ntile: h.matmul(
                                ps[2][0:1, 0:CS], onesb[:, 0:1],
                                probsT[:, CS * jt:CS * jt + CS],
                                start=(jt == 0), stop=(jt == ntile - 1)),
                                wait=jw + [("sDVE", ev_const), ("sDVE", dve_chunk0)],
                                inc="sPE")
                        it += 1
            while pend_sums:
                jt, jw = pend_sums.pop(0)
                kb.emit("PE", lambda h, jt=jt, ntile=ntile: h.matmul(
                    ps[2][0:1, 0:CS], onesb[:, 0:1], probsT[:, CS * jt:CS * jt + CS],
                    start=(jt == 0), stop=(jt == ntile - 1)),
                    wait=jw + [("sDVE", ev_const), ("sDVE", dve_chunk0)], inc="sPE")
            ev_sums_mm = kb.ctr["sPE"]
            prev_exp_last = exp_ct[ntile - 1]
            kb.emit("DVE", lambda h: h.tensor_copy(out=sums_sb[0:1, 0:CS], in_=ps[2][0:1, 0:CS]),
                    wait=[("sPE", ev_sums_mm)], inc="sDVE")
            ev_sums = kb.ctr["sDVE"]

            def mm_sumt(h):
                inst = None
                for sx in range(R):
                    inst = h.matmul(ps[7][:, sx:sx + 1], sums_sb[0:1, 128 * sx:128 * sx + 128],
                                    onesf[0:1, 0:1], start=True, stop=True)
                return inst
            kb.emit("PE", mm_sumt, wait=[("sDVE", ev_sums)], inc="sPE")
            kb.emit("DVE", lambda h: h.tensor_copy(out=sumc[:, 0:R], in_=ps[7][:, 0:R]),
                    wait=[("sPE", kb.ctr["sPE"])], inc="sDVE")
            kb.emit("DVE", lambda h, ci=ci: h.reciprocal(out=rcp[:, R * ci:R * ci + R],
                                                         in_=sumc[:, 0:R]),
                    wait=[("sDVE", kb.ctr["sDVE"])], inc="sDVE")
            ev_rcp = kb.ctr["sDVE"]

            nsl = max(1, DC // 2)
            for half in range(2 if DC >= 2 else 1):
                v_used = {}
                pe_half0 = kb.ctr["sPE"]
                act_half0 = prev_exp_last
                dve_half0 = prev_evac
                it = 0
                for g in range(Pg):
                    rg = min(g, NCH - 1 - g)
                    lg = 0 if g == rg else CS
                    for k in range(R):
                        pp = (it // VB) % 2
                        if it % VB == 0:
                            row0 = T * rg + lg + 128 * k
                            w = [("sCC", cc_v)]
                            if it >= 2 * VB:
                                w.append(("sPE", v_used[it - VB - 1]))
                            else:
                                w.append(("sPE", pe_half0))
                            tv = dvl.issue(lambda h, row0=row0, half=half, pp=pp, g=g: h.dma_start(
                                out=_s3(vb[pp][:], VB),
                                in_=_r3(vbuf[row0:row0 + 128 * VB,
                                              (D // 2) * half:(D // 2) * half + D // 2])),
                                wait=w, slot=pp)

                        def mmpv(h, pp=pp, it=it, nsl=nsl, ntile=ntile):
                            inst = None
                            voff = (it % VB) * (D // 2)
                            for d8 in range(nsl):
                                inst = h.matmul(
                                    P(d8, CS), vb[pp][:, voff + 128 * d8:voff + 128 * d8 + 128],
                                    probsT[:, CS * it:CS * it + CS],
                                    start=(it == 0), stop=(it == ntile - 1))
                            return inst
                        w = [tv]
                        if it == 0:
                            w += [("sACT", act_half0), ("sPE", ev_sums_mm),
                                  ("sDVE", max(ev_rcp, dve_half0))]
                        kb.emit("PE", mmpv, wait=w, inc="sPE")
                        v_used[it] = kb.ctr["sPE"]
                        if qunits and it % 2 == 0:
                            qunits.pop(0)()
                        it += 1
                ev_pv = kb.ctr["sPE"]

                def evpv(h, half=half, qoff=qoff, nsl=nsl):
                    inst = None
                    for d8 in range(nsl):
                        dc = half * nsl + d8
                        inst = h.tensor_copy(out=aout[:, T * dc + qoff:T * dc + qoff + CS],
                                             in_=P(d8, CS))
                    return inst
                kb.emit("DVE", evpv, wait=[("sPE", ev_pv)] + (F_proj if ci == 0 and half == 0 else []),
                        inc="sDVE")
                prev_evac = kb.ctr["sDVE"]
        while qunits:
            qunits.pop(0)()
        arms.append(rec)
        if arm_end is None:
            arm_end = kb.snap()
            ds_end = {d.name: d.state() for d in arm_streams}
        else:
            assert kb.snap()[0] == arm_end[0], "arm semaphore counts diverge"

    kb.restore(arm_end)
    for d in arm_streams:
        d.set_state(ds_end[d.name])
    common2 = kb.new_streams()
    kb.cur = common2
    F_attn = kb.fence() + [qt_res["st_q"]["tq2"]]
    A.pop()
    A.pop()
    kb.emit("POOL", lambda h: h.collective_compute(
        "AllGather", ALU.bypass, replica_groups=[list(range(n_cores))],
        ins=[w1qc.ap().opt()], outs=[w1qb.ap().opt()]),
        wait=[qt_res["st_q"]["tq2"]], inc="sCC")
    cc_w1 = kb.ctr["sCC"]
    kb.emit("POOL", lambda h: h.collective_compute(
        "AllGather", ALU.bypass, replica_groups=[list(range(n_cores))],
        ins=[w2qc.ap().opt()], outs=[w2qb.ap().opt()]), wait=[], inc="sCC")
    cc_w2 = kb.ctr["sCC"]

    A.push()
    stileF = A.bf16(NT * D)
    wob = [A.bf16(DC * OGW) for _ in range(2)]
    ev_wo_used = [0, 0]
    sslot_ev = {}
    wo_done = dbo.final_tokens()
    for og in range(NOG):
        p = og % 2
        two = dseq.issue(lambda h, og=og, p=p: h.dma_start(
            out=_s3(wob[p][:], DC), in_=_r3(wodram.ap()[:, OGW * og:OGW * og + OGW])),
            wait=[("sPE", ev_wo_used[p])] + (F_attn if og < 2 else []) + wo_done,
            slot=f"wo{p}")
        for tt in range(NT):
            slot = tt % 2
            idx = og * NT + tt

            def mmo(h, p=p, tt=tt, slot=slot):
                inst = None
                for dc in range(DC):
                    inst = h.matmul(P(slot, OGW),
                                    aout[:, T * dc + 128 * tt:T * dc + 128 * tt + 128],
                                    wob[p][:, OGW * dc:OGW * dc + OGW],
                                    start=(dc == 0), stop=(dc == DC - 1))
                return inst
            w = [two] + F_attn
            if idx >= 2:
                w.append(("sDVE", sslot_ev[idx - 2]))
            kb.emit("PE", mmo, wait=w, inc="sPE")
            ev_mmo = kb.ctr["sPE"]
            kb.emit("DVE", lambda h, og=og, tt=tt, slot=slot: h.tensor_scalar(
                out=stileF[:, D * tt + OGW * og:D * tt + OGW * og + OGW], in0=P(slot, OGW),
                scalar1=rcp[:, tt:tt + 1], scalar2=0.0, op0=ALU.mult, op1=ALU.add),
                wait=[("sPE", ev_mmo)], inc="sDVE")
            sslot_ev[idx] = kb.ctr["sDVE"]
        ev_wo_used[p] = kb.ctr["sPE"]
    ev_s = kb.ctr["sDVE"]
    t_sspill = []
    for tt in range(NT):
        t_sspill.append(dbo.issue(lambda h, tt=tt: h.dma_start(
            out=sspill[128 * tt:128 * tt + 128, :],
            in_=stileF[:, D * tt:D * tt + D]),
            wait=[("sDVE", ev_s)], slot=f"sp{tt}"))
    F_wo = kb.fence() + t_sspill
    A.pop()
    A.pop()

    A.push()
    hsb = A.bf16(FC * T)
    w1b = [A.bf16(DC * 512) for _ in range(2)]
    NW1 = max(1, F // 512)
    W1W = min(512, F)
    ev_w1_used = [0, 0]
    gelu_ct = {}
    for fg in range(NW1):
        p = fg % 2
        tw1 = dw1.issue(lambda h, fg=fg, p=p: h.dma_start(
            out=_s3(w1b[p][:, 0:DC * W1W], DC), in_=_r3(w1qb[:, W1W * fg:W1W * fg + W1W])),
            wait=[("sPE", ev_w1_used[p]), ("sCC", cc_w1)] + (F_wo if fg < 2 else []), slot=p)
        NCH1 = W1W // 128
        for half in range(max(1, NCH1 // 2)):
            fcs = list(range(2 * half, min(NCH1, 2 * half + 2)))
            hidx = fg * max(1, NCH1 // 2) + half
            sbase = 4 * (hidx % 2)
            slots = {}
            for j, fc4 in enumerate(fcs):
                for tb in range(NTB):
                    slots[(fc4, tb)] = sbase + j * NTB + tb

            def mmf1(h, p=p, fcs=fcs, slots=slots):
                inst = None
                for dc in range(DC):
                    for fc4 in fcs:
                        for tb in range(NTB):
                            inst = h.matmul(
                                P(slots[(fc4, tb)], TB),
                                w1b[p][:, W1W * dc + 128 * fc4:W1W * dc + 128 * fc4 + 128],
                                xn[:, T * dc + TB * tb:T * dc + TB * tb + TB],
                                start=(dc == 0), stop=(dc == DC - 1))
                return inst
            w = [tw1]
            if hidx >= 2:
                w.append(("sACT", gelu_ct[hidx - 2]))
            else:
                w += F_wo
            kb.emit("PE", mmf1, wait=w, inc="sPE")
            ev_mmf1 = kb.ctr["sPE"]
            for fc4 in fcs:
                fc = NCH1 * fg + fc4
                for tb in range(NTB):
                    kb.emit("ACT", lambda h, fc=fc, tb=tb, slot=slots[(fc4, tb)]: h.activation(
                        out=hsb[:, T * fc + TB * tb:T * fc + TB * tb + TB], in_=P(slot, TB),
                        func=AF.Gelu, bias=b1c[:, fc:fc + 1], scale=0.5),
                        wait=[("sPE", ev_mmf1), ("sDVE", ev_bc)], inc="sACT")
            gelu_ct[hidx] = kb.ctr["sACT"]
        ev_w1_used[p] = kb.ctr["sPE"]
    ev_h = kb.ctr["sACT"]
    A.pop()
    A.push()
    hsb2 = A.bf16(FC * T)
    assert hsb2.offset == hsb.offset
    w2b = [A.bf16((FB // 128) * OG) for _ in range(2)]
    gsb = A.at_bf16(xn.offset // 2, DC * T)

    NOG2 = D // OG
    NFB = max(1, F // FB)
    NOC = OG // 128
    F_ffn1 = [("sPE", kb.ctr["sPE"]), ("sACT", ev_h)]
    ev_w2_used = [0, 0]
    ev_silu_par = [0, 0]
    for og in range(NOG2):
        gpar = og % 2
        sbase = 4 * gpar
        for fq in range(NFB):
            p = (og * NFB + fq) % 2
            tw2 = dw2.issue(lambda h, og=og, fq=fq, p=p: h.dma_start(
                out=_s3(w2b[p][:], FB // 128),
                in_=_r3(w2qb[FB * fq:FB * fq + FB, OG * og:OG * og + OG])),
                wait=[("sPE", ev_w2_used[p]), ("sCC", cc_w2)]
                + (F_ffn1 if og * NFB + fq < 2 else []), slot=p)
            def mmf2(h, p=p, fq=fq, sbase=sbase):
                inst = None
                for fc in range(FB // 128):
                    for oc in range(NOC):
                        for tb in range(NTB):
                            inst = h.matmul(
                                P(sbase + oc * NTB + tb, TB),
                                w2b[p][:, OG * fc + 128 * oc:OG * fc + 128 * oc + 128],
                                hsb[:, T * ((FB // 128) * fq + fc) + TB * tb:
                                    T * ((FB // 128) * fq + fc) + TB * tb + TB],
                                start=(fq == 0 and fc == 0),
                                stop=(fq == NFB - 1 and fc == FB // 128 - 1))
                return inst
            w = [tw2, ("sACT", ev_h)]
            if fq == 0:
                w.append(("sACT", ev_silu_par[gpar]))
            kb.emit("PE", mmf2, wait=w, inc="sPE")
            ev_w2_used[p] = kb.ctr["sPE"]
        ev_mmf2 = kb.ctr["sPE"]
        for oc in range(NOC):
            for tb in range(NTB):
                slot = sbase + oc * NTB + tb
                dc = (OG * og) // 128 + oc
                kb.emit("ACT", lambda h, dc=dc, tb=tb, slot=slot: h.activation(
                    out=gsb[:, T * dc + TB * tb:T * dc + TB * tb + TB], in_=P(slot, TB),
                    func=AF.Silu, bias=b2c[:, dc:dc + 1], scale=0.5),
                    wait=[("sPE", ev_mmf2)], inc="sACT")
        ev_silu_par[gpar] = kb.ctr["sACT"]
    ev_g = kb.ctr["sACT"]
    A.pop()

    A.push()
    gtok = [A.bf16(D) for _ in range(2)]
    srb = [A.bf16(D) for _ in range(2)]
    xrb = [A.f32(D) for _ in range(2)]
    prodb = A.f32(D)
    outb = [A.f32(D) for _ in range(2)]
    F_ffn = kb.fence()
    ev_gt_free = [0, 0]
    ev_out_done = [None, None]
    ev_pt2 = 0
    t_out = None
    for tt in range(NT):
        p = tt % 2
        tx = dxc.issue(lambda h, tt=tt, p=p: h.dma_start(
            out=xrb[p][:], in_=xloc[128 * tt:128 * tt + 128, :]),
            wait=([("sDVE", ev_gt_free[p])] if tt >= 2 else F_ffn), slot=p)
        ts2_ = dsl.issue(lambda h, tt=tt, p=p: h.dma_start(
            out=srb[p][:], in_=sspill[128 * tt:128 * tt + 128, :]),
            wait=([("sDVE", ev_gt_free[p])] if tt >= 2 else F_ffn) + [t_sspill[tt]], slot=p)
        for half in range((DC + 7) // 8):
            pt = pt6 if half % 2 == 0 else pt7
            lo, hi = 8 * half, min(DC, 8 * half + 8)

            def trg(h, tt=tt, pt=pt, lo=lo, hi=hi):
                inst = None
                for dc in range(lo, hi):
                    inst = h.transpose(pt[:, 128 * (dc - lo):128 * (dc - lo) + 128],
                                       gsb[:, T * dc + 128 * tt:T * dc + 128 * tt + 128],
                                       ident[:, 0:128])
                return inst
            kb.emit("PE", trg, wait=[("sACT", ev_g), ("sDVE", ev_pt2),
                                     ("sDVE", ev_gt_free[p])] + (F_ffn if tt == 0 else []),
                    inc="sPE")
            ev_trg = kb.ctr["sPE"]

            def evg(h, p=p, pt=pt, lo=lo, hi=hi):
                inst = None
                for dc in range(lo, hi):
                    inst = h.tensor_copy(out=gtok[p][:, 128 * dc:128 * dc + 128],
                                         in_=pt[:, 128 * (dc - lo):128 * (dc - lo) + 128])
                return inst
            kb.emit("DVE", evg, wait=[("sPE", ev_trg)], inc="sDVE")
            ev_pt2 = kb.ctr["sDVE"]
        kb.emit("DVE", lambda h, p=p: h.tensor_tensor(
            out=prodb[:], in0=srb[p][:], in1=gtok[p][:], op=ALU.mult),
            wait=[ts2_, ("sDVE", ev_pt2)], inc="sDVE")
        ev_gt_free[p] = kb.ctr["sDVE"]
        w = [tx, ("sDVE", kb.ctr["sDVE"])]
        if ev_out_done[p] is not None:
            w.append(ev_out_done[p])
        kb.emit("DVE", lambda h, p=p: h.tensor_tensor(
            out=outb[p][:], in0=xrb[p][:], in1=prodb[:], op=ALU.add),
            wait=w, inc="sDVE")
        ev_oc = kb.ctr["sDVE"]
        t_out = dst_.issue(lambda h, tt=tt, p=p: h.dma_start(
            out=out[128 * tt:128 * tt + 128, :], in_=outb[p][:]),
            wait=[("sDVE", ev_oc)], slot=p)
        ev_out_done[p] = t_out
    kb.emit("SP", lambda h: h.nop(), wait=dst_.final_tokens())
    A.pop()

    def replay(blk, streams):
        for ename, setter in (("PE", blk.tensor), ("DVE", blk.vector), ("ACT", blk.scalar),
                              ("SP", blk.sync), ("POOL", blk.gpsimd)):
            ops = streams[ename]
            setter(lambda h, ops=ops: [op() for op in ops])

    with nc.Block("pro") as blk:
        replay(blk, common1)

    engs = [kb.h[e] for e in ENGS]
    idx = nc.partition_id([ETYPE[e] for e in ENGS])
    for c in nc.Switch(engines=engs, index=[idx] * len(engs), n=n_cores):
        for ename in ENGS:
            for op in arms[c][ename]:
                op()

    with nc.Block("epi") as blk:
        replay(blk, common2)

    st.close()
    return nc


def _host_inputs(S, D, F, n_cores, q, Wq, Wk, Wv, Wo, w1, b1, w2, b2):
    NCH = 16
    CS = S // NCH
    x = np.ascontiguousarray(np.asarray(q, dtype=np.float32).reshape(S, D))
    Wq = np.asarray(Wq, np.float32)
    Wk = np.asarray(Wk, np.float32)
    Wv = np.ascontiguousarray(np.asarray(Wv, np.float32))
    Wo = np.ascontiguousarray(np.asarray(Wo, np.float32))
    w1t = np.ascontiguousarray(np.asarray(w1, np.float32).T)
    w2t = np.ascontiguousarray(np.asarray(w2, np.float32).T)
    b1 = np.asarray(b1, np.float32).reshape(1, F)
    b2 = np.asarray(b2, np.float32).reshape(1, D)

    HD = D // 2
    orig = np.empty(D, np.int64)
    pair = np.arange(HD)
    m = pair // 128
    i = pair % 128
    orig[m * 256 + i] = 2 * pair
    orig[m * 256 + 128 + i] = 2 * pair + 1
    Wq_p = np.ascontiguousarray(Wq[:, orig])
    Wk_p = np.ascontiguousarray(Wk[:, orig])

    inv_freq = 1.0 / (10000.0 ** (np.arange(0, D, 2, dtype=np.float64) / D))
    ident = np.eye(128, dtype=np.float32)
    R = CS // 128
    masks = np.zeros((128, R * CS), np.float32)
    for r in range(R):
        kvtok = r * 128 + np.arange(128)[:, None]
        qtok = np.arange(CS)[None, :]
        masks[:, r * CS:(r + 1) * CS] = (kvtok <= qtok).astype(np.float32)

    RS1 = D // n_cores
    RS2 = F // n_cores
    in_maps = []
    for c in range(n_cores):
        rows = np.r_[c * CS:(c + 1) * CS, (NCH - 1 - c) * CS:(NCH - c) * CS]
        ang = rows[:, None].astype(np.float64) * inv_freq[None, :]
        in_maps.append({
            "xloc": np.ascontiguousarray(x[rows]),
            "cosl": np.ascontiguousarray(np.cos(ang).T.astype(np.float32)),
            "sinl": np.ascontiguousarray(np.sin(ang).T.astype(np.float32)),
            "wq": Wq_p, "wk": Wk_p, "wv": Wv, "wo": Wo,
            "w1s": np.ascontiguousarray(w1t[c * RS1:(c + 1) * RS1]),
            "w2s": np.ascontiguousarray(w2t[c * RS2:(c + 1) * RS2]),
            "b1r": b1, "b2r": b2,
            "identf": ident, "masksf": masks,
        })
    return in_maps


def _assemble(S, D, n_cores, results):
    NCH = 16
    CS = S // NCH
    out = np.empty((S, D), np.float32)
    for c in range(n_cores):
        o = results[c]["out"]
        out[c * CS:(c + 1) * CS] = o[:CS]
        out[(NCH - 1 - c) * CS:(NCH - c) * CS] = o[CS:]
    return out.reshape(1, S, D)


def kernel(L0_attn_q, L0_attn_W_q, L0_attn_W_k, L0_attn_W_v, L0_attn_W_o,
           L0_ffn_w1, L0_ffn_b1, L0_ffn_w2, L0_ffn_b2):
    S, D, F, n_cores = 8192, 2048, 8192, 8
    in_maps = _host_inputs(S, D, F, n_cores, L0_attn_q, L0_attn_W_q, L0_attn_W_k,
                           L0_attn_W_v, L0_attn_W_o, L0_ffn_w1, L0_ffn_b1,
                           L0_ffn_w2, L0_ffn_b2)
    nc = build_program(S, D, F, n_cores)
    res = run_bass_kernel_spmd(nc, in_maps, core_ids=list(range(n_cores)))
    return _assemble(S, D, n_cores, res.results)


# version 24
# speedup vs baseline: 1.0744x; 1.0213x over previous
import math
from contextlib import ExitStack

import numpy as np

import concourse.bass as bass
import concourse.mybir as mybir
from concourse.bass_utils import run_bass_kernel_spmd

F32 = mybir.dt.float32
BF16 = mybir.dt.bfloat16
AF = mybir.ActivationFunctionType
ALU = mybir.AluOpType
AX = mybir.AxisListType

ENGS = ["PE", "DVE", "ACT", "SP", "POOL"]
ETYPE = {
    "PE": mybir.EngineType.PE,
    "DVE": mybir.EngineType.DVE,
    "ACT": mybir.EngineType.Activation,
    "SP": mybir.EngineType.SP,
    "POOL": mybir.EngineType.Pool,
}


class KB:

    def __init__(self, nc, stack):
        self.nc = nc
        self.h = {"PE": nc.tensor, "DVE": nc.vector, "ACT": nc.scalar,
                  "SP": nc.sync, "POOL": nc.gpsimd}
        self.sems = {}
        self.ctr = {}
        self.stack = stack
        self.cur = None
        self.waited = {}

    def sem(self, name):
        if name not in self.sems:
            self.sems[name] = self.stack.enter_context(self.nc.semaphore(name))
        if name not in self.ctr:
            self.ctr[name] = 0
        return self.sems[name]

    def new_streams(self):
        return {e: [] for e in ENGS}

    def emit(self, eng, fn, wait=(), inc=None, inc_n=1):
        if eng in ("DVE", "ACT", "POOL"):
            wait = list(wait) + [("s" + eng, self.ctr["s" + eng])]
        w = []
        for (s, v) in wait:
            if v <= 0:
                continue
            k = (eng, s)
            if self.waited.get(k, -1) < v:
                self.waited[k] = v
                w.append((s, v))
        ret = None
        if inc is not None:
            self.sem(inc)
            self.ctr[inc] += inc_n
            ret = self.ctr[inc]
        handle = self.h[eng]
        sems = self.sems

        def closure(handle=handle, w=tuple(w), fn=fn, inc=inc, inc_n=inc_n):
            for (s, v) in w:
                handle.wait_ge(sems[s], v)
            inst = fn(handle)
            if inc is not None:
                assert inst is not None, "op with inc returned no instruction"
                inst.then_inc(sems[inc], inc_n)

        self.cur[eng].append(closure)
        return ret

    def fence(self):
        return [(s, self.ctr[s]) for s in ("sPE", "sDVE", "sACT")]

    def snap(self):
        return (dict(self.ctr), dict(self.waited))

    def restore(self, s):
        self.ctr = dict(s[0])
        self.waited = dict(s[1])


class DS:

    def __init__(self, kb, name, qeng):
        self.kb = kb
        self.name = name
        self.q = qeng
        self.n = {}

    def issue(self, fn, wait=(), slot=0):
        sem = f"{self.name}_{slot}"
        self.kb.sem(sem)
        self.n[slot] = self.n.get(slot, 0) + 1
        self.kb.emit(self.q, fn, wait=wait, inc=sem, inc_n=16)
        return (sem, 16 * self.n[slot])

    def final_tokens(self):
        return [(f"{self.name}_{k}", 16 * v) for k, v in self.n.items()]

    def state(self):
        return dict(self.n)

    def set_state(self, s):
        self.n = dict(s)


class Arena:

    def __init__(self, ap, words):
        self.ap = ap
        self.words = words
        self.off = 0
        self.marks = []

    def f32(self, n):
        o = self.off
        self.off += n
        assert self.off <= self.words, f"arena overflow: {self.off} > {self.words}"
        return self.ap[:, o:o + n]

    def bf16(self, nel):
        assert nel % 2 == 0
        return self.f32(nel // 2).bitcast(BF16)

    def at_f32(self, o, n):
        assert o + n <= self.words
        return self.ap[:, o:o + n]

    def at_bf16(self, o, nel):
        return self.at_f32(o, nel // 2).bitcast(BF16)

    def push(self):
        self.marks.append(self.off)

    def pop(self):
        self.off = self.marks.pop()


def _r3(ap2d):
    return ap2d.rearrange("(a p) b -> p a b", p=128)


def _s3(sbap, a):
    return sbap.rearrange("p (a b) -> p a b", a=a)


def build_program(S, D, F, n_cores=8, arena_words=53200):
    NCH = 16
    CS = S // NCH
    T = 2 * CS
    NT = T // 128
    DC = D // 128
    FC = F // 128
    HC = DC // 2
    R = CS // 128
    TB = min(512, T)
    NTB = T // TB
    KL = CS if CS < 256 else CS // 2
    NKL = CS // KL
    MAXKT = S // 128
    SCALE = 1.0 / math.sqrt(D)
    RS1 = D // n_cores
    RS2 = F // n_cores
    QF = min(F, 512)
    QF2 = min(D, 512)
    OG = min(256, D)
    FB = min(2048, F)
    OGW = min(512, D)
    NOG = D // OGW
    NBC = (D + 511) // 512

    nc = bass.Bass("TRN2", target_bir_lowering=False, debug=False)

    def din(name, shape):
        return nc.dram_tensor(name, shape, F32, kind="ExternalInput").ap()

    xloc = din("xloc", [T, D])
    cosl = din("cosl", [D // 2, T])
    sinl = din("sinl", [D // 2, T])
    HC_ = D // 256
    OGV_ = min(512, D)
    OGW_ = min(512, D)
    wq = din("wq", [HC_ * 128, 2 * D])
    wk = din("wk", [HC_ * 128, 2 * D])
    wv = din("wv", [(D // OGV_) * 128, (D // 128) * OGV_])
    wo = din("wo", [(D // OGW_) * 128, (D // 128) * OGW_])
    w1s = din("w1s", [RS1, F])
    w2s = din("w2s", [RS2, D])
    b1r = din("b1r", [1, F])
    b2r = din("b2r", [1, D])
    identf = din("identf", [128, 128])
    masksf = din("masksf", [128, R * CS])
    out = nc.dram_tensor("out", [T, D], F32, kind="ExternalOutput").ap()

    kcon = nc.dram_tensor("kcon", [128, (T // KL) * DC * KL], BF16)
    vcon = nc.dram_tensor("vcon", [T, D], BF16)
    kbuf = nc.dram_tensor("kbuf", [n_cores * 128, (T // KL) * DC * KL], BF16, addr_space="Shared")
    vbuf = nc.dram_tensor("vbuf", [n_cores * T, D], BF16, addr_space="Shared")
    w1qc = nc.dram_tensor("w1qc", [RS1, F], BF16)
    w2qc = nc.dram_tensor("w2qc", [RS2, D], BF16)
    w1qb = nc.dram_tensor("w1qb", [D, F], BF16, addr_space="Shared")
    w2qb = nc.dram_tensor("w2qb", [F, D], BF16, addr_space="Shared")
    pcon = nc.dram_tensor("pcon", [1, 2], F32)
    pbuf = nc.dram_tensor("pbuf", [1, 2], F32)
    sspill = nc.dram_tensor("sspill", [T, D], BF16)
    wodram = nc.dram_tensor("wodram", [(D // min(512, D)) * 128, (D // 128) * min(512, D)], BF16)

    st = ExitStack()
    kb = KB(nc, st)
    arena_t = st.enter_context(nc.sbuf_tensor("AR", [128, arena_words], F32))
    A = Arena(arena_t[:], arena_words)

    ident = A.bf16(128)
    b1c = A.f32(FC)
    b2c = A.f32(DC)
    rcp = A.f32(NT)
    svec = A.f32(2)
    nsvec = A.f32(2)
    onesb = A.f32(1).bitcast(BF16)
    onesf = A.f32(1)
    sums_sb = A.f32(CS)
    sumc = A.f32(max(R, 2))
    allr = A.f32(2)
    sc2 = A.f32(2)
    acc = A.f32(2)
    rqt = A.f32(2)
    epsv = A.f32(1)
    persist_end = A.off

    ps = [st.enter_context(nc.psum_tensor(f"ps{i}", [128, 512], F32)) for i in range(8)]

    def P(i, n=512):
        return ps[i][:, 0:n]

    pt6 = ps[6][:].bitcast(BF16)
    pt7 = ps[7][:].bitcast(BF16)

    for e in ENGS:
        kb.sem("s" + e)
    kb.sem("sCC")

    XN_W = DC * T // 2
    xn = A.bf16(DC * T)
    scratch0 = A.off

    common1 = kb.new_streams()
    kb.cur = common1

    dmisc = DS(kb, "dmisc", "POOL")
    dseq = DS(kb, "dseq", "SP")
    dqnt = DS(kb, "dqnt", "POOL")
    dq1 = DS(kb, "dq1", "POOL")
    dvw = DS(kb, "dvw", "POOL")
    dbo = DS(kb, "dbo", "ACT")
    dcs = DS(kb, "dcs", "SP")
    dwp = DS(kb, "dwp", "POOL")
    dkh = DS(kb, "dkh", "SP")
    dvl = DS(kb, "dvl", "ACT")
    dw1 = DS(kb, "dw1", "SP")
    dw2 = DS(kb, "dw2", "SP")
    dsl = DS(kb, "dsl", "ACT")
    dxc = DS(kb, "dxc", "SP")
    dst_ = DS(kb, "dst", "SP")

    t_ident = dmisc.issue(lambda h: h.dma_start(out=ident[:, 0:128], in_=identf), slot="id")
    kb.emit("DVE", lambda h: h.memset(onesb[:, 0:1], 1.0), inc="sDVE")
    kb.emit("DVE", lambda h: h.memset(onesf[:, 0:1], 1.0), inc="sDVE")
    kb.emit("DVE", lambda h: h.memset(epsv[:, 0:1], 1e-5), inc="sDVE")
    ev_const = kb.ctr["sDVE"]

    A.push()
    masks = A.bf16(R * CS)
    qh = A.bf16(DC * T)
    khl = A.bf16(DC * T)
    t_masks = dmisc.issue(lambda h: h.dma_start(out=masks[:], in_=masksf), slot="mk")

    A.push()
    qwb = [A.f32(QF) for _ in range(2)]
    qa = A.f32(QF)
    qb = A.f32(QF)
    q2b = A.bf16(QF)

    A.push()
    WQW_ = (D // 128) * min(512, D) // max(1, ((D // 128) * min(512, D)) // 2048)
    stgw = A.f32(WQW_)
    wcast = [A.bf16(WQW_) for _ in range(2)]
    bbuf = A.f32(min(1024, F))
    xbuf = [A.f32(D) for _ in range(2)]
    xntok = [A.bf16(D) for _ in range(2)]
    bno = [A.f32(6 * NBC) for _ in range(2)]
    mv = [A.f32(2) for _ in range(2)]
    rstd = [A.f32(2) for _ in range(2)]

    BQ = min(1024, F)
    ev_bprev = 0
    for src_row, n, pslot in ((b1r, FC, 4), (b2r, DC, 5)):
        ncol = 128 * n
        for j4 in range(ncol // BQ if ncol >= BQ else 1):
            w_ = min(BQ, ncol)
            tb_ = dseq.issue(lambda h, src_row=src_row, j4=j4, w_=w_: h.dma_start(
                out=bbuf[0:1, 0:w_], in_=src_row[0:1, BQ * j4:BQ * j4 + w_]),
                wait=[("sPE", ev_bprev)], slot="bb")

            def mmb(h, j4=j4, w_=w_, pslot=pslot):
                inst = None
                for j in range(w_ // 128):
                    inst = h.matmul(ps[pslot][:, (BQ // 128) * j4 + j:(BQ // 128) * j4 + j + 1],
                                    bbuf[0:1, 128 * j:128 * j + 128],
                                    onesf[0:1, 0:1], start=True, stop=True)
                return inst
            kb.emit("PE", mmb, wait=[tb_, ("sDVE", ev_const)], inc="sPE")
            ev_bprev = kb.ctr["sPE"]
    ev_bcol_pe = kb.ctr["sPE"]
    kb.emit("DVE", lambda h: h.tensor_copy(out=b1c[:], in_=ps[4][:, 0:FC]),
            wait=[("sPE", ev_bcol_pe)], inc="sDVE")
    kb.emit("DVE", lambda h: h.tensor_copy(out=b2c[:], in_=ps[5][:, 0:DC]),
            wait=[("sDVE", kb.ctr["sDVE"])], inc="sDVE")
    ev_bc = kb.ctr["sDVE"]

    def slice_chunks(rows, cols, qf):
        res = []
        r = 0
        while r < rows:
            rr = min(128, rows - r)
            c = 0
            while c < cols:
                cc = min(qf, cols - c)
                res.append((r, rr, c, cc))
                c += cc
            r += rr
        return res

    ch1 = slice_chunks(RS1, F, QF)
    ch2 = slice_chunks(RS2, D, QF2)

    kb.emit("DVE", lambda h: h.memset(acc[:], 0.0), inc="sDVE")
    ev_accz = kb.ctr["sDVE"]
    st_p1 = {"evqf": [0, 0], "i": 0}
    p1units = []
    for mat, chunks, col in ((w1s, ch1, 0), (w2s, ch2, 1)):
        for (r, rr, c, cc) in chunks:
            def p1unit(mat=mat, r=r, rr=rr, c=c, cc=cc, col=col):
                p = st_p1["i"] % 2
                tq = dqnt.issue(lambda h: h.dma_start(
                    out=qwb[p][0:rr, 0:cc], in_=mat[r:r + rr, c:c + cc]),
                    wait=[("sDVE", st_p1["evqf"][p])], slot=f"p1{p}")
                kb.emit("DVE", lambda h: h.reduce_sum(
                    out=rqt[0:rr, 0:1], in_=qwb[p][0:rr, 0:cc], axis=AX.X,
                    apply_absolute_value=True), wait=[tq, ("sDVE", ev_accz)], inc="sDVE")
                kb.emit("DVE", lambda h: h.tensor_tensor(
                    out=acc[0:rr, col:col + 1], in0=acc[0:rr, col:col + 1],
                    in1=rqt[0:rr, 0:1], op=ALU.add),
                    wait=[("sDVE", kb.ctr["sDVE"])], inc="sDVE")
                st_p1["evqf"][p] = kb.ctr["sDVE"]
                st_p1["i"] += 1
            p1units.append(p1unit)




    NOGWO = D // OGW_
    QWO = max(1, (DC * OGW_) // 2048)
    WQW = (DC * OGW_) // QWO
    st_wo = {"ld": 0, "cast": 0, "i": 0}
    wo_units = []
    for wi in range(NOGWO * QWO):
        ogb, qq = wi // QWO, wi % QWO

        def wo_unit(ogb=ogb, qq=qq):
            i = st_wo["i"]
            pw = i % 2
            tl = dseq.issue(lambda h: h.dma_start(
                out=stgw[:, 0:WQW],
                in_=wo[128 * ogb:128 * ogb + 128, WQW * qq:WQW * qq + WQW]),
                wait=[("sACT", st_wo["cast"])], slot="wop")
            kb.emit("ACT", lambda h: h.copy(out=wcast[pw][:, 0:WQW], in_=stgw[:, 0:WQW]),
                    wait=[tl, (f"dbo_wo{pw}", 16 * (i // 2)) if i >= 2 else ("sACT", 0)],
                    inc="sACT")
            st_wo["cast"] = kb.ctr["sACT"]
            dbo.issue(lambda h: h.dma_start(
                out=wodram.ap()[128 * ogb:128 * ogb + 128, WQW * qq:WQW * qq + WQW],
                in_=wcast[pw][:, 0:WQW]),
                wait=[("sACT", st_wo["cast"])], slot=f"wo{pw}")
            st_wo["i"] += 1
        wo_units.append(wo_unit)

    NWO_U = (D // min(512, D)) * max(1, ((D // 128) * min(512, D)) // 2048)
    F_wopc = [("dbo_wo0", 16 * ((NWO_U + 1) // 2)), ("dbo_wo1", 16 * (NWO_U // 2))]
    A.push()
    OGV = min(512, D)
    NOGV = D // OGV
    VCH = min(2, DC)
    NVF = DC // VCH
    NWAVE = max(1, NOGV // 2)
    WSZ = min(2, NOGV)
    vl = A.at_bf16(khl.offset // 2, NT * D)
    wvb = [A.bf16(DC * OGV) for _ in range(WSZ)]
    stgv = [A.f32(VCH * OGV) for _ in range(2)]
    st_v = {"iv": 0, "cast": {}, "used": [0] * WSZ, "slot_ev": {}, "idx": 0}

    def emit_v_og_loads(wave, j):
        og = WSZ * wave + j
        for hf in range(NVF):
            sp_ = st_v["iv"] % 2
            twv_ = dvw.issue(lambda h, og=og, hf=hf, sp_=sp_: h.dma_start(
                out=stgv[sp_][:],
                in_=wv[128 * og:128 * og + 128,
                       VCH * OGV * hf:VCH * OGV * (hf + 1)]),
                wait=[("sACT", st_v["cast"].get(st_v["iv"] - 2, 0))],
                slot=f"wv{sp_}")
            kb.emit("ACT", lambda h, j=j, hf=hf, sp_=sp_: h.copy(
                out=wvb[j][:, VCH * OGV * hf:VCH * OGV * (hf + 1)], in_=stgv[sp_][:]),
                wait=[twv_, ("sPE", st_v["used"][j])], inc="sACT")
            st_v["cast"][st_v["iv"]] = kb.ctr["sACT"]
            st_v["iv"] += 1
        return ("sACT", kb.ctr["sACT"])

    def emit_v_tile(wave, toks, tt):
        for j in range(WSZ):
            og = WSZ * wave + j
            tquad = toks[j]
            slot = (tt * WSZ + j) % 2
            idx = st_v["idx"]

            def mmv(h, j=j, tt=tt, slot=slot):
                inst = None
                for dc in range(DC):
                    inst = h.matmul(P(slot, OGV),
                                    xn[:, T * dc + 128 * tt:T * dc + 128 * tt + 128],
                                    wvb[j][:, OGV * dc:OGV * dc + OGV],
                                    start=(dc == 0), stop=(dc == DC - 1))
                return inst
            w = [tquad, ("sDVE", ev_xn_t[tt])]
            if idx >= 2:
                w.append(("sDVE", st_v["slot_ev"][idx - 2]))
            kb.emit("PE", mmv, wait=w, inc="sPE")
            ev_mmv = kb.ctr["sPE"]
            kb.emit("DVE", lambda h, og=og, tt=tt, slot=slot: h.tensor_copy(
                out=vl[:, D * tt + OGV * og:D * tt + OGV * og + OGV], in_=P(slot, OGV)),
                wait=[("sPE", ev_mmv)], inc="sDVE")
            st_v["slot_ev"][idx] = kb.ctr["sDVE"]
            st_v["idx"] += 1
        if tt == NT - 1:
            for j in range(WSZ):
                st_v["used"][j] = kb.ctr["sPE"]

    ev_xfree = [0, 0]
    ev_ptfree = 0
    ev_xn_t = []
    v_toks0 = {}
    for tt in range(NT):
        p = tt % 2
        tx = dseq.issue(
            lambda h, tt=tt, p=p: h.dma_start(out=xbuf[p][:], in_=xloc[128 * tt:128 * tt + 128, :]),
            wait=[("sDVE", ev_xfree[p])], slot=f"x{p}")

        def bn(h, p=p):
            inst = None
            for k in range(NBC):
                w = min(512, D - 512 * k)
                inst = h.bn_stats(out=bno[p][:, 6 * k:6 * k + 6],
                                  in_=xbuf[p][:, 512 * k:512 * k + w])
            return inst
        kb.emit("DVE", bn, wait=[tx], inc="sDVE")
        kb.emit("DVE", lambda h, p=p: h.bn_aggr(out=mv[p][:], in_=bno[p][:]),
                wait=[("sDVE", kb.ctr["sDVE"])], inc="sDVE")
        ev_mv = kb.ctr["sDVE"]
        kb.emit("ACT", lambda h, p=p: h.activation(out=rstd[p][:, 0:1], in_=mv[p][:, 1:2],
                                                   func=AF.Ln, bias=epsv[:, 0:1]),
                wait=[("sDVE", ev_mv), ("sDVE", ev_const)], inc="sACT")
        kb.emit("ACT", lambda h, p=p: h.activation(out=rstd[p][:, 0:1], in_=rstd[p][:, 0:1],
                                                   func=AF.Exp, scale=-0.5),
                wait=[("sACT", kb.ctr["sACT"])], inc="sACT")
        ev_rstd = kb.ctr["sACT"]
        kb.emit("DVE", lambda h, p=p: h.tensor_scalar(out=xntok[p][:], in0=xbuf[p][:],
                                                      scalar1=mv[p][:, 0:1], scalar2=rstd[p][:, 0:1],
                                                      op0=ALU.subtract, op1=ALU.mult),
                wait=[("sACT", ev_rstd)], inc="sDVE")
        ev_xfree[p] = kb.ctr["sDVE"]
        for half in range((DC + 7) // 8):
            pt = pt6 if half % 2 == 0 else pt7
            lo, hi = 8 * half, min(DC, 8 * half + 8)

            def tr(h, p=p, pt=pt, lo=lo, hi=hi):
                inst = None
                for dc in range(lo, hi):
                    inst = h.transpose(pt[:, 128 * (dc - lo):128 * (dc - lo) + 128],
                                       xntok[p][:, 128 * dc:128 * dc + 128], ident[:, 0:128])
                return inst
            kb.emit("PE", tr, wait=[("sDVE", ev_xfree[p]), t_ident, ("sDVE", ev_ptfree)],
                    inc="sPE")
            ev_tr = kb.ctr["sPE"]

            def ev(h, pt=pt, lo=lo, hi=hi, tt=tt):
                inst = None
                for dc in range(lo, hi):
                    inst = h.tensor_copy(out=xn[:, T * dc + 128 * tt:T * dc + 128 * tt + 128],
                                         in_=pt[:, 128 * (dc - lo):128 * (dc - lo) + 128])
                return inst
            kb.emit("DVE", ev, wait=[("sPE", ev_tr)], inc="sDVE")
            ev_ptfree = kb.ctr["sDVE"]
        ev_xn_t.append(kb.ctr["sDVE"])
        for _ in range(2):
            if wo_units:
                wo_units.pop(0)()
        if tt < WSZ:
            v_toks0[tt] = emit_v_og_loads(0, tt)
        if tt >= 2:
            emit_v_tile(0, v_toks0, tt - 2)
        for _ in range(2):
            if p1units:
                p1units.pop(0)()
    while wo_units:
        wo_units.pop(0)()
    for tt in range(max(0, NT - 2), NT):
        emit_v_tile(0, v_toks0, tt)
    ev_xn = kb.ctr["sDVE"]
    for wv_ in range(1, NWAVE):
        tqw = [emit_v_og_loads(wv_, j) for j in range(WSZ)]
        for tt in range(NT):
            emit_v_tile(wv_, tqw, tt)
            for _ in range(2):
                if p1units:
                    p1units.pop(0)()

    while p1units:
        p1units.pop(0)()
    ev_v = kb.ctr["sDVE"]
    t_vcon = dbo.issue(lambda h: h.dma_start(
        out=_r3(vcon.ap()), in_=_s3(vl[:], NT)),
        wait=[("sDVE", ev_v)], slot="vc")
    kb.emit("POOL", lambda h: h.collective_compute(
        "AllGather", ALU.bypass, replica_groups=[list(range(n_cores))],
        ins=[vcon.ap().opt()], outs=[vbuf.ap().opt()]), wait=[t_vcon], inc="sCC")
    cc_v = kb.ctr["sCC"]
    ev_acc = kb.ctr["sDVE"]

    def mm_part(h):
        h.matmul(ps[3][0:1, 0:1], acc[:, 0:1], onesf[:, 0:1], start=True, stop=True)
        return h.matmul(ps[3][0:1, 1:2], acc[:, 1:2], onesf[:, 0:1], start=True, stop=True)
    kb.emit("PE", mm_part, wait=[("sDVE", ev_acc)], inc="sPE")
    kb.emit("DVE", lambda h: h.tensor_copy(out=allr[0:1, 0:2], in_=ps[3][0:1, 0:2]),
            wait=[("sPE", kb.ctr["sPE"])], inc="sDVE")
    t_pw = dbo.issue(lambda h: h.dma_start(out=pcon[:, :], in_=allr[0:1, 0:2]),
                     wait=[("sDVE", kb.ctr["sDVE"])], slot="pc")
    kb.emit("POOL", lambda h: h.collective_compute(
        "AllReduce", ALU.add, replica_groups=[list(range(n_cores))],
        ins=[pcon.ap().opt()], outs=[pbuf.ap().opt()]),
        wait=[t_pw], inc="sCC")
    cc_scale = kb.ctr["sCC"]




    F_v = [("sPE", kb.ctr["sPE"]), ("sDVE", ev_v), t_vcon] + F_wopc
    A.pop()
    A.pop()
    A.push()
    cosb = [A.f32(T) for _ in range(2)]
    sinb = [A.f32(T) for _ in range(2)]
    wqb = [A.bf16(256 * DC) for _ in range(3)]
    wkb = [A.bf16(256 * DC) for _ in range(3)]
    rt = [A.f32(TB) for _ in range(4)]
    HCH = min(8, DC)
    NHF = DC // HCH
    stgp = [A.f32(HCH * 256) for _ in range(2)]
    cast_p = {}
    ip = 0

    ev_rope_par = [0, 0]
    ev_cs_free = [0, 0]
    ev_w_used = {"q": [0, 0, 0], "k": [0, 0, 0]}
    t_kcon = None
    cc_k = None
    for which, wsrc, wbt, dst, wkey in ((1, wk, None, None, "k"), (0, wq, None, None, "q")):
        wbuf = wkb if which == 1 else wqb
        dst = khl if which == 1 else qh
        for m in range(HC):
            pp = m % 2
            wp3 = m % 3
            base = 4 * pp
            tc = dcs.issue(lambda h, m=m, pp=pp: h.dma_start(
                out=cosb[pp][:], in_=cosl[128 * m:128 * m + 128, :]),
                wait=[("sDVE", ev_cs_free[pp])] + (F_v if which == 1 and m < 2 else []),
                slot=f"c{pp}")
            ts_ = dcs.issue(lambda h, m=m, pp=pp: h.dma_start(
                out=sinb[pp][:], in_=sinl[128 * m:128 * m + 128, :]), slot=f"s{pp}")
            for hf in range(NHF):
                sp_ = ip % 2
                tw_ = dseq.issue(lambda h, m=m, hf=hf, wsrc=wsrc, sp_=sp_: h.dma_start(
                    out=stgp[sp_][:],
                    in_=wsrc[128 * m:128 * m + 128,
                             HCH * 256 * hf:HCH * 256 * (hf + 1)]),
                    wait=[("sACT", cast_p.get(ip - 2, 0))] + (F_v if which == 1 and m < 2 else []),
                    slot=f"wp{sp_}")
                kb.emit("ACT", lambda h, wp3=wp3, hf=hf, wbuf=wbuf, sp_=sp_: h.copy(
                    out=wbuf[wp3][:, HCH * 256 * hf:HCH * 256 * (hf + 1)], in_=stgp[sp_][:]),
                    wait=[tw_, ("sPE", ev_w_used[wkey][wp3])], inc="sACT")
                cast_p[ip] = kb.ctr["sACT"]
                ip += 1
            tw = ("sACT", kb.ctr["sACT"])

            def mmproj(h, wbuf=wbuf, wp3=wp3, base=base):
                inst = None
                for dc in range(DC):
                    for oc in range(2):
                        for tb in range(NTB):
                            slot = base + oc * NTB + tb
                            inst = h.matmul(
                                P(slot, TB),
                                wbuf[wp3][:, 256 * dc + 128 * oc:256 * dc + 128 * oc + 128],
                                xn[:, T * dc + TB * tb:T * dc + TB * tb + TB],
                                start=(dc == 0), stop=(dc == DC - 1))
                return inst
            kb.emit("PE", mmproj, wait=[tw, ("sDVE", ev_xn), ("sDVE", ev_rope_par[pp])],
                    inc="sPE")
            ev_mm = kb.ctr["sPE"]
            ev_w_used[wkey][wp3] = ev_mm
            for tb in range(NTB):
                sA = base + tb
                sB = base + NTB + tb
                o1 = T * (2 * m) + TB * tb
                o2 = T * (2 * m + 1) + TB * tb

                def rope1(h, sA=sA, sB=sB, pp=pp, tb=tb):
                    h.tensor_tensor(out=rt[0][:], in0=P(sA, TB),
                                    in1=cosb[pp][:, TB * tb:TB * tb + TB], op=ALU.mult)
                    h.tensor_tensor(out=rt[1][:], in0=P(sB, TB),
                                    in1=sinb[pp][:, TB * tb:TB * tb + TB], op=ALU.mult)
                    h.tensor_tensor(out=rt[2][:], in0=P(sA, TB),
                                    in1=sinb[pp][:, TB * tb:TB * tb + TB], op=ALU.mult)
                    return h.tensor_tensor(out=rt[3][:], in0=P(sB, TB),
                                           in1=cosb[pp][:, TB * tb:TB * tb + TB], op=ALU.mult)
                kb.emit("DVE", rope1, wait=[("sPE", ev_mm), tc, ts_], inc="sDVE")
                ev_rt = kb.ctr["sDVE"]

                def rope2(h, o1=o1, o2=o2, dst=dst):
                    h.tensor_tensor(out=dst[:, o1:o1 + TB], in0=rt[0][:], in1=rt[1][:],
                                    op=ALU.subtract)
                    return h.tensor_tensor(out=dst[:, o2:o2 + TB], in0=rt[2][:], in1=rt[3][:],
                                           op=ALU.add)
                kb.emit("DVE", rope2, wait=[("sDVE", ev_rt)], inc="sDVE")
            ev_rope_par[pp] = kb.ctr["sDVE"]
            ev_cs_free[pp] = kb.ctr["sDVE"]
        if which == 1:
            ev_k = kb.ctr["sDVE"]
            t_kcons = []
            for tb_ in range(T // KL):
                t_kcons.append(dbo.issue(lambda h, tb_=tb_: h.dma_start(
                    out=_s3(kcon.ap()[:, DC * KL * tb_:DC * KL * (tb_ + 1)], DC),
                    in_=khl[:].rearrange("p (dc t) -> p dc t", dc=DC)[:, :, KL * tb_:KL * (tb_ + 1)]),
                    wait=[("sDVE", ev_k)], slot=f"kc{tb_ % 2}"))
            t_kcon = t_kcons[-1]
            kb.emit("POOL", lambda h: h.collective_compute(
                "AllGather", ALU.bypass, replica_groups=[list(range(n_cores))],
                ins=[kcon.ap().opt()], outs=[kbuf.ap().opt()]),
                wait=t_kcons, inc="sCC")
            cc_k = kb.ctr["sCC"]
    t_pr = dbo.issue(lambda h: h.dma_start(out=allr[0:1, 0:2], in_=pbuf[:, :]),
                      wait=[("sCC", cc_scale)], slot="pr")
    kb.emit("DVE", lambda h: h.tensor_scalar(out=sc2[0:1, 0:2], in0=allr[0:1, 0:2],
                                             scalar1=1.0 / (D * F), scalar2=1e-8,
                                             op0=ALU.mult, op1=ALU.max),
            wait=[t_pr], inc="sDVE")
    ev_sc2 = kb.ctr["sDVE"]

    ones_row = sums_sb
    kb.emit("DVE", lambda h: h.memset(ones_row[0:1, 0:128], 1.0), inc="sDVE")
    ev_or = kb.ctr["sDVE"]

    def mm_svec2(h):
        return h.matmul(ps[3][:, 0:2], ones_row[0:1, 0:128], sc2[0:1, 0:2],
                        start=True, stop=True)
    kb.emit("PE", mm_svec2, wait=[("sDVE", ev_sc2), ("sDVE", ev_or)], inc="sPE")
    kb.emit("DVE", lambda h: h.tensor_copy(out=svec[:], in_=ps[3][:, 0:2]),
            wait=[("sPE", kb.ctr["sPE"])], inc="sDVE")
    kb.emit("DVE", lambda h: h.tensor_scalar(out=nsvec[:], in0=svec[:], scalar1=-1.0,
                                             scalar2=0.0, op0=ALU.mult, op1=ALU.add),
            wait=[("sDVE", kb.ctr["sDVE"])], inc="sDVE")
    ev_svec = kb.ctr["sDVE"]

    ev_qk = kb.ctr["sDVE"]

    def emit_quant_tail():
        res = {}
        st_q = {"evqf": [0, 0], "tq2": None, "i": 0}
        units = []
        for mat, qc, chunks, col in ((w1s, w1qc, ch1, 0), (w2s, w2qc, ch2, 1)):
            for (r, rr, c, cc) in chunks:
                def unit(mat=mat, qc=qc, r=r, rr=rr, c=c, cc=cc, col=col):
                    p = st_q["i"] % 2
                    tq = dqnt.issue(lambda h: h.dma_start(
                        out=qwb[p][0:rr, 0:cc], in_=mat[r:r + rr, c:c + cc]),
                        wait=[("sACT", st_q["evqf"][p]), ("sDVE", ev_acc)], slot=p)
                    kb.emit("ACT", lambda h: h.activation(
                        out=qa[0:rr, 0:cc], in_=qwb[p][0:rr, 0:cc], func=AF.Sign,
                        bias=nsvec[0:rr, col:col + 1], scale=2.0),
                        wait=[tq, ("sDVE", ev_svec)], inc="sACT")
                    kb.emit("ACT", lambda h: h.activation(
                        out=qb[0:rr, 0:cc], in_=qwb[p][0:rr, 0:cc], func=AF.Sign,
                        bias=svec[0:rr, col:col + 1], scale=2.0),
                        wait=[("sACT", kb.ctr["sACT"])], inc="sACT")
                    st_q["evqf"][p] = ev_sign = kb.ctr["sACT"]
                    kb.emit("DVE", lambda h: h.tensor_tensor(
                        out=q2b[0:rr, 0:cc], in0=qa[0:rr, 0:cc], in1=qb[0:rr, 0:cc],
                        op=ALU.add),
                        wait=[("sACT", ev_sign),
                              (st_q["tq2"] if st_q["tq2"] else ("sDVE", 0))],
                        inc="sDVE")
                    ev_q2 = kb.ctr["sDVE"]
                    st_q["tq2"] = dbo.issue(lambda h: h.dma_start(
                        out=qc[r:r + rr, c:c + cc], in_=q2b[0:rr, 0:cc]),
                        wait=[("sDVE", ev_q2)], slot="q2")
                    st_q["i"] += 1
                units.append(unit)
        res["units"] = units
        res["st_q"] = st_q
        return res


    F_proj = kb.fence() + [t_vcon, t_kcon]
    A.pop()

    aout = A.at_bf16(khl.offset // 2, DC * T)

    A.push()
    probsT = A.bf16(MAXKT * CS)
    khb = [A.bf16(DC * KL) for _ in range(2)]
    VB = 2 if R % 2 == 0 else 1
    vb = [A.bf16(VB * (D // 2)) for _ in range(2)]

    arm_streams = (dkh, dvl, dqnt, dbo)
    snap0 = kb.snap()
    ds0 = {d.name: d.state() for d in arm_streams}
    arms = []
    arm_end = None
    for core in range(n_cores):
        kb.restore(snap0)
        for d in arm_streams:
            d.set_state(ds0[d.name])
        rec = kb.new_streams()
        kb.cur = rec
        qt_res = emit_quant_tail()
        qunits = list(qt_res["units"])
        drip_ct = 0
        prev_exp_last = 0
        prev_evac = 0

        for ci, chunk in enumerate((core, NCH - 1 - core)):
            qoff = ci * CS
            Pg = chunk + 1
            ntile = Pg * R
            exp_ct = {}
            kh_used = {}
            act_chunk0 = prev_exp_last
            dve_evac0 = prev_evac
            pe_chunk0 = kb.ctr["sPE"]
            dve_chunk0 = kb.ctr["sDVE"]
            pend_sums = []
            it = 0
            for g in range(Pg):
                rg = min(g, NCH - 1 - g)
                lg = 0 if g == rg else CS
                for li in range(NKL):
                    ld = g * NKL + li
                    pp = ld % 2
                    w = [("sCC", cc_k)] + (F_proj if ld < 2 and ci == 0 else [])
                    if ld >= 2:
                        w.append(("sPE", kh_used[ld - 2]))
                    elif ci == 1:
                        w.append(("sPE", pe_chunk0))
                    tb_ = (lg + KL * li) // KL
                    tk = dkh.issue(lambda h, rg=rg, tb_=tb_, pp=pp: h.dma_start(
                        out=khb[pp][:],
                        in_=kbuf[128 * rg:128 * rg + 128,
                                 DC * KL * tb_:DC * KL * (tb_ + 1)]),
                        wait=w, slot=pp)
                    for k in range(KL // 128):
                        sl = (0, 1, 3, 4, 5, 6)[it % 6]

                        def mmsc(h, pp=pp, k=k, sl=sl, qoff=qoff):
                            inst = None
                            for dc in range(DC):
                                inst = h.matmul(
                                    P(sl, CS),
                                    khb[pp][:, KL * dc + 128 * k:KL * dc + 128 * k + 128],
                                    qh[:, T * dc + qoff:T * dc + qoff + CS],
                                    start=(dc == 0), stop=(dc == DC - 1))
                            return inst
                        w = [tk]
                        w.append(("sACT", exp_ct[it - 6] if it >= 6 else act_chunk0))
                        if it < 6:
                            w.append(("sDVE", dve_evac0))
                        kb.emit("PE", mmsc, wait=w, inc="sPE")
                        ev_sc_mm = kb.ctr["sPE"]
                        kh_used[ld] = ev_sc_mm
                        kb.emit("ACT", lambda h, it=it, sl=sl: h.activation(
                            out=probsT[:, CS * it:CS * it + CS], in_=P(sl, CS),
                            func=AF.Exp, scale=SCALE),
                            wait=[("sPE", ev_sc_mm)], inc="sACT")
                        exp_ct[it] = kb.ctr["sACT"]
                        sum_wait = [("sACT", exp_ct[it])]
                        if g == chunk:
                            kt = li * (KL // 128) + k
                            kb.emit("DVE", lambda h, it=it, kt=kt: h.tensor_tensor(
                                out=probsT[:, CS * it:CS * it + CS],
                                in0=probsT[:, CS * it:CS * it + CS],
                                in1=masks[:, CS * kt:CS * kt + CS], op=ALU.mult),
                                wait=[("sACT", exp_ct[it]), t_masks], inc="sDVE")
                            sum_wait = [("sDVE", kb.ctr["sDVE"])]
                        pend_sums.append((it, sum_wait))
                        if qunits and drip_ct >= 8 and (drip_ct - 8) % 4 == 0:
                            qunits.pop(0)()
                        drip_ct += 1
                        if len(pend_sums) >= 2:
                            jt, jw = pend_sums.pop(0)
                            kb.emit("PE", lambda h, jt=jt, ntile=ntile: h.matmul(
                                ps[2][0:1, 0:CS], onesb[:, 0:1],
                                probsT[:, CS * jt:CS * jt + CS],
                                start=(jt == 0), stop=(jt == ntile - 1)),
                                wait=jw + [("sDVE", ev_const), ("sDVE", dve_chunk0)],
                                inc="sPE")
                        it += 1
            while pend_sums:
                jt, jw = pend_sums.pop(0)
                kb.emit("PE", lambda h, jt=jt, ntile=ntile: h.matmul(
                    ps[2][0:1, 0:CS], onesb[:, 0:1], probsT[:, CS * jt:CS * jt + CS],
                    start=(jt == 0), stop=(jt == ntile - 1)),
                    wait=jw + [("sDVE", ev_const), ("sDVE", dve_chunk0)], inc="sPE")
            ev_sums_mm = kb.ctr["sPE"]
            prev_exp_last = exp_ct[ntile - 1]
            kb.emit("DVE", lambda h: h.tensor_copy(out=sums_sb[0:1, 0:CS], in_=ps[2][0:1, 0:CS]),
                    wait=[("sPE", ev_sums_mm)], inc="sDVE")
            ev_sums = kb.ctr["sDVE"]

            def mm_sumt(h):
                inst = None
                for sx in range(R):
                    inst = h.matmul(ps[7][:, sx:sx + 1], sums_sb[0:1, 128 * sx:128 * sx + 128],
                                    onesf[0:1, 0:1], start=True, stop=True)
                return inst
            kb.emit("PE", mm_sumt, wait=[("sDVE", ev_sums)], inc="sPE")
            kb.emit("DVE", lambda h: h.tensor_copy(out=sumc[:, 0:R], in_=ps[7][:, 0:R]),
                    wait=[("sPE", kb.ctr["sPE"])], inc="sDVE")
            kb.emit("DVE", lambda h, ci=ci: h.reciprocal(out=rcp[:, R * ci:R * ci + R],
                                                         in_=sumc[:, 0:R]),
                    wait=[("sDVE", kb.ctr["sDVE"])], inc="sDVE")
            ev_rcp = kb.ctr["sDVE"]

            nsl = max(1, DC // 2)
            for half in range(2 if DC >= 2 else 1):
                v_used = {}
                pe_half0 = kb.ctr["sPE"]
                act_half0 = prev_exp_last
                dve_half0 = prev_evac
                it = 0
                for g in range(Pg):
                    rg = min(g, NCH - 1 - g)
                    lg = 0 if g == rg else CS
                    for k in range(R):
                        pp = (it // VB) % 2
                        if it % VB == 0:
                            row0 = T * rg + lg + 128 * k
                            w = [("sCC", cc_v)]
                            if it >= 2 * VB:
                                w.append(("sPE", v_used[it - VB - 1]))
                            else:
                                w.append(("sPE", pe_half0))
                            tv = dvl.issue(lambda h, row0=row0, half=half, pp=pp, g=g: h.dma_start(
                                out=_s3(vb[pp][:], VB),
                                in_=_r3(vbuf[row0:row0 + 128 * VB,
                                              (D // 2) * half:(D // 2) * half + D // 2])),
                                wait=w, slot=pp)

                        def mmpv(h, pp=pp, it=it, nsl=nsl, ntile=ntile):
                            inst = None
                            voff = (it % VB) * (D // 2)
                            for d8 in range(nsl):
                                inst = h.matmul(
                                    P(d8, CS), vb[pp][:, voff + 128 * d8:voff + 128 * d8 + 128],
                                    probsT[:, CS * it:CS * it + CS],
                                    start=(it == 0), stop=(it == ntile - 1))
                            return inst
                        w = [tv]
                        if it == 0:
                            w += [("sACT", act_half0), ("sPE", ev_sums_mm),
                                  ("sDVE", max(ev_rcp, dve_half0))]
                        kb.emit("PE", mmpv, wait=w, inc="sPE")
                        v_used[it] = kb.ctr["sPE"]
                        if qunits and it % 2 == 0:
                            qunits.pop(0)()
                        it += 1
                ev_pv = kb.ctr["sPE"]

                def evpv(h, half=half, qoff=qoff, nsl=nsl):
                    inst = None
                    for d8 in range(nsl):
                        dc = half * nsl + d8
                        inst = h.tensor_copy(out=aout[:, T * dc + qoff:T * dc + qoff + CS],
                                             in_=P(d8, CS))
                    return inst
                kb.emit("DVE", evpv, wait=[("sPE", ev_pv)] + (F_proj if ci == 0 and half == 0 else []),
                        inc="sDVE")
                prev_evac = kb.ctr["sDVE"]
        while qunits:
            qunits.pop(0)()
        arms.append(rec)
        if arm_end is None:
            arm_end = kb.snap()
            ds_end = {d.name: d.state() for d in arm_streams}
        else:
            assert kb.snap()[0] == arm_end[0], "arm semaphore counts diverge"

    kb.restore(arm_end)
    for d in arm_streams:
        d.set_state(ds_end[d.name])
    common2 = kb.new_streams()
    kb.cur = common2
    F_attn = kb.fence() + [qt_res["st_q"]["tq2"]]
    A.pop()
    A.pop()
    kb.emit("POOL", lambda h: h.collective_compute(
        "AllGather", ALU.bypass, replica_groups=[list(range(n_cores))],
        ins=[w1qc.ap().opt()], outs=[w1qb.ap().opt()]),
        wait=[qt_res["st_q"]["tq2"]], inc="sCC")
    cc_w1 = kb.ctr["sCC"]
    kb.emit("POOL", lambda h: h.collective_compute(
        "AllGather", ALU.bypass, replica_groups=[list(range(n_cores))],
        ins=[w2qc.ap().opt()], outs=[w2qb.ap().opt()]), wait=[], inc="sCC")
    cc_w2 = kb.ctr["sCC"]

    A.push()
    stileF = A.bf16(NT * D)
    wob = [A.bf16(DC * OGW) for _ in range(2)]
    ev_wo_used = [0, 0]
    sslot_ev = {}
    wo_done = dbo.final_tokens()
    for og in range(NOG):
        p = og % 2
        two = dseq.issue(lambda h, og=og, p=p: h.dma_start(
            out=wob[p][:], in_=wodram.ap()[128 * og:128 * og + 128, :]),
            wait=[("sPE", ev_wo_used[p])] + (F_attn if og < 2 else []) + wo_done,
            slot=f"wo{p}")
        for tt in range(NT):
            slot = tt % 2
            idx = og * NT + tt

            def mmo(h, p=p, tt=tt, slot=slot):
                inst = None
                for dc in range(DC):
                    inst = h.matmul(P(slot, OGW),
                                    aout[:, T * dc + 128 * tt:T * dc + 128 * tt + 128],
                                    wob[p][:, OGW * dc:OGW * dc + OGW],
                                    start=(dc == 0), stop=(dc == DC - 1))
                return inst
            w = [two] + F_attn
            if idx >= 2:
                w.append(("sDVE", sslot_ev[idx - 2]))
            kb.emit("PE", mmo, wait=w, inc="sPE")
            ev_mmo = kb.ctr["sPE"]
            kb.emit("DVE", lambda h, og=og, tt=tt, slot=slot: h.tensor_scalar(
                out=stileF[:, D * tt + OGW * og:D * tt + OGW * og + OGW], in0=P(slot, OGW),
                scalar1=rcp[:, tt:tt + 1], scalar2=0.0, op0=ALU.mult, op1=ALU.add),
                wait=[("sPE", ev_mmo)], inc="sDVE")
            sslot_ev[idx] = kb.ctr["sDVE"]
        ev_wo_used[p] = kb.ctr["sPE"]
    ev_s = kb.ctr["sDVE"]
    t_sspill = []
    for tt in range(NT):
        t_sspill.append(dbo.issue(lambda h, tt=tt: h.dma_start(
            out=sspill[128 * tt:128 * tt + 128, :],
            in_=stileF[:, D * tt:D * tt + D]),
            wait=[("sDVE", ev_s)], slot=f"sp{tt}"))
    F_wo = kb.fence() + t_sspill
    A.pop()
    A.pop()

    A.push()
    hsb = A.bf16(FC * T)
    w1b = [A.bf16(DC * 512) for _ in range(2)]
    NW1 = max(1, F // 512)
    W1W = min(512, F)
    ev_w1_used = [0, 0]
    gelu_ct = {}
    for fg in range(NW1):
        p = fg % 2
        tw1 = dw1.issue(lambda h, fg=fg, p=p: h.dma_start(
            out=_s3(w1b[p][:, 0:DC * W1W], DC), in_=_r3(w1qb[:, W1W * fg:W1W * fg + W1W])),
            wait=[("sPE", ev_w1_used[p]), ("sCC", cc_w1)] + (F_wo if fg < 2 else []), slot=p)
        NCH1 = W1W // 128
        for half in range(max(1, NCH1 // 2)):
            fcs = list(range(2 * half, min(NCH1, 2 * half + 2)))
            hidx = fg * max(1, NCH1 // 2) + half
            sbase = 4 * (hidx % 2)
            slots = {}
            for j, fc4 in enumerate(fcs):
                for tb in range(NTB):
                    slots[(fc4, tb)] = sbase + j * NTB + tb

            def mmf1(h, p=p, fcs=fcs, slots=slots):
                inst = None
                for dc in range(DC):
                    for fc4 in fcs:
                        for tb in range(NTB):
                            inst = h.matmul(
                                P(slots[(fc4, tb)], TB),
                                w1b[p][:, W1W * dc + 128 * fc4:W1W * dc + 128 * fc4 + 128],
                                xn[:, T * dc + TB * tb:T * dc + TB * tb + TB],
                                start=(dc == 0), stop=(dc == DC - 1))
                return inst
            w = [tw1]
            if hidx >= 2:
                w.append(("sACT", gelu_ct[hidx - 2]))
            else:
                w += F_wo
            kb.emit("PE", mmf1, wait=w, inc="sPE")
            ev_mmf1 = kb.ctr["sPE"]
            for fc4 in fcs:
                fc = NCH1 * fg + fc4
                for tb in range(NTB):
                    kb.emit("ACT", lambda h, fc=fc, tb=tb, slot=slots[(fc4, tb)]: h.activation(
                        out=hsb[:, T * fc + TB * tb:T * fc + TB * tb + TB], in_=P(slot, TB),
                        func=AF.Gelu, bias=b1c[:, fc:fc + 1], scale=0.5),
                        wait=[("sPE", ev_mmf1), ("sDVE", ev_bc)], inc="sACT")
            gelu_ct[hidx] = kb.ctr["sACT"]
        ev_w1_used[p] = kb.ctr["sPE"]
    ev_h = kb.ctr["sACT"]
    A.pop()
    A.push()
    hsb2 = A.bf16(FC * T)
    assert hsb2.offset == hsb.offset
    w2b = [A.bf16((FB // 128) * OG) for _ in range(2)]
    gsb = A.at_bf16(xn.offset // 2, DC * T)

    NOG2 = D // OG
    NFB = max(1, F // FB)
    NOC = OG // 128
    F_ffn1 = [("sPE", kb.ctr["sPE"]), ("sACT", ev_h)]
    ev_w2_used = [0, 0]
    ev_silu_par = [0, 0]
    for og in range(NOG2):
        gpar = og % 2
        sbase = 4 * gpar
        for fq in range(NFB):
            p = (og * NFB + fq) % 2
            tw2 = dw2.issue(lambda h, og=og, fq=fq, p=p: h.dma_start(
                out=_s3(w2b[p][:], FB // 128),
                in_=_r3(w2qb[FB * fq:FB * fq + FB, OG * og:OG * og + OG])),
                wait=[("sPE", ev_w2_used[p]), ("sCC", cc_w2)]
                + (F_ffn1 if og * NFB + fq < 2 else []), slot=p)
            def mmf2(h, p=p, fq=fq, sbase=sbase):
                inst = None
                for fc in range(FB // 128):
                    for oc in range(NOC):
                        for tb in range(NTB):
                            inst = h.matmul(
                                P(sbase + oc * NTB + tb, TB),
                                w2b[p][:, OG * fc + 128 * oc:OG * fc + 128 * oc + 128],
                                hsb[:, T * ((FB // 128) * fq + fc) + TB * tb:
                                    T * ((FB // 128) * fq + fc) + TB * tb + TB],
                                start=(fq == 0 and fc == 0),
                                stop=(fq == NFB - 1 and fc == FB // 128 - 1))
                return inst
            w = [tw2, ("sACT", ev_h)]
            if fq == 0:
                w.append(("sACT", ev_silu_par[gpar]))
            kb.emit("PE", mmf2, wait=w, inc="sPE")
            ev_w2_used[p] = kb.ctr["sPE"]
        ev_mmf2 = kb.ctr["sPE"]
        for oc in range(NOC):
            for tb in range(NTB):
                slot = sbase + oc * NTB + tb
                dc = (OG * og) // 128 + oc
                kb.emit("ACT", lambda h, dc=dc, tb=tb, slot=slot: h.activation(
                    out=gsb[:, T * dc + TB * tb:T * dc + TB * tb + TB], in_=P(slot, TB),
                    func=AF.Silu, bias=b2c[:, dc:dc + 1], scale=0.5),
                    wait=[("sPE", ev_mmf2)], inc="sACT")
        ev_silu_par[gpar] = kb.ctr["sACT"]
    ev_g = kb.ctr["sACT"]
    A.pop()

    A.push()
    gtok = [A.bf16(D) for _ in range(2)]
    srb = [A.bf16(D) for _ in range(2)]
    xrb = [A.f32(D) for _ in range(2)]
    prodb = A.f32(D)
    outb = [A.f32(D) for _ in range(2)]
    F_ffn = kb.fence()
    ev_gt_free = [0, 0]
    ev_out_done = [None, None]
    ev_pt2 = 0
    t_out = None
    for tt in range(NT):
        p = tt % 2
        tx = dxc.issue(lambda h, tt=tt, p=p: h.dma_start(
            out=xrb[p][:], in_=xloc[128 * tt:128 * tt + 128, :]),
            wait=([("sDVE", ev_gt_free[p])] if tt >= 2 else F_ffn), slot=p)
        ts2_ = dsl.issue(lambda h, tt=tt, p=p: h.dma_start(
            out=srb[p][:], in_=sspill[128 * tt:128 * tt + 128, :]),
            wait=([("sDVE", ev_gt_free[p])] if tt >= 2 else F_ffn) + [t_sspill[tt]], slot=p)
        for half in range((DC + 7) // 8):
            pt = pt6 if half % 2 == 0 else pt7
            lo, hi = 8 * half, min(DC, 8 * half + 8)

            def trg(h, tt=tt, pt=pt, lo=lo, hi=hi):
                inst = None
                for dc in range(lo, hi):
                    inst = h.transpose(pt[:, 128 * (dc - lo):128 * (dc - lo) + 128],
                                       gsb[:, T * dc + 128 * tt:T * dc + 128 * tt + 128],
                                       ident[:, 0:128])
                return inst
            kb.emit("PE", trg, wait=[("sACT", ev_g), ("sDVE", ev_pt2),
                                     ("sDVE", ev_gt_free[p])] + (F_ffn if tt == 0 else []),
                    inc="sPE")
            ev_trg = kb.ctr["sPE"]

            def evg(h, p=p, pt=pt, lo=lo, hi=hi):
                inst = None
                for dc in range(lo, hi):
                    inst = h.tensor_copy(out=gtok[p][:, 128 * dc:128 * dc + 128],
                                         in_=pt[:, 128 * (dc - lo):128 * (dc - lo) + 128])
                return inst
            kb.emit("DVE", evg, wait=[("sPE", ev_trg)], inc="sDVE")
            ev_pt2 = kb.ctr["sDVE"]
        kb.emit("DVE", lambda h, p=p: h.tensor_tensor(
            out=prodb[:], in0=srb[p][:], in1=gtok[p][:], op=ALU.mult),
            wait=[ts2_, ("sDVE", ev_pt2)], inc="sDVE")
        ev_gt_free[p] = kb.ctr["sDVE"]
        w = [tx, ("sDVE", kb.ctr["sDVE"])]
        if ev_out_done[p] is not None:
            w.append(ev_out_done[p])
        kb.emit("DVE", lambda h, p=p: h.tensor_tensor(
            out=outb[p][:], in0=xrb[p][:], in1=prodb[:], op=ALU.add),
            wait=w, inc="sDVE")
        ev_oc = kb.ctr["sDVE"]
        t_out = dst_.issue(lambda h, tt=tt, p=p: h.dma_start(
            out=out[128 * tt:128 * tt + 128, :], in_=outb[p][:]),
            wait=[("sDVE", ev_oc)], slot=p)
        ev_out_done[p] = t_out
    kb.emit("SP", lambda h: h.nop(), wait=dst_.final_tokens())
    A.pop()

    def replay(blk, streams):
        for ename, setter in (("PE", blk.tensor), ("DVE", blk.vector), ("ACT", blk.scalar),
                              ("SP", blk.sync), ("POOL", blk.gpsimd)):
            ops = streams[ename]
            setter(lambda h, ops=ops: [op() for op in ops])

    with nc.Block("pro") as blk:
        replay(blk, common1)

    engs = [kb.h[e] for e in ENGS]
    idx = nc.partition_id([ETYPE[e] for e in ENGS])
    for c in nc.Switch(engines=engs, index=[idx] * len(engs), n=n_cores):
        for ename in ENGS:
            for op in arms[c][ename]:
                op()

    with nc.Block("epi") as blk:
        replay(blk, common2)

    st.close()
    return nc


def _host_inputs(S, D, F, n_cores, q, Wq, Wk, Wv, Wo, w1, b1, w2, b2):
    NCH = 16
    CS = S // NCH
    x = np.ascontiguousarray(np.asarray(q, dtype=np.float32).reshape(S, D))
    Wq = np.asarray(Wq, np.float32)
    Wk = np.asarray(Wk, np.float32)
    Wv = np.ascontiguousarray(np.asarray(Wv, np.float32))
    Wo = np.ascontiguousarray(np.asarray(Wo, np.float32))
    w1t = np.ascontiguousarray(np.asarray(w1, np.float32).T)
    w2t = np.ascontiguousarray(np.asarray(w2, np.float32).T)
    b1 = np.asarray(b1, np.float32).reshape(1, F)
    b2 = np.asarray(b2, np.float32).reshape(1, D)

    HD = D // 2
    orig = np.empty(D, np.int64)
    pair = np.arange(HD)
    m = pair // 128
    i = pair % 128
    orig[m * 256 + i] = 2 * pair
    orig[m * 256 + 128 + i] = 2 * pair + 1
    Wq_p = Wq[:, orig]
    Wk_p = Wk[:, orig]

    DC_ = D // 128

    def _blk(w, width):
        ng = D // width
        return np.ascontiguousarray(
            w.reshape(DC_, 128, ng, width).transpose(2, 1, 0, 3).reshape(ng * 128, DC_ * width))

    Wq_l = _blk(Wq_p, 256)
    Wk_l = _blk(Wk_p, 256)
    Wv_l = _blk(Wv, min(512, D))
    Wo_l = _blk(Wo, min(512, D))

    inv_freq = 1.0 / (10000.0 ** (np.arange(0, D, 2, dtype=np.float64) / D))
    ident = np.eye(128, dtype=np.float32)
    R = CS // 128
    masks = np.zeros((128, R * CS), np.float32)
    for r in range(R):
        kvtok = r * 128 + np.arange(128)[:, None]
        qtok = np.arange(CS)[None, :]
        masks[:, r * CS:(r + 1) * CS] = (kvtok <= qtok).astype(np.float32)

    RS1 = D // n_cores
    RS2 = F // n_cores
    in_maps = []
    for c in range(n_cores):
        rows = np.r_[c * CS:(c + 1) * CS, (NCH - 1 - c) * CS:(NCH - c) * CS]
        ang = rows[:, None].astype(np.float64) * inv_freq[None, :]
        in_maps.append({
            "xloc": np.ascontiguousarray(x[rows]),
            "cosl": np.ascontiguousarray(np.cos(ang).T.astype(np.float32)),
            "sinl": np.ascontiguousarray(np.sin(ang).T.astype(np.float32)),
            "wq": Wq_l, "wk": Wk_l, "wv": Wv_l, "wo": Wo_l,
            "w1s": np.ascontiguousarray(w1t[c * RS1:(c + 1) * RS1]),
            "w2s": np.ascontiguousarray(w2t[c * RS2:(c + 1) * RS2]),
            "b1r": b1, "b2r": b2,
            "identf": ident, "masksf": masks,
        })
    return in_maps


def _assemble(S, D, n_cores, results):
    NCH = 16
    CS = S // NCH
    out = np.empty((S, D), np.float32)
    for c in range(n_cores):
        o = results[c]["out"]
        out[c * CS:(c + 1) * CS] = o[:CS]
        out[(NCH - 1 - c) * CS:(NCH - c) * CS] = o[CS:]
    return out.reshape(1, S, D)


def kernel(L0_attn_q, L0_attn_W_q, L0_attn_W_k, L0_attn_W_v, L0_attn_W_o,
           L0_ffn_w1, L0_ffn_b1, L0_ffn_w2, L0_ffn_b2):
    S, D, F, n_cores = 8192, 2048, 8192, 8
    in_maps = _host_inputs(S, D, F, n_cores, L0_attn_q, L0_attn_W_q, L0_attn_W_k,
                           L0_attn_W_v, L0_attn_W_o, L0_ffn_w1, L0_ffn_b1,
                           L0_ffn_w2, L0_ffn_b2)
    nc = build_program(S, D, F, n_cores)
    res = run_bass_kernel_spmd(nc, in_maps, core_ids=list(range(n_cores)))
    return _assemble(S, D, n_cores, res.results)
